# version 03
# baseline (speedup 1.0000x reference)
import math
from contextlib import ExitStack

import numpy as np
import concourse.bass as bass
import concourse.mybir as mybir
from concourse.bass_utils import run_bass_kernel_spmd
from concourse.bass_core_rust import IndirectOffsetOnAxis

F32 = mybir.dt.float32
BF16 = mybir.dt.bfloat16
U32 = mybir.dt.uint32
U8 = mybir.dt.uint8
AF = mybir.ActivationFunctionType
ALU = mybir.AluOpType
AX = mybir.AxisListType

B, S, D = 4, 4096, 1024
H, DH, NF = 16, 64, 256
E0, K0, CAP, F0, SF = 64, 6, 384, 512, 2048
E1, M1 = 8, 2048
T_ALL = B * S
TB = S
TC = 2048
NCH = D // 128
NSLOT = E0 * CAP
LN_EPS = 1e-5
YAT_EPS = 0.1
N_CORES = 8

ENGINES = ("tensor", "vector", "scalar", "gpsimd", "sync")


class _Op:
    __slots__ = ("eng", "fn", "reads", "writes", "is_dma", "needs_inc",
                 "inc_idx", "waits", "tag", "dma_barrier", "flush_wait", "flush_q")

    def __init__(self, eng, fn, reads, writes, is_dma, tag):
        self.eng = eng
        self.fn = fn
        self.reads = reads
        self.writes = writes
        self.is_dma = is_dma
        self.needs_inc = is_dma
        self.inc_idx = None
        self.waits = {}
        self.tag = tag


class AutoProg:
    def __init__(self):
        self.ops = []
        self._writer = {}
        self._readers = {}
        self._dma_count = {}
        self._flushed = {}
        self._issue_block = {}

    def _flush(self, q):
        n = self._dma_count.get(q, 0)
        fi, fn_cnt = self._flushed.get(q, (None, -1))
        if fn_cnt == n:
            return fi
        op = _Op("gpsimd", None, (), (), False, "flush")
        op.dma_barrier = {}
        op.needs_inc = True
        op.flush_wait = 16 * n
        op.flush_q = q
        i = len(self.ops)
        self.ops.append(op)
        self._flushed[q] = (i, n)
        if q != "gpsimd":
            self._issue_block[q] = i
        return i

    def do(self, eng, fn, reads=(), writes=(), dma=False, tag=""):
        op = _Op(eng, fn, tuple(reads), tuple(writes), dma, tag)
        raw = set()
        war_waw = set()
        for b_ in op.reads:
            j = self._writer.get(b_)
            if j is not None:
                raw.add(j)
        for b_ in op.writes:
            j = self._writer.get(b_)
            if j is not None:
                war_waw.add(j)
            for j in self._readers.get(b_, ()):
                war_waw.add(j)
        op.dma_barrier = {}
        deps = set()
        for j in sorted(raw | war_waw):
            oj = self.ops[j]
            if oj.is_dma:
                fi = self._flush(oj.eng)
                if self.ops[fi].eng == eng and not op.is_dma and False:
                    pass
                deps.add(fi)
                continue
            same = (oj.eng == eng) and not op.is_dma
            if same and eng == "tensor" and j not in raw:
                continue
            deps.add(j)
        if dma and eng in self._issue_block:
            deps.add(self._issue_block.pop(eng))
        i = len(self.ops)
        for j in deps:
            self.ops[j].needs_inc = True
            op.waits[j] = True
        self.ops.append(op)
        if dma:
            self._dma_count[eng] = self._dma_count.get(eng, 0) + 1
            self._flushed.pop(eng, None)
        for b_ in op.reads:
            self._readers.setdefault(b_, []).append(i)
        for b_ in op.writes:
            self._writer[b_] = i
            self._readers[b_] = []
        return i

    def emit(self, nc, eng_map, stack):
        sems = {}
        for e in ENGINES:
            sems[("c", e)] = stack.enter_context(nc.semaphore(f"sc_{e}"))
            sems[("d", e)] = stack.enter_context(nc.semaphore(f"sd_{e}"))
        cnt = {k: 0 for k in sems}
        for op in self.ops:
            if op.needs_inc:
                key = ("d" if op.is_dma else "c", op.eng)
                cnt[key] += 16 if op.is_dma else 1
                op.inc_idx = (key, cnt[key])
        for op in self.ops:
            tg = {}
            for j in op.waits:
                oj = self.ops[j]
                key, val = oj.inc_idx
                tg[key] = max(tg.get(key, 0), val)
            for q, n in op.dma_barrier.items():
                key = ("d", q)
                tg[key] = max(tg.get(key, 0), 16 * n)
            op.waits = tg

        blocks = {}
        for op in self.ops:
            blocks.setdefault(op.eng, []).append(op)

        flush_dummy = stack.enter_context(nc.sbuf_tensor("flush_dummy", [1, 3584], mybir.dt.bfloat16))
        flush_n = [0]
        with nc.Block() as block:
            for e, ops in blocks.items():
                def section(eng, ops=ops, e=e):
                    waited = {}
                    for op in ops:
                        for key, val in op.waits.items():
                            if waited.get(key, 0) < val:
                                eng.wait_ge(sems[key], val)
                                waited[key] = val
                        if op.fn is None:
                            if op.flush_wait > 0:
                                eng.wait_ge(sems[("d", op.flush_q)], op.flush_wait)
                            key, val = op.inc_idx
                            fi = flush_n[0]; flush_n[0] += 1
                            assert fi < 3584, "flush slots exhausted"
                            eng.memset(flush_dummy[0:1, fi:fi + 1],
                                       0.0).then_inc(sems[key], 1)
                            continue
                        inst = op.fn(eng)
                        if op.needs_inc:
                            key, val = op.inc_idx
                            inst.then_inc(sems[key], 16 if op.is_dma else 1)
                getattr(block, e)(section)

            def drain(eng):
                for key, val in cnt.items():
                    if val > 0:
                        eng.wait_ge(sems[key], val)
            block.sync(drain)
        return sems, cnt


def _stage_common(inp):
    st = {}
    for nm in ("Wq", "Wk", "Wv", "Wo"):
        st[nm] = np.ascontiguousarray(inp[f"L0_attn_{nm}"], np.float32)
    for nm in ("bq", "bk", "bv", "bo"):
        v = np.asarray(inp[f"L0_attn_{nm}"], np.float32)
        st[nm] = np.ascontiguousarray(v.reshape(NCH, 128).T)
    P = np.asarray(inp["L0_attn_projection"], np.float32).transpose(0, 2, 1)
    st["projT"] = np.ascontiguousarray(P.reshape(8, 2 * DH, NF))
    st["rwT"] = np.ascontiguousarray(np.asarray(inp["L0_ffn_router_weight"], np.float32).T)
    st["ebias"] = np.ascontiguousarray(np.asarray(inp["L0_ffn_expert_bias"], np.float32).reshape(1, E0))
    st["w1"] = np.ascontiguousarray(inp["L0_ffn_w1"], np.float32)
    st["w2"] = np.ascontiguousarray(inp["L0_ffn_w2"], np.float32)
    st["sw1T"] = np.ascontiguousarray(np.asarray(inp["L0_ffn_sw1"], np.float32).T)
    st["sw2T"] = np.ascontiguousarray(np.asarray(inp["L0_ffn_sw2"], np.float32).T)
    st["WfT"] = np.ascontiguousarray(np.asarray(inp["L1_attn_W"], np.float32).T)
    st["bfv"] = np.ascontiguousarray(np.asarray(inp["L1_attn_b"], np.float32).reshape(NCH, 128).T)
    st["Wrt1"] = np.ascontiguousarray(inp["L1_ffn_W_router"], np.float32)
    st["Wg1"] = np.ascontiguousarray(inp["L1_ffn_W_gate"], np.float32)
    st["Wu1"] = np.ascontiguousarray(inp["L1_ffn_W_up"], np.float32)
    st["Wo1"] = np.ascontiguousarray(inp["L1_ffn_W_out"], np.float32)
    st["bg1"] = np.ascontiguousarray(np.asarray(inp["L1_ffn_b_gate"], np.float32).reshape(E1, M1 // 128, 128).transpose(0, 2, 1))
    st["bu1"] = np.ascontiguousarray(np.asarray(inp["L1_ffn_b_up"], np.float32).reshape(E1, M1 // 128, 128).transpose(0, 2, 1))
    st["bo1"] = np.ascontiguousarray(np.asarray(inp["L1_ffn_b_out"], np.float32).reshape(E1, NCH, 128).transpose(0, 2, 1))
    return st


def _stage_core(inp, b, s):
    st = {}
    x = np.asarray(inp["L0_attn_x"], np.float32)
    xf = x.reshape(T_ALL, D)
    order = [bb for bb in range(B) if bb != b] + [b]
    xT = np.concatenate([xf[bb * TB:(bb + 1) * TB].T for bb in order], axis=1)
    st["xT_perm"] = np.ascontiguousarray(xT)
    st["x_tok"] = np.ascontiguousarray(x[b])
    selg = np.zeros((4, 1), np.float32)
    for gi, bb in enumerate(order):
        selg[gi, 0] = 1.0 if bb < b else 0.0
    st["selg"] = selg
    rot = ((np.arange(TB) + s * TC) % TB).astype(np.uint32)
    st["rot_idx"] = rot.reshape(TB // 128, 128, 1)
    ksh = np.zeros((128, TB // 512), np.float32)
    for kt in range(TB // 512):
        kl0 = kt * 512
        ksh[:, kt] = 4096.0 if (s == 1 and kl0 >= TC) else 0.0
    st["kshift"] = np.ascontiguousarray(ksh)
    return st


DRAM_SPECS = {
    "xT_perm": ([D, T_ALL], F32, "in"),
    "x_tok": ([TB, D], F32, "in"),
    "selg": ([4, 1], F32, "in"),
    "rot_idx": ([TB // 128, 128, 1], U32, "in"),
    "kshift": ([128, TB // 512], F32, "in"),
    "Wq": ([D, D], F32, "in"), "Wk": ([D, D], F32, "in"),
    "Wv": ([D, D], F32, "in"), "Wo": ([D, D], F32, "in"),
    "bq": ([128, NCH], F32, "in"), "bk": ([128, NCH], F32, "in"),
    "bv": ([128, NCH], F32, "in"), "bo": ([128, NCH], F32, "in"),
    "projT": ([8, 128, NF], F32, "in"),
    "rwT": ([D, E0], F32, "in"),
    "ebias": ([1, E0], F32, "in"),
    "w1": ([E0, D, F0], F32, "in"),
    "w2": ([E0, F0, D], F32, "in"),
    "sw1T": ([D, SF], F32, "in"),
    "sw2T": ([SF, D], F32, "in"),
    "WfT": ([D, D], F32, "in"),
    "bfv": ([128, NCH], F32, "in"),
    "Wrt1": ([D, E1], F32, "in"),
    "Wg1": ([E1, D, M1], F32, "in"),
    "Wu1": ([E1, D, M1], F32, "in"),
    "Wo1": ([E1, M1, D], F32, "in"),
    "bg1": ([E1, 128, M1 // 128], F32, "in"),
    "bu1": ([E1, 128, M1 // 128], F32, "in"),
    "bo1": ([E1, 128, NCH], F32, "in"),
    "mu_own": ([1, TB], F32, "inter"),
    "r_own": ([1, TB], F32, "inter"),
    "Tmap_w": ([NSLOT + 1, 2], F32, "inter"),
    "xnT": ([D, TB], BF16, "inter"),
    "xn_tok": ([TB + 1, D], BF16, "inter"),
    "x1": ([TB + 1, D], F32, "inter"),
    "f_tok": ([TB, D], BF16, "inter"),
    "xn2_tok": ([TC + 1, D], BF16, "inter"),
    "hid1": ([128, M1 // 128, TC], BF16, "inter"),
    "y": ([TC, D], F32, "out"),
}


def _chunks(ap):
    return ap.rearrange("(c p) t -> p c t", p=128)


def build_p0(p, d, sb, ps):
    TILE = 512
    NT = T_ALL // TILE
    NOT_OWN = NT - TB // TILE

    p.do("gpsimd", lambda g: g.memset(sb["ones_f1"][:], 1.0), writes=["ones_f1"])
    p.do("gpsimd", lambda g: g.memset(sb["ones_b1"][:], 1.0), writes=["ones_b1"])
    p.do("gpsimd", lambda g: g.memset(sb["ones_row"][:], 1.0), writes=["ones_row"])
    p.do("gpsimd", lambda g: g.iota(sb["iota64"][:], pattern=[[1, E0]], base=0,
                                    channel_multiplier=0, allow_small_or_imprecise_dtypes=True),
         writes=["iota64"])
    p.do("gpsimd", lambda g: g.iota(sb["iotac"][:], pattern=[[1, 1]], base=0,
                                    channel_multiplier=1, allow_small_or_imprecise_dtypes=True),
         writes=["iotac"])
    p.do("gpsimd", lambda g: g.memset(sb["su"][:], 0.0), writes=["su"])
    p.do("gpsimd", lambda g: g.affine_select(out=sb["su"][:], in_=sb["su"][:],
                                             compare_op=ALU.is_ge, fill=1.0, base=0,
                                             pattern=[[-1, 128]], channel_multiplier=1),
         reads=["su"], writes=["su"])
    p.do("gpsimd", lambda g: g.memset(sb["ones128"][:], 1.0), writes=["ones128"])
    p.do("gpsimd", lambda g: g.memset(sb["identf"][:], 0.0), writes=["identf"])
    p.do("gpsimd", lambda g: g.affine_select(out=sb["identf"][:], in_=sb["identf"][:],
                                             compare_op=ALU.not_equal, fill=1.0, base=0,
                                             pattern=[[-1, 128]], channel_multiplier=1),
         reads=["identf"], writes=["identf"])
    p.do("gpsimd", lambda g: g.memset(sb["init"][:], 0.0), writes=["init"])
    p.do("gpsimd", lambda g: g.memset(sb["init"][:, 0:384:2], float(TB)),
         reads=["init"], writes=["init"])
    p.do("sync", lambda sy: sy.dma_start(
        out=d["Tmap_w"][0:NSLOT, :].rearrange("(p r) c -> p (r c)", p=128),
        in_=sb["init"][:, 0:384]), reads=["init"], writes=["Tmap_w"], dma=True)
    p.do("sync", lambda sy: sy.dma_start(
        out=d["Tmap_w"][NSLOT:NSLOT + 1, :], in_=sb["init"][0:1, 0:2]),
        reads=["init"], writes=["Tmap_w"], dma=True)

    p.do("sync", lambda sy: sy.dma_start(
        out=sb["rwf"][:].rearrange("p (c e) -> p c e", c=NCH), in_=_chunks(d["rwT"])),
        writes=["rwf"], dma=True)
    p.do("sync", lambda sy: sy.dma_start(out=sb["row64"][0:1, :], in_=d["ebias"]),
         writes=["row64"], dma=True)
    p.do("sync", lambda sy: sy.dma_start(out=sb["sel4"][0:4, 0:1], in_=d["selg"]),
         writes=["sel4"], dma=True)
    p.do("tensor", lambda t: t.matmul(ps["m64"][:, 0:E0], sb["ones_row"][:],
                                      sb["row64"][0:1, :], start=True, stop=True),
         reads=["ones_row", "row64"], writes=["ps_brd"])
    p.do("vector", lambda v: v.tensor_copy(out=sb["ebias_b"][:], in_=ps["m64"][:, 0:E0]),
         reads=["ps_brd"], writes=["ebias_b"])
    for c in range(NCH):
        p.do("tensor", lambda t, c=c: t.matmul(
            ps["cnt"][0:E0, 4:5], sb["rwf"][:, c * E0:(c + 1) * E0], sb["ones_f1"][:, 0:1],
            start=(c == 0), stop=(c == NCH - 1)),
            reads=["rwf", "ones_f1"], writes=["ps_cnt"])
    p.do("vector", lambda v: v.tensor_copy(out=sb["cnt64"][0:E0, 0:1], in_=ps["cnt"][0:E0, 4:5]),
         reads=["ps_cnt"], writes=["cnt64"])
    p.do("tensor", lambda t: t.transpose(ps["tsm"][0:1, 0:E0], sb["cnt64"][0:E0, 0:1],
                                         sb["identf"][0:E0, 0:E0]),
         reads=["cnt64", "identf"], writes=["ps_trow"])
    p.do("vector", lambda v: v.tensor_copy(out=sb["row64b"][0:1, 0:E0], in_=ps["tsm"][0:1, 0:E0]),
         reads=["ps_trow"], writes=["row64b"])
    p.do("tensor", lambda t: t.matmul(ps["m64"][:, 0:E0], sb["ones_row"][:],
                                      sb["row64b"][0:1, 0:E0], start=True, stop=True),
         reads=["ones_row", "row64b"], writes=["ps_brd"])
    p.do("vector", lambda v: v.tensor_copy(out=sb["rwsum_b"][:], in_=ps["m64"][:, 0:E0]),
         reads=["ps_brd"], writes=["rwsum_b"])
    p.do("gpsimd", lambda g: g.memset(sb["offg_b"][:], 0.0), writes=["offg_b"])
    p.do("gpsimd", lambda g: g.memset(sb["nslotc"][:], float(NSLOT)), writes=["nslotc"])

    n_own = 0
    for i in range(NT):
        gi = i // (TB // TILE)
        own = gi == 3
        xbuf = "xa" if i % 2 == 0 else "xb"
        p.do("sync", lambda sy, i=i, xbuf=xbuf: sy.dma_start(
            out=sb[xbuf][:].rearrange("p (c t) -> p c t", c=NCH),
            in_=_chunks(d["xT_perm"])[:, :, i * TILE:(i + 1) * TILE]),
            writes=[xbuf], dma=True)
        p.do("scalar", lambda sc, xbuf=xbuf: sc.activation(
            out=sb["xsq"][:], in_=sb[xbuf][:], func=AF.Square),
            reads=[xbuf], writes=["xsq"])
        for c in range(NCH):
            p.do("tensor", lambda t, c=c, xbuf=xbuf: t.matmul(
                ps["stat"][0:1, 0:TILE], sb["ones_f1"][:, 0:1],
                sb[xbuf][:, c * TILE:(c + 1) * TILE], start=(c == 0), stop=(c == NCH - 1)),
                reads=[xbuf, "ones_f1"], writes=["ps_stat"])
        p.do("vector", lambda v: v.tensor_scalar_mul(
            out=sb["murow"][0:1, :], in0=ps["stat"][0:1, 0:TILE], scalar1=1.0 / D),
            reads=["ps_stat"], writes=["murow"])
        for c in range(NCH):
            p.do("tensor", lambda t, c=c: t.matmul(
                ps["stat"][0:1, 0:TILE], sb["ones_f1"][:, 0:1],
                sb["xsq"][:, c * TILE:(c + 1) * TILE], start=(c == 0), stop=(c == NCH - 1)),
                reads=["xsq", "ones_f1", "murow"], writes=["ps_stat"])
        for c in range(NCH):
            p.do("tensor", lambda t, c=c, xbuf=xbuf: t.matmul(
                ps["log"][:, 0:TILE], sb["rwf"][:, c * E0:(c + 1) * E0],
                sb[xbuf][:, c * TILE:(c + 1) * TILE], start=(c == 0), stop=(c == NCH - 1)),
                reads=[xbuf, "rwf"], writes=["ps_log"])
        p.do("vector", lambda v: v.tensor_scalar_mul(
            out=sb["m2row"][0:1, :], in0=ps["stat"][0:1, 0:TILE], scalar1=1.0 / D),
            reads=["ps_stat"], writes=["m2row"])
        p.do("vector", lambda v: v.tensor_mul(
            out=sb["rrow"][0:1, :], in0=sb["murow"][0:1, :], in1=sb["murow"][0:1, :]),
            reads=["murow"], writes=["rrow"])
        p.do("vector", lambda v: v.tensor_sub(
            out=sb["m2row"][0:1, :], in0=sb["m2row"][0:1, :], in1=sb["rrow"][0:1, :]),
            reads=["m2row", "rrow"], writes=["m2row"])
        p.do("vector", lambda v: v.tensor_scalar_add(
            out=sb["m2row"][0:1, :], in0=sb["m2row"][0:1, :], scalar1=LN_EPS),
            reads=["m2row"], writes=["m2row"])
        p.do("scalar", lambda sc: sc.activation(
            out=sb["rrow"][0:1, :], in_=sb["m2row"][0:1, :], func=AF.Sqrt),
            reads=["m2row"], writes=["rrow"])
        p.do("vector", lambda v: v.reciprocal(out=sb["rrow"][0:1, :], in_=sb["rrow"][0:1, :]),
             reads=["rrow"], writes=["rrow"])
        if own:
            j0 = (i - NOT_OWN) * TILE
            p.do("sync", lambda sy, j0=j0: sy.dma_start(
                out=d["mu_own"][0:1, j0:j0 + TILE], in_=sb["murow"][0:1, :]),
                reads=["murow"], writes=["mu_own"], dma=True)
            p.do("sync", lambda sy, j0=j0: sy.dma_start(
                out=d["r_own"][0:1, j0:j0 + TILE], in_=sb["rrow"][0:1, :]),
                reads=["rrow"], writes=["r_own"], dma=True)
        p.do("vector", lambda v: v.tensor_copy(out=sb["logsb"][0:E0, :], in_=ps["log"][:, 0:TILE]),
             reads=["ps_log"], writes=["logsb"])

        if own and i == NOT_OWN:
            p.do("vector", lambda v: v.tensor_copy(out=sb["cnt64"][0:E0, 0:3], in_=ps["cnt"][0:E0, 0:3]),
                 reads=["ps_cnt"], writes=["cnt64"])
            p.do("tensor", lambda t: t.transpose(ps["tsm"][0:3, 0:E0], sb["cnt64"][0:E0, 0:3],
                                                 sb["identf"][0:E0, 0:E0]),
                 reads=["cnt64", "identf"], writes=["ps_trow"])
            p.do("vector", lambda v: v.tensor_copy(out=sb["cntT"][0:3, 0:E0], in_=ps["tsm"][0:3, 0:E0]),
                 reads=["ps_trow"], writes=["cntT"])
            p.do("tensor", lambda t: t.matmul(ps["tsm"][0:1, 0:E0], sb["sel4"][0:3, 0:1],
                                              sb["cntT"][0:3, 0:E0], start=True, stop=True),
                 reads=["sel4", "cntT"], writes=["ps_trow"])
            p.do("vector", lambda v: v.tensor_copy(out=sb["row64b"][0:1, 0:E0],
                                                   in_=ps["tsm"][0:1, 0:E0]),
                 reads=["ps_trow"], writes=["row64b"])
            p.do("tensor", lambda t: t.matmul(ps["m64"][:, 0:E0], sb["ones_row"][:],
                                              sb["row64b"][0:1, 0:E0], start=True, stop=True),
                 reads=["ones_row", "row64b"], writes=["ps_brd"])
            p.do("vector", lambda v: v.tensor_copy(out=sb["offg_b"][:], in_=ps["m64"][:, 0:E0]),
                 reads=["ps_brd"], writes=["offg_b"])

        for j in range(TILE // 128):
            p.do("tensor", lambda t, j=j: t.transpose(
                ps["tsm"][:, E0:E0 + 1], sb["murow"][0:1, j * 128:(j + 1) * 128],
                sb["identf"][0:1, 0:1]),
                reads=["murow", "identf"], writes=["ps_tcol"])
            p.do("vector", lambda v: v.tensor_copy(out=sb["mucol"][:, 0:1], in_=ps["tsm"][:, E0:E0 + 1]),
                 reads=["ps_tcol"], writes=["mucol"])
            p.do("tensor", lambda t, j=j: t.transpose(
                ps["tsm"][:, E0 + 1:E0 + 2], sb["rrow"][0:1, j * 128:(j + 1) * 128],
                sb["identf"][0:1, 0:1]),
                reads=["rrow", "identf"], writes=["ps_tcol"])
            p.do("vector", lambda v: v.tensor_copy(out=sb["rcol"][:, 0:1], in_=ps["tsm"][:, E0 + 1:E0 + 2]),
                 reads=["ps_tcol"], writes=["rcol"])
            p.do("vector", lambda v: v.tensor_scalar_mul(
                out=sb["negr"][:, 0:1], in0=sb["rcol"][:, 0:1], scalar1=-1.0),
                reads=["rcol"], writes=["negr"])
            p.do("tensor", lambda t, j=j: t.transpose(
                ps["m64"][:, E0:2 * E0], sb["logsb"][0:E0, j * 128:(j + 1) * 128],
                sb["identf"][0:E0, 0:E0]),
                reads=["logsb", "identf"], writes=["ps_ttok"])
            p.do("vector", lambda v: v.scalar_tensor_tensor(
                out=sb["prs"][:], in0=sb["rwsum_b"][:], scalar=sb["mucol"][:, 0:1],
                in1=ps["m64"][:, E0:2 * E0], op0=ALU.mult, op1=ALU.subtract),
                reads=["rwsum_b", "mucol", "ps_ttok"], writes=["prs"])
            p.do("scalar", lambda sc: sc.activation(
                out=sb["sc64"][:], in_=sb["prs"][:], func=AF.Sigmoid, scale=sb["negr"][:, 0:1]),
                reads=["prs", "negr"], writes=["sc64"])
            p.do("vector", lambda v: v.tensor_add(
                out=sb["bsc"][:], in0=sb["sc64"][:], in1=sb["ebias_b"][:]),
                reads=["sc64", "ebias_b"], writes=["bsc"])
            p.do("vector", lambda v: v.max(out=sb["top8"][:], in_=sb["bsc"][:]),
                 reads=["bsc"], writes=["top8"])
            nn = n_own
            p.do("vector", lambda v, nn=nn, own=own: v.tensor_scalar(
                out=(sb["vtok"][:, nn * E0:(nn + 1) * E0] if own else sb["Mtmp"][:]),
                in0=sb["bsc"][:], scalar1=sb["top8"][:, 5:6], scalar2=None, op0=ALU.is_ge),
                reads=["bsc", "top8"], writes=["vtok" if own else "Mtmp"])
            first = (i % (TB // TILE) == 0) and j == 0
            last = (i % (TB // TILE) == TB // TILE - 1) and j == TILE // 128 - 1
            p.do("tensor", lambda t, nn=nn, own=own, gi=gi, first=first, last=last: t.matmul(
                ps["cnt"][0:E0, gi:gi + 1],
                (sb["vtok"][:, nn * E0:(nn + 1) * E0] if own else sb["Mtmp"][:]),
                sb["ones_b1"][:, 0:1], start=first, stop=last),
                reads=["vtok" if own else "Mtmp", "ones_b1"], writes=["ps_cnt"])

            if own:
                for m2 in range(nn + 1):
                    p.do("tensor", lambda t, m2=m2, nn=nn: t.matmul(
                        ps["m64"][:, 2 * E0:3 * E0], (sb["su"][:] if m2 == nn else sb["ones128"][:]),
                        sb["vtok"][:, m2 * E0:(m2 + 1) * E0], start=(m2 == 0), stop=(m2 == nn)),
                        reads=["su", "ones128", "vtok"], writes=["ps_pref"])
                p.do("vector", lambda v: v.tensor_copy(out=sb["prefx"][:], in_=ps["m64"][:, 2 * E0:3 * E0]),
                     reads=["ps_pref"], writes=["prefx"])
                p.do("vector", lambda v, nn=nn: v.tensor_mul(
                    out=sb["scr"][:], in0=sb["sc64"][:], in1=sb["vtok"][:, nn * E0:(nn + 1) * E0]),
                    reads=["sc64", "vtok"], writes=["scr"])
                p.do("vector", lambda v: v.reduce_sum(out=sb["ssum"][:, 0:1], in_=sb["scr"][:],
                                                      axis=AX.X),
                     reads=["scr"], writes=["ssum"])
                p.do("vector", lambda v: v.tensor_scalar_add(
                    out=sb["ssum"][:, 0:1], in0=sb["ssum"][:, 0:1], scalar1=1e-20),
                    reads=["ssum"], writes=["ssum"])
                p.do("vector", lambda v: v.reciprocal(out=sb["sinv"][:, 0:1], in_=sb["ssum"][:, 0:1]),
                     reads=["ssum"], writes=["sinv"])
                p.do("vector", lambda v, nn=nn: v.tensor_scalar_add(
                    out=sb["tokc"][:, 0:1], in0=sb["iotac"][:], scalar1=float(nn * 128)),
                    reads=["iotac"], writes=["tokc"])
                pb = nn % 2
                for kk in range(K0):
                    sl = pb * K0 + kk
                    p.do("vector", lambda v, kk=kk: v.tensor_scalar(
                        out=sb["oh"][:], in0=sb["bsc"][:], scalar1=sb["top8"][:, kk:kk + 1],
                        scalar2=None, op0=ALU.is_equal),
                        reads=["bsc", "top8"], writes=["oh"])
                    p.do("vector", lambda v: v.scalar_tensor_tensor(
                        out=sb["scr"][:], in0=sb["oh"][:], scalar=-16384.0,
                        in1=sb["iota64"][:], op0=ALU.mult, op1=ALU.add),
                        reads=["oh", "iota64"], writes=["scr"])
                    p.do("vector", lambda v: v.tensor_reduce(
                        out=sb["idxc"][:, 0:1], in_=sb["scr"][:], axis=AX.X, op=ALU.min),
                        reads=["scr"], writes=["idxc"])
                    p.do("vector", lambda v: v.tensor_scalar_add(
                        out=sb["idxc"][:, 0:1], in0=sb["idxc"][:, 0:1], scalar1=16384.0),
                        reads=["idxc"], writes=["idxc"])
                    p.do("vector", lambda v: v.tensor_mul(out=sb["scr"][:], in0=sb["sc64"][:],
                                                          in1=sb["oh"][:]),
                         reads=["sc64", "oh"], writes=["scr"])
                    p.do("vector", lambda v: v.reduce_max(out=sb["sck"][:, 0:1], in_=sb["scr"][:],
                                                          axis=AX.X),
                         reads=["scr"], writes=["sck"])
                    p.do("vector", lambda v: v.tensor_mul(out=sb["scr"][:], in0=sb["prefx"][:],
                                                          in1=sb["oh"][:]),
                         reads=["prefx", "oh"], writes=["scr"])
                    p.do("vector", lambda v: v.reduce_max(out=sb["rkc"][:, 0:1], in_=sb["scr"][:],
                                                          axis=AX.X),
                         reads=["scr"], writes=["rkc"])
                    p.do("vector", lambda v: v.tensor_mul(out=sb["scr"][:], in0=sb["offg_b"][:],
                                                          in1=sb["oh"][:]),
                         reads=["offg_b", "oh"], writes=["scr"])
                    p.do("vector", lambda v: v.reduce_max(out=sb["offc"][:, 0:1], in_=sb["scr"][:],
                                                          axis=AX.X),
                         reads=["scr"], writes=["offc"])
                    p.do("vector", lambda v: v.tensor_add(out=sb["posc"][:, 0:1],
                                                          in0=sb["offc"][:, 0:1], in1=sb["rkc"][:, 0:1]),
                         reads=["offc", "rkc"], writes=["posc"])
                    p.do("vector", lambda v: v.tensor_scalar(
                        out=sb["validf"][:, 0:1], in0=sb["posc"][:, 0:1], scalar1=float(CAP),
                        scalar2=None, op0=ALU.is_lt),
                        reads=["posc"], writes=["validf"])
                    p.do("vector", lambda v: v.tensor_mul(out=sb["wc"][:, 0:1], in0=sb["sck"][:, 0:1],
                                                          in1=sb["sinv"][:, 0:1]),
                         reads=["sck", "sinv"], writes=["wc"])
                    p.do("vector", lambda v: v.tensor_mul(out=sb["wc"][:, 0:1], in0=sb["wc"][:, 0:1],
                                                          in1=sb["validf"][:, 0:1]),
                         reads=["wc", "validf"], writes=["wc"])
                    p.do("vector", lambda v: v.scalar_tensor_tensor(
                        out=sb["slotc"][:, 0:1], in0=sb["idxc"][:, 0:1], scalar=float(CAP),
                        in1=sb["posc"][:, 0:1], op0=ALU.mult, op1=ALU.add),
                        reads=["idxc", "posc"], writes=["slotc"])
                    p.do("vector", lambda v: v.tensor_scalar_add(
                        out=sb["slotc"][:, 0:1], in0=sb["slotc"][:, 0:1], scalar1=-float(NSLOT)),
                        reads=["slotc"], writes=["slotc"])
                    p.do("vector", lambda v: v.scalar_tensor_tensor(
                        out=sb["slotc"][:, 0:1], in0=sb["slotc"][:, 0:1], scalar=sb["validf"][:, 0:1],
                        in1=sb["nslotc"][:, 0:1], op0=ALU.mult, op1=ALU.add),
                        reads=["slotc", "validf", "nslotc"], writes=["slotc"])
                    p.do("vector", lambda v, sl=sl: v.tensor_copy(
                        out=sb["pay"][:, 2 * sl:2 * sl + 1], in_=sb["tokc"][:, 0:1]),
                        reads=["tokc"], writes=[f"pay{sl}"])
                    p.do("vector", lambda v, sl=sl: v.tensor_copy(
                        out=sb["pay"][:, 2 * sl + 1:2 * sl + 2], in_=sb["wc"][:, 0:1]),
                        reads=["wc"], writes=[f"pay{sl}"])
                    p.do("vector", lambda v, sl=sl: v.tensor_copy(
                        out=sb["slotu"][:, sl:sl + 1], in_=sb["slotc"][:, 0:1]),
                        reads=["slotc"], writes=[f"slotu{sl}"])
                    p.do("gpsimd", lambda g, sl=sl: g.indirect_dma_start(
                        out=d["Tmap_w"], out_offset=IndirectOffsetOnAxis(sb["slotu"][:, sl:sl + 1], 0),
                        in_=sb["pay"][:, 2 * sl:2 * sl + 2], in_offset=None),
                        reads=[f"pay{sl}", f"slotu{sl}", "Tmap_w"], writes=["Tmap_w"], dma=True)
                n_own += 1
    return p


P0_SBUF = {
    "ones_f1": ([128, 1], F32), "ones_b1": ([128, 1], BF16),
    "ones_row": ([1, 128], F32),
    "iota64": ([128, E0], F32), "iotac": ([128, 1], F32),
    "su": ([128, 128], BF16), "ones128": ([128, 128], BF16),
    "identf": ([128, 128], F32),
    "init": ([128, 384], F32),
    "rwf": ([128, NCH * E0], F32),
    "row64": ([1, E0], F32), "row64b": ([1, E0], F32),
    "sel4": ([4, 1], F32), "cnt64": ([E0, 8], F32), "cntT": ([4, E0], F32),
    "ebias_b": ([128, E0], F32), "rwsum_b": ([128, E0], F32), "offg_b": ([128, E0], F32),
    "xa": ([128, NCH * 512], F32), "xb": ([128, NCH * 512], F32),
    "xsq": ([128, NCH * 512], F32),
    "murow": ([1, 512], F32), "m2row": ([1, 512], F32), "rrow": ([1, 512], F32),
    "logsb": ([E0, 512], F32),
    "mucol": ([128, 1], F32), "rcol": ([128, 1], F32), "negr": ([128, 1], F32),
    "prs": ([128, E0], F32), "sc64": ([128, E0], F32), "bsc": ([128, E0], F32),
    "top8": ([128, 8], F32),
    "Mtmp": ([128, E0], BF16),
    "prefx": ([128, E0], F32), "scr": ([128, E0], F32),
    "ssum": ([128, 1], F32), "sinv": ([128, 1], F32), "tokc": ([128, 1], F32),
    "oh": ([128, E0], F32), "idxc": ([128, 1], F32), "sck": ([128, 1], F32),
    "rkc": ([128, 1], F32), "offc": ([128, 1], F32), "posc": ([128, 1], F32),
    "validf": ([128, 1], F32), "wc": ([128, 1], F32), "nslotc": ([128, 1], F32),
    "slotc": ([128, 1], F32),
    "pay": ([128, 24], F32), "slotu": ([128, 12], U32),
    "vtok": ([128, 4 * D], BF16),
}

P0_PSUM = {
    "stat": ([1, 512], F32),
    "log": ([E0, 512], F32),
    "cnt": ([E0, 8], F32),
    "m64": ([128, 3 * E0], F32),
    "tsm": ([128, 66], F32),
}


def _build_launch(phase_fns, sbuf_specs, psum_specs, dram_roles):
    nc = bass.Bass("TRN2", target_bir_lowering=False, debug=False)
    d = {}
    out_names = []
    for name, (shape, dt, role) in dram_roles.items():
        if role == "in":
            d[name] = nc.dram_tensor(name, shape, dt, kind="ExternalInput").ap()
        elif role == "out":
            d[name] = nc.dram_tensor(name, shape, dt, kind="ExternalOutput").ap()
            out_names.append(name)
        else:
            d[name] = nc.dram_tensor(name, shape, dt, kind="Internal").ap()
    with ExitStack() as stack:
        sb = {}
        for name, (shape, dt) in sbuf_specs.items():
            sb[name] = stack.enter_context(nc.sbuf_tensor(f"s_{name}", shape, dt))
        ps = {}
        for name, spec in psum_specs.items():
            shape, dt = spec[0], spec[1]
            ps[name] = stack.enter_context(nc.psum_tensor(f"p_{name}", shape, dt))
        prog = AutoProg()
        for fn in phase_fns:
            fn(prog, d, sb, ps)
        sems, cnt = prog.emit(nc, _engines(nc), stack)
    return nc, out_names


def _engines(nc):
    return {"tensor": nc.tensor, "vector": nc.vector, "scalar": nc.scalar,
            "gpsimd": nc.gpsimd, "sync": nc.sync}


def _roles_for(names_in, names_out, extra_internal=()):
    roles = {}
    for n in names_in:
        shape, dt, _ = DRAM_SPECS[n]
        roles[n] = (shape, dt, "in")
    for n in names_out:
        shape, dt, _ = DRAM_SPECS[n]
        roles[n] = (shape, dt, "out")
    for n in extra_internal:
        shape, dt, _ = DRAM_SPECS[n]
        roles[n] = (shape, dt, "internal")
    return roles


def merge_specs(*dicts):
    out = {}
    for dd in dicts:
        for k, v in dd.items():
            if k in out:
                assert out[k][0] == v[0] and out[k][1] == v[1], f"spec clash {k}"
            out[k] = v
    return out


def run_phase(phase_fns, sbuf_specs, psum_specs, names_in, names_out, in_maps,
              internal=()):
    roles = _roles_for(names_in, names_out, internal)
    nc, out_names = _build_launch(phase_fns, sbuf_specs, psum_specs, roles)
    res = run_bass_kernel_spmd(nc, in_maps, core_ids=list(range(N_CORES)))
    return res.results




def build_p1(p, d, sb, ps):
    TILE = 512
    NOT_OWN = (T_ALL - TB) // TILE

    p.do("gpsimd", lambda g: g.memset(sb["identb"][:], 0.0), writes=["identb"])
    p.do("gpsimd", lambda g: g.affine_select(out=sb["identb"][:], in_=sb["identb"][:],
                                             compare_op=ALU.not_equal, fill=1.0, base=0,
                                             pattern=[[-1, 128]], channel_multiplier=1),
         reads=["identb"], writes=["identb"])

    for q in range(4):
        p.do("gpsimd", lambda g, q=q: g.dma_start(
            out=d["x1"][q * 1024:(q + 1) * 1024, :], in_=d["x_tok"][q * 1024:(q + 1) * 1024, :]),
            reads=["x_tok"], writes=["x1"], dma=True)
    p.do("gpsimd", lambda g: g.memset(sb["zrow16"][:], 0.0), writes=["zrow16"])
    p.do("gpsimd", lambda g: g.dma_start(out=d["x1"][TB:TB + 1, :], in_=sb["zrow16"][0:1, 0:D]),
         reads=["zrow16"], writes=["x1"], dma=True)
    p.do("gpsimd", lambda g: g.dma_start(out=d["xn_tok"][TB:TB + 1, :], in_=sb["zrow16"][0:1, 0:D]),
         reads=["zrow16"], writes=["xn_tok"], dma=True)

    for i in range(TB // TILE):
        xbuf = "xa" if i % 2 == 0 else "xb"
        it = NOT_OWN + i
        p.do("sync", lambda sy, it=it, xbuf=xbuf: sy.dma_start(
            out=sb[xbuf][:].rearrange("p (c t) -> p c t", c=NCH),
            in_=_chunks(d["xT_perm"])[:, :, it * TILE:(it + 1) * TILE]),
            writes=[xbuf], dma=True)
        p.do("scalar", lambda sc, i=i: sc.dma_start(
            out=sb["murow"][0:1, :], in_=d["mu_own"][0:1, i * TILE:(i + 1) * TILE]),
            reads=["mu_own"], writes=["murow"], dma=True)
        p.do("scalar", lambda sc, i=i: sc.dma_start(
            out=sb["rrow"][0:1, :], in_=d["r_own"][0:1, i * TILE:(i + 1) * TILE]),
            reads=["r_own"], writes=["rrow"], dma=True)
        p.do("tensor", lambda t: t.matmul(ps["bmu"][:], sb["ones_row"][:],
                                          sb["murow"][0:1, :], start=True, stop=True),
             reads=["ones_row", "murow"], writes=["ps_bmu"])
        p.do("tensor", lambda t: t.matmul(ps["brr"][:], sb["ones_row"][:],
                                          sb["rrow"][0:1, :], start=True, stop=True),
             reads=["ones_row", "rrow"], writes=["ps_brr"])
        for c in range(NCH):
            p.do("vector", lambda v, c=c, xbuf=xbuf: v.tensor_sub(
                out=sb["xsq"][:, c * TILE:(c + 1) * TILE],
                in0=sb[xbuf][:, c * TILE:(c + 1) * TILE], in1=ps["bmu"][:]),
                reads=[xbuf, "ps_bmu"], writes=["xsq"])
        for c in range(NCH):
            p.do("vector", lambda v, c=c: v.tensor_mul(
                out=sb["xnt"][:, c * TILE:(c + 1) * TILE],
                in0=sb["xsq"][:, c * TILE:(c + 1) * TILE], in1=ps["brr"][:]),
                reads=["xsq", "ps_brr"], writes=["xnt"])
        p.do("sync", lambda sy, i=i: sy.dma_start(
            out=_chunks(d["xnT"])[:, :, i * TILE:(i + 1) * TILE],
            in_=sb["xnt"][:].rearrange("p (c t) -> p c t", c=NCH)),
            reads=["xnt"], writes=["xnT"], dma=True)
        for j in range(TILE // 128):
            for c in range(NCH):
                p.do("tensor", lambda t, j=j, c=c: t.transpose(
                    ps["ptb"][:, c * 128:(c + 1) * 128],
                    sb["xnt"][:, c * TILE + j * 128:c * TILE + (j + 1) * 128],
                    sb["identb"][:]),
                    reads=["xnt", "identb"], writes=["ps_ptb"])
            p.do("vector", lambda v: v.tensor_copy(out=sb["tokrow"][:], in_=ps["ptb"][:]),
                 reads=["ps_ptb"], writes=["tokrow"])
            p.do("scalar", lambda sc, i=i, j=j: sc.dma_start(
                out=d["xn_tok"][i * TILE + j * 128:i * TILE + (j + 1) * 128, :],
                in_=sb["tokrow"][:]),
                reads=["tokrow"], writes=["xn_tok"], dma=True)
    return p


P1_SBUF = {
    "identb": ([128, 128], BF16),
    "zrow16": ([1, D], BF16),
    "murow": ([1, 512], F32), "rrow": ([1, 512], F32),
    "xa": ([128, NCH * 512], F32), "xb": ([128, NCH * 512], F32),
    "xsq": ([128, NCH * 512], F32),
    "xnt": ([128, NCH * 512], BF16),
    "tokrow": ([128, D], BF16),
    "ones_row": ([1, 128], F32),
}

P1_PSUM = {
    "bmu": ([128, 512], F32),
    "brr": ([128, 512], F32),
    "ptb": ([128, D], BF16),
}


def _load_w_cast(p, d, sb, name, dst):
    p.do("gpsimd", lambda g: g.dma_start(
        out=sb[dst][:].rearrange("p (kc mc m) -> p kc mc m", kc=NCH, mc=NCH),
        in_=d[name].rearrange("(kc p) (mc m) -> p kc mc m", p=128, m=128)),
        reads=[name], writes=[dst], dma=True)


def build_p2(p, d, sb, ps):
    TILE = 512
    NT = TB // TILE
    for nm, dst in (("Wq", "wq"), ("Wk", "wk"), ("Wv", "wv"), ("Wo", "wo")):
        _load_w_cast(p, d, sb, nm, dst)
    p.do("gpsimd", lambda g: g.dma_start(
        out=sb["projb"][:].rearrange("p (h f) -> p h f", h=8),
        in_=d["projT"].rearrange("h p f -> p h f")),
        reads=["projT"], writes=["projb"], dma=True)
    for nm in ("bq", "bk", "bv", "bo"):
        p.do("scalar", lambda sc, nm=nm: sc.dma_start(out=sb[f"{nm}s"][:], in_=d[nm]),
             reads=[nm], writes=[f"{nm}s"], dma=True)
    p.do("gpsimd", lambda g: g.memset(sb["hsel"][:], 0.0), writes=["hsel"])
    for c in range(NCH):
        p.do("gpsimd", lambda g, c=c: g.memset(sb["hsel"][0:64, c * 16 + 2 * c:c * 16 + 2 * c + 1], 1.0),
             reads=["hsel"], writes=["hsel"])
        p.do("gpsimd", lambda g, c=c: g.memset(sb["hsel"][64:128, c * 16 + 2 * c + 1:c * 16 + 2 * c + 2], 1.0),
             reads=["hsel"], writes=["hsel"])
    p.do("vector", lambda v: v.tensor_copy(out=sb["hself"][:], in_=sb["hsel"][:]),
         reads=["hsel"], writes=["hself"])
    for c in range(NCH):
        p.do("tensor", lambda t, c=c: t.transpose(
            ps["log"][0:16, 0:128], sb["hself"][:, c * 16:(c + 1) * 16], sb["identf"][:]),
            reads=["hself", "identf"], writes=["ps_log"])
        p.do("vector", lambda v, c=c: v.tensor_copy(
            out=sb["gsel"][0:16, c * 128:(c + 1) * 128], in_=ps["log"][0:16, 0:128]),
            reads=["ps_log"], writes=["gsel"])
    p.do("gpsimd", lambda g: g.memset(sb["xb"][:, 0:2048], 0.0), writes=["xb"])
    p.do("gpsimd", lambda g: g.memset(sb["ksum"][:], 0.0), writes=["ksum"])

    def spherical(xkind, tile_i, src_buf, dst_buf):
        p.do("scalar", lambda sc: sc.activation(out=sb["xnt"][:], in_=sb[src_buf][:],
                                                func=AF.Square),
             reads=[src_buf], writes=["xnt"])
        for c in range(NCH):
            p.do("tensor", lambda t, c=c: t.matmul(
                ps["log"][0:16, 0:TILE], sb["hsel"][:, c * 16:(c + 1) * 16],
                sb["xnt"][:, c * TILE:(c + 1) * TILE], start=(c == 0), stop=(c == NCH - 1)),
                reads=["xnt", "hsel"], writes=["ps_log"])
        p.do("scalar", lambda sc: sc.activation(out=sb["logsb"][0:16, 0:TILE], in_=ps["log"][0:16, 0:TILE],
                                                func=AF.Sqrt),
             reads=["ps_log"], writes=["logsb"])
        p.do("vector", lambda v: v.tensor_scalar_add(out=sb["logsb"][0:16, 0:TILE], in0=sb["logsb"][0:16, 0:TILE],
                                                     scalar1=1e-8),
             reads=["logsb"], writes=["logsb"])
        p.do("vector", lambda v: v.reciprocal(out=sb["logsb"][0:16, 0:TILE], in_=sb["logsb"][0:16, 0:TILE]),
             reads=["logsb"], writes=["logsb"])
        for c in range(NCH):
            p.do("tensor", lambda t, c=c: t.matmul(
                ps["bmu"][:, 0:TILE], sb["gsel"][0:16, c * 128:(c + 1) * 128],
                sb["logsb"][0:16, 0:TILE], start=True, stop=True),
                reads=["gsel", "logsb"], writes=["ps_bmu"])
            p.do("vector", lambda v, c=c: v.tensor_mul(
                out=sb[src_buf][:, c * TILE:(c + 1) * TILE],
                in0=sb[src_buf][:, c * TILE:(c + 1) * TILE], in1=ps["bmu"][:, 0:TILE]),
                reads=[src_buf, "ps_bmu"], writes=[src_buf])

    def proj_feats(src_buf, h, fc, out_buf):
        hp, hr = h // 2, (h % 2) * 64
        p.do("tensor", lambda t, hp=hp, hr=hr, h=h, fc=fc: t.matmul(
            ps["brr"][:, 0:TILE],
            sb["projb"][hr:hr + 64, hp * NF + fc * 128:hp * NF + (fc + 1) * 128],
            sb[src_buf][hr:hr + 64, hp * TILE:(hp + 1) * TILE], start=True, stop=True),
            reads=["projb", src_buf], writes=["ps_brr"])
        p.do("vector", lambda v: v.tensor_scalar(
            out=sb[out_buf][:], in0=ps["brr"][:, 0:TILE], scalar1=0.0, scalar2=YAT_EPS,
            op0=ALU.max, op1=ALU.add),
            reads=["ps_brr"], writes=[out_buf])

    for i in range(NT):
        xb_ = "xnta" if i % 2 == 0 else "xntb"
        p.do("sync", lambda sy, i=i, xb_=xb_: sy.dma_start(
            out=sb[xb_][:].rearrange("p (c t) -> p c t", c=NCH),
            in_=_chunks(d["xnT"])[:, :, i * TILE:(i + 1) * TILE]),
            reads=["xnT"], writes=[xb_], dma=True)
        for dstb, wname, bname in (("ksb", "wk", "bks"), ("vsb", "wv", "bvs")):
            for mc in range(NCH):
                for kc in range(NCH):
                    p.do("tensor", lambda t, mc=mc, kc=kc, wname=wname, xb_=xb_: t.matmul(
                        ps["bmu"][:, 0:TILE], sb[wname][:, (kc * NCH + mc) * 128:(kc * NCH + mc) * 128 + 128],
                        sb[xb_][:, kc * TILE:(kc + 1) * TILE],
                        start=(kc == 0), stop=(kc == NCH - 1)),
                        reads=[wname, xb_], writes=["ps_bmu"])
                p.do("vector", lambda v, mc=mc, dstb=dstb, bname=bname: v.tensor_scalar_add(
                    out=sb[dstb][:, mc * TILE:(mc + 1) * TILE], in0=ps["bmu"][:, 0:TILE],
                    scalar1=sb[bname][:, mc:mc + 1]),
                    reads=["ps_bmu", bname], writes=[dstb])
        for j in range(TILE // 128):
            for c in range(NCH):
                p.do("tensor", lambda t, j=j, c=c: t.transpose(
                    ps["ptb"][:, c * 128:(c + 1) * 128],
                    sb["vsb"][:, c * TILE + j * 128:c * TILE + (j + 1) * 128],
                    sb["identb"][:]),
                    reads=["vsb", "identb"], writes=["ps_ptb"])
            p.do("vector", lambda v, j=j: v.tensor_copy(
                out=sb["vtok"][:, j * D:(j + 1) * D], in_=ps["ptb"][:]),
                reads=["ps_ptb"], writes=["vtok"])
        spherical("K", i, "ksb", None)
        for h in range(H):
            for fc in range(2):
                proj_feats("ksb", h, fc, "kp")
                col = (h * 2 + fc)
                p.do("vector", lambda v, col=col: v.reduce_sum(
                    out=sb["kps"][:, 0:1], in_=sb["kp"][:], axis=AX.X),
                    reads=["kp"], writes=["kps"])
                p.do("vector", lambda v, col=col: v.tensor_add(
                    out=sb["ksum"][:, col:col + 1], in0=sb["ksum"][:, col:col + 1],
                    in1=sb["kps"][:, 0:1]),
                    reads=["ksum", "kps"], writes=["ksum"])
                for j in range(TILE // 128):
                    p.do("tensor", lambda t, j=j: t.transpose(
                        ps["ptb"][:, j * 128:(j + 1) * 128],
                        sb["kp"][:, j * 128:(j + 1) * 128], sb["identb"][:]),
                        reads=["kp", "identb"], writes=["ps_ptb"])
                p.do("vector", lambda v: v.tensor_copy(out=sb["tokrow"][:, 0:512], in_=ps["ptb"][:, 0:512]),
                     reads=["ps_ptb"], writes=["tokrow"])
                for j in range(TILE // 128):
                    p.do("tensor", lambda t, j=j, h=h: t.matmul(
                        ps["m64"][:, 0:E0], sb["tokrow"][:, j * 128:(j + 1) * 128],
                        sb["vtok"][:, j * D + h * 64:j * D + (h + 1) * 64],
                        start=(j == 0), stop=(j == TILE // 128 - 1)),
                        reads=["tokrow", "vtok"], writes=["ps_brd"])
                p.do("vector", lambda v, col=col: v.tensor_add(
                    out=sb["xb"][:, col * 64:(col + 1) * 64],
                    in0=sb["xb"][:, col * 64:(col + 1) * 64], in1=ps["m64"][:, 0:E0]),
                    reads=["xb", "ps_brd"], writes=["xb"])

    p.do("vector", lambda v: v.tensor_copy(out=sb["vsb"][:, 0:2048], in_=sb["xb"][:, 0:2048]),
         reads=["xb"], writes=["vsb"])
    p.do("vector", lambda v: v.tensor_copy(out=sb["ksum16"][:], in_=sb["ksum"][:]),
         reads=["ksum"], writes=["ksum16"])

    for i in range(NT):
        xb_ = "xnta" if i % 2 == 0 else "xntb"
        p.do("sync", lambda sy, i=i, xb_=xb_: sy.dma_start(
            out=sb[xb_][:].rearrange("p (c t) -> p c t", c=NCH),
            in_=_chunks(d["xnT"])[:, :, i * TILE:(i + 1) * TILE]),
            reads=["xnT"], writes=[xb_], dma=True)
        for mc in range(NCH):
            for kc in range(NCH):
                p.do("tensor", lambda t, mc=mc, kc=kc, xb_=xb_: t.matmul(
                    ps["bmu"][:, 0:TILE], sb["wq"][:, (kc * NCH + mc) * 128:(kc * NCH + mc) * 128 + 128],
                    sb[xb_][:, kc * TILE:(kc + 1) * TILE],
                    start=(kc == 0), stop=(kc == NCH - 1)),
                    reads=["wq", xb_], writes=["ps_bmu"])
            p.do("vector", lambda v, mc=mc: v.tensor_scalar_add(
                out=sb["ksb"][:, mc * TILE:(mc + 1) * TILE], in0=ps["bmu"][:, 0:TILE],
                scalar1=sb["bqs"][:, mc:mc + 1]),
                reads=["ps_bmu", "bqs"], writes=["ksb"])
        spherical("Q", i, "ksb", None)
        for h in range(H):
            for fc in range(2):
                proj_feats("ksb", h, fc, "kp")
                col = h * 2 + fc
                p.do("tensor", lambda t, col=col, fc=fc: t.matmul(
                    ps["log"][0:E0, 0:TILE], sb["vsb"][:, col * 64:(col + 1) * 64],
                    sb["kp"][:], start=(fc == 0), stop=(fc == 1)),
                    reads=["vsb", "kp"], writes=["ps_log"])
                p.do("tensor", lambda t, col=col, fc=fc: t.matmul(
                    ps["stat"][0:1, 0:TILE], sb["ksum16"][:, col:col + 1],
                    sb["kp"][:], start=(fc == 0), stop=(fc == 1)),
                    reads=["ksum16", "kp"], writes=["ps_stat"])
            p.do("vector", lambda v: v.tensor_scalar_max(
                out=sb["murow"][0:1, :], in0=ps["stat"][0:1, 0:TILE], scalar1=1e-6),
                reads=["ps_stat"], writes=["murow"])
            p.do("vector", lambda v: v.reciprocal(out=sb["murow"][0:1, :], in_=sb["murow"][0:1, :]),
                 reads=["murow"], writes=["murow"])
            p.do("tensor", lambda t: t.matmul(
                ps["bmu"][:, 0:TILE], sb["ones_row"][:], sb["murow"][0:1, :],
                start=True, stop=True),
                reads=["ones_row", "murow"], writes=["ps_bmu"])
            hp, hr = h // 2, (h % 2) * 64
            p.do("vector", lambda v: v.tensor_copy(out=sb["xsq"][:, 0:TILE], in_=ps["bmu"][:, 0:TILE]),
                 reads=["ps_bmu"], writes=["xsq"])
            p.do("vector", lambda v, hp=hp, hr=hr: v.tensor_mul(
                out=sb["asb"][hr:hr + 64, hp * TILE:(hp + 1) * TILE],
                in0=ps["log"][0:E0, 0:TILE], in1=sb["xsq"][0:64, 0:TILE]),
                reads=["ps_log", "xsq"], writes=["asb"])
        for mc in range(NCH):
            for kc in range(NCH):
                p.do("tensor", lambda t, mc=mc, kc=kc: t.matmul(
                    ps["brr"][:, 0:TILE], sb["wo"][:, (kc * NCH + mc) * 128:(kc * NCH + mc) * 128 + 128],
                    sb["asb"][:, kc * TILE:(kc + 1) * TILE],
                    start=(kc == 0), stop=(kc == NCH - 1)),
                    reads=["wo", "asb"], writes=["ps_brr"])
            p.do("vector", lambda v, mc=mc: v.tensor_scalar_add(
                out=sb["xa"][:, mc * TILE:(mc + 1) * TILE], in0=ps["brr"][:, 0:TILE],
                scalar1=sb["bos"][:, mc:mc + 1]),
                reads=["ps_brr", "bos"], writes=["xa"])
        for j in range(TILE // 128):
            for half in range(2):
                for c4 in range(4):
                    c = half * 4 + c4
                    p.do("tensor", lambda t, j=j, c=c, c4=c4: t.transpose(
                        ps["bmu"][:, c4 * 128:(c4 + 1) * 128],
                        sb["xa"][:, c * TILE + j * 128:c * TILE + (j + 1) * 128],
                        sb["identf"][:]),
                        reads=["xa", "identf"], writes=["ps_bmu"])
                p.do("vector", lambda v, half=half: v.tensor_copy(
                    out=sb["xsq"][:, 512 + half * 512:512 + (half + 1) * 512], in_=ps["bmu"][:, 0:512]),
                    reads=["ps_bmu"], writes=["atok"])
            p.do("gpsimd", lambda g, i=i, j=j: g.dma_start(
                out=d["x1"][i * TILE + j * 128:i * TILE + (j + 1) * 128, :],
                in_=sb["xsq"][:, 512:1536], accum_op=ALU.add),
                reads=["atok", "x1"], writes=["x1"], dma=True)
    return p


P2_SBUF = {
    "wq": ([128, NCH * NCH * 128], BF16), "wk": ([128, NCH * NCH * 128], BF16),
    "wv": ([128, NCH * NCH * 128], BF16), "wo": ([128, NCH * NCH * 128], BF16),
    "projb": ([128, 8 * NF], BF16),
    "bqs": ([128, NCH], F32), "bks": ([128, NCH], F32),
    "bvs": ([128, NCH], F32), "bos": ([128, NCH], F32),
    "hsel": ([128, NCH * 16], BF16), "hself": ([128, NCH * 16], F32),
    "gsel": ([16, NCH * 128], F32),
    "identf": ([128, 128], F32), "identb": ([128, 128], BF16),
    "ones_row": ([1, 128], F32),
    "xnta": ([128, NCH * 512], BF16), "xntb": ([128, NCH * 512], BF16),
    "ksb": ([128, NCH * 512], BF16), "vsb": ([128, NCH * 512], BF16),

    "kp": ([128, 512], BF16), "kps": ([128, 1], F32),
    "ksum": ([128, 32], F32), "ksum16": ([128, 32], BF16),
    "asb": ([128, NCH * 512], BF16),
}

P2_PSUM = {}


def build_p3(p, d, sb, ps):
    TILE = 512
    NT = TB // TILE
    NM1 = SF // 128
    for half, dst in ((0, "wq"), (1, "wk")):
        p.do("gpsimd", lambda g, half=half, dst=dst: g.dma_start(
            out=sb[dst][:].rearrange("p (kc mc m) -> p kc mc m", kc=NCH, mc=NCH),
            in_=d["sw1T"].rearrange("(kc p) (mc m) -> p kc mc m", p=128, m=128)[:, :, half * 8:(half + 1) * 8, :]),
            reads=["sw1T"], writes=[dst], dma=True)
    for half, dst in ((0, "wv"), (1, "wo")):
        p.do("gpsimd", lambda g, half=half, dst=dst: g.dma_start(
            out=sb[dst][:].rearrange("p (kc mc m) -> p kc mc m", kc=NCH, mc=NCH),
            in_=d["sw2T"].rearrange("(kc p) (mc m) -> p kc mc m", p=128, m=128)[:, half * 8:(half + 1) * 8, :, :]),
            reads=["sw2T"], writes=[dst], dma=True)

    for i in range(NT):
        xb_ = "xnta" if i % 2 == 0 else "xntb"
        p.do("sync", lambda sy, i=i, xb_=xb_: sy.dma_start(
            out=sb[xb_][:].rearrange("p (c t) -> p c t", c=NCH),
            in_=_chunks(d["xnT"])[:, :, i * TILE:(i + 1) * TILE]),
            reads=["xnT"], writes=[xb_], dma=True)
        for mc in range(NM1):
            wname = "wq" if mc < 8 else "wk"
            mloc = mc % 8
            for kc in range(NCH):
                p.do("tensor", lambda t, mc=mc, kc=kc, wname=wname, mloc=mloc, xb_=xb_: t.matmul(
                    ps["bmu"][:, 0:TILE], sb[wname][:, (kc * NCH + mloc) * 128:(kc * NCH + mloc) * 128 + 128],
                    sb[xb_][:, kc * TILE:(kc + 1) * TILE],
                    start=(kc == 0), stop=(kc == NCH - 1)),
                    reads=[wname, xb_], writes=["ps_bmu"])
            dstb = "ksb" if mc < 8 else "vsb"
            p.do("scalar", lambda sc, mc=mc, dstb=dstb, mloc=mloc: sc.activation(
                out=sb[dstb][:, mloc * TILE:(mloc + 1) * TILE], in_=ps["bmu"][:, 0:TILE],
                func=AF.Gelu_apprx_tanh),
                reads=["ps_bmu"], writes=[dstb])
        for mc in range(NCH):
            for kc in range(SF // 128):
                wname = "wv" if kc < 8 else "wo"
                kloc = kc % 8
                sname = "ksb" if kc < 8 else "vsb"
                p.do("tensor", lambda t, mc=mc, kc=kc, wname=wname, kloc=kloc, sname=sname: t.matmul(
                    ps["brr"][:, 0:TILE], sb[wname][:, (kloc * NCH + mc) * 128:(kloc * NCH + mc) * 128 + 128],
                    sb[sname][:, (kloc) * TILE:(kloc + 1) * TILE],
                    start=(kc == 0), stop=(kc == SF // 128 - 1)),
                    reads=[wname, sname], writes=["ps_brr"])
            p.do("vector", lambda v, mc=mc: v.tensor_copy(
                out=sb["xa"][:, mc * TILE:(mc + 1) * TILE], in_=ps["brr"][:, 0:TILE]),
                reads=["ps_brr"], writes=["xa"])
        for j in range(TILE // 128):
            for half in range(2):
                for c4 in range(4):
                    c = half * 4 + c4
                    p.do("tensor", lambda t, j=j, c=c, c4=c4: t.transpose(
                        ps["bmu"][:, c4 * 128:(c4 + 1) * 128],
                        sb["xa"][:, c * TILE + j * 128:c * TILE + (j + 1) * 128],
                        sb["identf"][:]),
                        reads=["xa", "identf"], writes=["ps_bmu"])
                p.do("vector", lambda v, half=half: v.tensor_copy(
                    out=sb["xsq"][:, 512 + half * 512:512 + (half + 1) * 512], in_=ps["bmu"][:, 0:512]),
                    reads=["ps_bmu"], writes=["atok"])
            p.do("gpsimd", lambda g, i=i, j=j: g.dma_start(
                out=d["x1"][i * TILE + j * 128:i * TILE + (j + 1) * 128, :],
                in_=sb["xsq"][:, 512:1536], accum_op=ALU.add),
                reads=["atok", "x1"], writes=["x1"], dma=True)
    return p


P3_SBUF = {}
P3_PSUM = {}


def build_p4(p, d, sb, ps):
    NS3 = CAP // 128
    for e in range(E0):
        p.do("sync", lambda sy, e=e: sy.dma_start(
            out=sb["xb"][:].rearrange("p (kc mc m) -> p kc mc m", kc=NCH, mc=4),
            in_=d["w1"][e].rearrange("(kc p) (mc m) -> p kc mc m", p=128, m=128)),
            reads=["w1"], writes=["xb"], dma=True)
        p.do("scalar", lambda sc: sc.activation(out=sb["ksb"][:], in_=sb["xb"][:],
                                                func=AF.Copy),
             reads=["xb"], writes=["ksb"])
        p.do("sync", lambda sy, e=e: sy.dma_start(
            out=sb["xa"][:].rearrange("p (kc dd) -> p kc dd", kc=4),
            in_=d["w2"][e].rearrange("(kc p) dd -> p kc dd", p=128)),
            reads=["w2"], writes=["xa"], dma=True)
        p.do("vector", lambda v: v.tensor_copy(out=sb["vsb"][:], in_=sb["xa"][:]),
             reads=["xa"], writes=["vsb"])
        p.do("scalar", lambda sc, e=e: sc.dma_start(
            out=sb["tmw"][:].rearrange("p (s c) -> p s c", s=NS3),
            in_=d["Tmap_w"][e * CAP:(e + 1) * CAP, :].rearrange("(s p) c -> p s c", p=128)),
            reads=["Tmap_w"], writes=["tmw"], dma=True)
        for s3 in range(NS3):
            p.do("vector", lambda v, s3=s3: v.tensor_copy(
                out=sb["idxu"][:, s3:s3 + 1], in_=sb["tmw"][:, 2 * s3:2 * s3 + 1]),
                reads=["tmw"], writes=["idxu"])
        for s3 in range(NS3):
            p.do("gpsimd", lambda g, s3=s3: g.indirect_dma_start(
                out=sb["tokrow"][:], out_offset=None,
                in_=d["xn_tok"], in_offset=IndirectOffsetOnAxis(sb["idxu"][:, s3:s3 + 1], 0)),
                reads=["xn_tok", "idxu"], writes=["tokrow"], dma=True)
            for c in range(NCH):
                p.do("tensor", lambda t, c=c: t.transpose(
                    ps["ptb"][:, c * 128:(c + 1) * 128],
                    sb["tokrow"][:, c * 128:(c + 1) * 128], sb["identb"][:]),
                    reads=["tokrow", "identb"], writes=["ps_ptb"])
            p.do("vector", lambda v, s3=s3: v.tensor_copy(
                out=sb["asb"][:, 0:NCH * CAP].rearrange("p (c t) -> p c t", c=NCH, t=CAP)[:, :, s3 * 128:(s3 + 1) * 128],
                in_=ps["ptb"][:].rearrange("p (c t) -> p c t", c=NCH, t=128)),
                reads=["ps_ptb"], writes=["asb"])
        for mc in range(4):
            for kc in range(NCH):
                p.do("tensor", lambda t, mc=mc, kc=kc: t.matmul(
                    ps["bmu"][:, 0:CAP], sb["ksb"][:, (kc * 4 + mc) * 128:(kc * 4 + mc) * 128 + 128],
                    sb["asb"][:, kc * CAP:(kc + 1) * CAP],
                    start=(kc == 0), stop=(kc == NCH - 1)),
                    reads=["ksb", "asb"], writes=["ps_bmu"])
            p.do("scalar", lambda sc, mc=mc: sc.activation(
                out=sb["vtok"][:, mc * CAP:(mc + 1) * CAP], in_=ps["bmu"][:, 0:CAP],
                func=AF.Gelu_apprx_tanh),
                reads=["ps_bmu"], writes=["vtok"])
        for s3 in range(NS3):
            for dh in range(2):
                for mc in range(4):
                    p.do("tensor", lambda t, s3=s3, dh=dh, mc=mc: t.matmul(
                        ps["brr"][:, 0:512],
                        sb["vtok"][:, mc * CAP + s3 * 128:mc * CAP + (s3 + 1) * 128],
                        sb["vsb"][:, mc * D + dh * 512:mc * D + (dh + 1) * 512],
                        start=(mc == 0), stop=(mc == 3)),
                        reads=["vtok", "vsb"], writes=["ps_brr"])
                p.do("vector", lambda v, s3=s3, dh=dh: v.tensor_scalar_mul(
                    out=sb["xsq"][:, dh * 512:(dh + 1) * 512], in0=ps["brr"][:, 0:512],
                    scalar1=sb["tmw"][:, 2 * s3 + 1:2 * s3 + 2]),
                    reads=["ps_brr", "tmw"], writes=["xsq", "atok"])
            p.do("gpsimd", lambda g, s3=s3: g.indirect_dma_start(
                out=d["x1"], out_offset=IndirectOffsetOnAxis(sb["idxu"][:, s3:s3 + 1], 0),
                in_=sb["xsq"][:, 0:1024], in_offset=None, compute_op=ALU.add),
                reads=["xsq", "atok", "idxu", "x1"], writes=["x1"], dma=True)
    return p


P4_SBUF = {
    "tmw": ([128, 2 * (CAP // 128)], F32),
    "idxu": ([128, CAP // 128], U32),
}
P4_PSUM = {}


_FT = ("wq", "wk", "wv", "wo")
_XN2 = ("ksb", "vsb", "asb", "xnt")


def _ft_slice(sb, c, t0, t1):
    return sb[_FT[c // 2]][:, (c % 2) * 4096 + t0:(c % 2) * 4096 + t1]


def _xn2_slice(sb, c, t0, t1):
    return sb[_XN2[c // 2]][:, (c % 2) * TC + t0:(c % 2) * TC + t1]


def build_p5(p, d, sb, ps):
    GT = 512
    p.do("scalar", lambda sc: sc.dma_start(out=sb["bfs"][:], in_=d["bfv"]),
         reads=["bfv"], writes=["bfs"], dma=True)
    for half, dst in ((0, "xntb"), (1, "vtok")):
        p.do("gpsimd", lambda g_, half=half, dst=dst: g_.dma_start(
            out=sb[dst][:].rearrange("p (kc mc m) -> p kc mc m", kc=4, mc=NCH),
            in_=d["WfT"].rearrange("(kc p) (mc m) -> p kc mc m", p=128, m=128)[:, half * 4:(half + 1) * 4, :, :]),
            reads=["WfT"], writes=[dst], dma=True)
    for g in range(TB // GT):
        for j in range(4):
            lt = g * 4 + j
            p.do("scalar", lambda sc, lt=lt: sc.dma_start(
                out=sb["idxq"][:], in_=d["rot_idx"][lt]),
                reads=["rot_idx"], writes=["idxq"], dma=True)
            p.do("gpsimd", lambda g_: g_.indirect_dma_start(
                out=sb["xsq"][:, 0:D], out_offset=None,
                in_=d["x1"], in_offset=IndirectOffsetOnAxis(sb["idxq"][:], 0)),
                reads=["x1", "idxq"], writes=["xsq", "atok"], dma=True)
            p.do("vector", lambda v: v.reduce_sum(out=sb["mrun"][:, 0:1],
                                                  in_=sb["xsq"][:, 0:D], axis=AX.X),
                 reads=["xsq", "atok"], writes=["mrun"])
            p.do("vector", lambda v: v.tensor_scalar_mul(out=sb["mrun"][:, 0:1],
                                                         in0=sb["mrun"][:, 0:1], scalar1=1.0 / D),
                 reads=["mrun"], writes=["mrun"])
            p.do("vector", lambda v: v.tensor_scalar(
                out=sb["xsq"][:, 0:D], in0=sb["xsq"][:, 0:D],
                scalar1=sb["mrun"][:, 0:1], scalar2=None, op0=ALU.subtract),
                reads=["xsq", "mrun", "atok"], writes=["xsq", "atok"])
            p.do("scalar", lambda sc: sc.activation(out=sb["xa"][:, 0:D],
                                                    in_=sb["xsq"][:, 0:D], func=AF.Square),
                 reads=["xsq", "atok"], writes=["xa"])
            p.do("vector", lambda v: v.reduce_sum(out=sb["lrun"][:, 0:1],
                                                  in_=sb["xa"][:, 0:D], axis=AX.X),
                 reads=["xa"], writes=["lrun"])
            p.do("vector", lambda v: v.tensor_scalar(
                out=sb["lrun"][:, 0:1], in0=sb["lrun"][:, 0:1],
                scalar1=1.0 / D, scalar2=LN_EPS, op0=ALU.mult, op1=ALU.add),
                reads=["lrun"], writes=["lrun"])
            p.do("scalar", lambda sc: sc.activation(out=sb["lrun"][:, 1:2],
                                                    in_=sb["lrun"][:, 0:1], func=AF.Sqrt),
                 reads=["lrun"], writes=["lrun"])
            p.do("vector", lambda v: v.reciprocal(out=sb["lrun"][:, 2:3], in_=sb["lrun"][:, 1:2]),
                 reads=["lrun"], writes=["lrun"])
            p.do("vector", lambda v: v.tensor_scalar_mul(
                out=sb["tokrow"][:], in0=sb["xsq"][:, 0:D], scalar1=sb["lrun"][:, 2:3]),
                reads=["xsq", "lrun", "atok"], writes=["tokrow"])
            if lt < TC // 128:
                p.do("scalar", lambda sc, lt=lt: sc.dma_start(
                    out=d["xn2_tok"][lt * 128:(lt + 1) * 128, :], in_=sb["tokrow"][:]),
                    reads=["tokrow"], writes=["xn2_tok"], dma=True)
            for c in range(NCH):
                p.do("tensor", lambda t, c=c: t.transpose(
                    ps["ptb"][:, c * 128:(c + 1) * 128],
                    sb["tokrow"][:, c * 128:(c + 1) * 128], sb["identb"][:]),
                    reads=["tokrow", "identb"], writes=["ps_ptb"])
            p.do("vector", lambda v, j=j: v.tensor_copy(
                out=sb["xnta"][:].rearrange("p (c t) -> p c t", c=NCH, t=GT)[:, :, j * 128:(j + 1) * 128],
                in_=ps["ptb"][:].rearrange("p (c t) -> p c t", c=NCH, t=128)),
                reads=["ps_ptb"], writes=["xnta"])
        if g * GT < TC:
            for c in range(NCH):
                p.do("vector", lambda v, c=c, g=g: v.tensor_copy(
                    out=_xn2_slice(sb, c, g * GT, (g + 1) * GT),
                    in_=sb["xnta"][:].rearrange("p (c t) -> p c t", c=NCH, t=GT)[:, c, :]),
                    reads=["xnta"], writes=[_XN2[c // 2]])
        for mc in range(NCH):
            for kc in range(NCH):
                wsrc = "xntb" if kc < 4 else "vtok"
                kc4 = kc % 4
                p.do("tensor", lambda t, mc=mc, kc=kc, kc4=kc4, wsrc=wsrc: t.matmul(
                    ps["bmu"][:, 0:GT], sb[wsrc][:, (kc4 * NCH + mc) * 128:(kc4 * NCH + mc) * 128 + 128],
                    sb["xnta"][:].rearrange("p (c t) -> p c t", c=NCH, t=GT)[:, kc, :],
                    start=(kc == 0), stop=(kc == NCH - 1)),
                    reads=[wsrc, "xnta"], writes=["ps_bmu"])
            p.do("vector", lambda v, mc=mc, g=g: v.tensor_scalar_add(
                out=_ft_slice(sb, mc, g * GT, (g + 1) * GT), in0=ps["bmu"][:, 0:GT],
                scalar1=sb["bfs"][:, mc:mc + 1]),
                reads=["ps_bmu", "bfs"], writes=[_FT[mc // 2]])
        for j in range(4):
            for c in range(NCH):
                p.do("tensor", lambda t, c=c, g=g, j=j: t.transpose(
                    ps["ptb"][:, c * 128:(c + 1) * 128],
                    _ft_slice(sb, c, g * GT + j * 128, g * GT + (j + 1) * 128),
                    sb["identb"][:]),
                    reads=[_FT[c // 2], "identb"], writes=["ps_ptb"])
            p.do("vector", lambda v: v.tensor_copy(out=sb["tokrow"][:], in_=ps["ptb"][:]),
                 reads=["ps_ptb"], writes=["tokrow"])
            p.do("scalar", lambda sc, g=g, j=j: sc.dma_start(
                out=d["f_tok"][g * GT + j * 128:g * GT + (j + 1) * 128, :],
                in_=sb["tokrow"][:]),
                reads=["tokrow"], writes=["f_tok"], dma=True)
    return p


def build_p6(p, d, sb, ps):
    p.do("gpsimd", lambda g_: g_.iota(sb["xsq"][:, 0:512], pattern=[[-1, 512]], base=0,
                                      channel_multiplier=1, allow_small_or_imprecise_dtypes=True),
         writes=["xsq", "atok"])
    p.do("scalar", lambda sc: sc.dma_start(out=sb["kshf"][:], in_=d["kshift"]),
         reads=["kshift"], writes=["kshf"], dma=True)
    for qg in range(2):
        accbuf = {qi: ("xa" if qi < 4 else "xb", (qi % 4) * 1024) for qi in range(8)}
        for qi in range(8):
            bufn, c0 = accbuf[qi]
            p.do("vector", lambda v, bufn=bufn, c0=c0: v.memset(sb[bufn][:, c0:c0 + 1024], 0.0),
                 writes=[bufn])
        p.do("vector", lambda v, qg=qg: v.memset(sb["mrun"][:, qg * 8:(qg + 1) * 8], -1e30),
             writes=["mrun"])
        p.do("vector", lambda v, qg=qg: v.memset(sb["lrun"][:, qg * 8:(qg + 1) * 8], 0.0),
             writes=["lrun"])
        for kt in range(TB // 512):
            p.do("sync", lambda sy, kt=kt: sy.dma_start(
                out=sb["vtok"][:].rearrange("p (j t) -> p j t", j=4),
                in_=d["f_tok"][kt * 512:(kt + 1) * 512, :].rearrange("(j p) t -> p j t", p=128)),
                reads=["f_tok"], writes=["vtok"], dma=True)
            for qi in range(8):
                qt = qg * 8 + qi
                bufn, c0 = accbuf[qi]
                for c in range(NCH):
                    p.do("tensor", lambda t, c=c, qt=qt, kt=kt: t.matmul(
                        ps["brr"][:, 0:512],
                        _ft_slice(sb, c, qt * 128, (qt + 1) * 128),
                        _ft_slice(sb, c, kt * 512, (kt + 1) * 512),
                        start=(c == 0), stop=(c == NCH - 1)),
                        reads=[_FT[c // 2]], writes=["ps_brr"])
                base = float(qt * 128 - kt * 512)
                p.do("vector", lambda v, base=base: v.tensor_scalar_add(
                    out=sb["xsq"][:, 512:1024], in0=sb["xsq"][:, 0:512], scalar1=base),
                    reads=["xsq", "atok"], writes=["xsq", "atok"])
                p.do("vector", lambda v, kt=kt: v.tensor_scalar_add(
                    out=sb["xsq"][:, 512:1024], in0=sb["xsq"][:, 512:1024],
                    scalar1=sb["kshf"][:, kt:kt + 1]),
                    reads=["xsq", "kshf", "atok"], writes=["xsq", "atok"])
                p.do("scalar", lambda sc: sc.activation(out=sb["xsq"][:, 512:1024],
                                                        in_=sb["xsq"][:, 512:1024], func=AF.Abs),
                     reads=["xsq", "atok"], writes=["xsq", "atok"])
                p.do("vector", lambda v: v.tensor_scalar_min(
                    out=sb["xsq"][:, 512:1024], in0=sb["xsq"][:, 512:1024], scalar1=10.0),
                    reads=["xsq", "atok"], writes=["xsq", "atok"])
                p.do("vector", lambda v: v.scalar_tensor_tensor(
                    out=sb["xsq"][:, 1024:1536], in0=sb["xsq"][:, 512:1024], scalar=0.01,
                    in1=ps["brr"][:, 0:512], op0=ALU.mult, op1=ALU.add),
                    reads=["xsq", "ps_brr", "atok"], writes=["xsq", "atok"])
                p.do("vector", lambda v: v.reduce_max(out=sb["scr"][:, 0:1],
                                                      in_=sb["xsq"][:, 1024:1536], axis=AX.X),
                     reads=["xsq", "atok"], writes=["scr"])
                p.do("vector", lambda v, qt=qt: v.tensor_max(
                    out=sb["scr"][:, 1:2], in0=sb["mrun"][:, qt:qt + 1], in1=sb["scr"][:, 0:1]),
                    reads=["mrun", "scr"], writes=["scr"])
                p.do("vector", lambda v, qt=qt: v.tensor_sub(
                    out=sb["scr"][:, 2:3], in0=sb["mrun"][:, qt:qt + 1], in1=sb["scr"][:, 1:2]),
                    reads=["mrun", "scr"], writes=["scr"])
                p.do("scalar", lambda sc: sc.activation(out=sb["scr"][:, 3:4],
                                                        in_=sb["scr"][:, 2:3], func=AF.Exp),
                     reads=["scr"], writes=["scr"])
                p.do("vector", lambda v: v.tensor_scalar_mul(
                    out=sb["scr"][:, 4:5], in0=sb["scr"][:, 1:2], scalar1=-1.0),
                    reads=["scr"], writes=["scr"])
                p.do("scalar", lambda sc: sc.activation(
                    out=sb["kp"][:], in_=sb["xsq"][:, 1024:1536], func=AF.Exp,
                    bias=sb["scr"][:, 4:5], accum_out=sb["scr"][:, 5:6]),
                    reads=["xsq", "scr", "atok"], writes=["kp", "scr"])
                p.do("vector", lambda v, qt=qt: v.scalar_tensor_tensor(
                    out=sb["lrun"][:, qt:qt + 1], in0=sb["lrun"][:, qt:qt + 1],
                    scalar=sb["scr"][:, 3:4], in1=sb["scr"][:, 5:6],
                    op0=ALU.mult, op1=ALU.add),
                    reads=["lrun", "scr"], writes=["lrun"])
                p.do("vector", lambda v, qt=qt: v.tensor_copy(
                    out=sb["mrun"][:, qt:qt + 1], in_=sb["scr"][:, 1:2]),
                    reads=["scr"], writes=["mrun"])
                p.do("vector", lambda v, bufn=bufn, c0=c0: v.tensor_scalar_mul(
                    out=sb[bufn][:, c0:c0 + 1024], in0=sb[bufn][:, c0:c0 + 1024],
                    scalar1=sb["scr"][:, 3:4]),
                    reads=[bufn, "scr"], writes=[bufn])
                for j in range(4):
                    p.do("tensor", lambda t, j=j: t.transpose(
                        ps["ptb"][:, j * 128:(j + 1) * 128],
                        sb["kp"][:, j * 128:(j + 1) * 128], sb["identb"][:]),
                        reads=["kp", "identb"], writes=["ps_ptb"])
                p.do("vector", lambda v: v.tensor_copy(out=sb["tokrow"][:, 0:512],
                                                       in_=ps["ptb"][:, 0:512]),
                     reads=["ps_ptb"], writes=["tokrow"])
                for dh in range(2):
                    for j in range(4):
                        p.do("tensor", lambda t, j=j, dh=dh: t.matmul(
                            ps["bmu"][:, 0:512], sb["tokrow"][:, j * 128:(j + 1) * 128],
                            sb["vtok"][:].rearrange("p (j t) -> p j t", j=4)[:, j, dh * 512:(dh + 1) * 512],
                            start=(j == 0), stop=(j == 3)),
                            reads=["tokrow", "vtok"], writes=["ps_bmu"])
                    p.do("vector", lambda v, bufn=bufn, c0=c0, dh=dh: v.tensor_add(
                        out=sb[bufn][:, c0 + dh * 512:c0 + (dh + 1) * 512],
                        in0=sb[bufn][:, c0 + dh * 512:c0 + (dh + 1) * 512],
                        in1=ps["bmu"][:, 0:512]),
                        reads=[bufn, "ps_bmu"], writes=[bufn])
        for qi in range(8):
            qt = qg * 8 + qi
            bufn, c0 = accbuf[qi]
            p.do("vector", lambda v, qt=qt: v.reciprocal(out=sb["scr"][:, 6:7],
                                                         in_=sb["lrun"][:, qt:qt + 1]),
                 reads=["lrun"], writes=["scr"])
            p.do("vector", lambda v, bufn=bufn, c0=c0: v.tensor_scalar_mul(
                out=sb["xsq"][:, 1536:2560], in0=sb[bufn][:, c0:c0 + 1024],
                scalar1=sb["scr"][:, 6:7]),
                reads=[bufn, "scr"], writes=["xsq", "atok"])
            p.do("scalar", lambda sc, qt=qt: sc.dma_start(
                out=sb["idxq"][:], in_=d["rot_idx"][qt]),
                reads=["rot_idx"], writes=["idxq"], dma=True)
            p.do("gpsimd", lambda g_: g_.indirect_dma_start(
                out=sb["xsq"][:, 2560:3584], out_offset=None,
                in_=d["x1"], in_offset=IndirectOffsetOnAxis(sb["idxq"][:], 0)),
                reads=["x1", "idxq"], writes=["xsq", "atok"], dma=True)
            p.do("vector", lambda v: v.tensor_add(
                out=sb["xsq"][:, 1536:2560], in0=sb["xsq"][:, 1536:2560],
                in1=sb["xsq"][:, 2560:3584]),
                reads=["xsq", "atok"], writes=["xsq", "atok"])
            p.do("sync", lambda sy, qt=qt: sy.dma_start(
                out=d["y"][qt * 128:(qt + 1) * 128, :], in_=sb["xsq"][:, 1536:2560]),
                reads=["xsq", "atok"], writes=["y"], dma=True)
    return p


def build_p7(p, d, sb, ps):
    NST = TC // 512
    p.do("gpsimd", lambda g_: g_.dma_start(
        out=sb["wrt"][:].rearrange("p (kc e) -> p kc e", kc=NCH),
        in_=d["Wrt1"].rearrange("(kc p) e -> p kc e", p=128)),
        reads=["Wrt1"], writes=["wrt"], dma=True)
    p.do("gpsimd", lambda g_: g_.memset(sb["gsel"][:], 0.0), writes=["gsel"])
    for e in range(E1):
        p.do("gpsimd", lambda g_, e=e: g_.affine_select(
            out=sb["gsel"][0:E1, e * 128:(e + 1) * 128],
            in_=sb["gsel"][0:E1, e * 128:(e + 1) * 128],
            compare_op=ALU.not_equal, fill=1.0, base=-e,
            pattern=[[0, 128]], channel_multiplier=1),
            reads=["gsel"], writes=["gsel"])
    for st in range(NST):
        for kc in range(NCH):
            p.do("tensor", lambda t, st=st, kc=kc: t.matmul(
                ps["log"][0:E1, 0:512], sb["wrt"][:, kc * E1:(kc + 1) * E1],
                _xn2_slice(sb, kc, st * 512, (st + 1) * 512),
                start=(kc == 0), stop=(kc == NCH - 1)),
                reads=["wrt", _XN2[kc // 2]], writes=["ps_log"])
        for j4 in range(4):
            tile = st * 4 + j4
            p.do("vector", lambda v, j4=j4: v.tensor_copy(
                out=sb["logsb"][0:E1, 0:128], in_=ps["log"][0:E1, j4 * 128:(j4 + 1) * 128]),
                reads=["ps_log"], writes=["logsb"])
            p.do("tensor", lambda t: t.transpose(
                ps["tsm"][:, 0:E1], sb["logsb"][0:E1, 0:128], sb["identf"][0:E1, 0:E1]),
                reads=["logsb", "identf"], writes=["ps_tcol"])
            p.do("vector", lambda v: v.tensor_copy(out=sb["scr"][:, 0:E1], in_=ps["tsm"][:, 0:E1]),
                 reads=["ps_tcol"], writes=["scr"])
            p.do("vector", lambda v: v.reduce_max(out=sb["scr"][:, 8:9], in_=sb["scr"][:, 0:E1],
                                                  axis=AX.X),
                 reads=["scr"], writes=["scr"])
            p.do("vector", lambda v: v.tensor_scalar_mul(out=sb["scr"][:, 9:10],
                                                         in0=sb["scr"][:, 8:9], scalar1=-1.0),
                 reads=["scr"], writes=["scr"])
            p.do("scalar", lambda sc: sc.activation(
                out=sb["oh"][:, 0:E1], in_=sb["scr"][:, 0:E1], func=AF.Exp,
                bias=sb["scr"][:, 9:10], accum_out=sb["scr"][:, 10:11]),
                reads=["scr"], writes=["oh", "scr"])
            p.do("vector", lambda v: v.max(out=sb["top8"][:], in_=sb["oh"][:, 0:E1]),
                 reads=["oh"], writes=["top8"])
            p.do("vector", lambda v: v.tensor_add(out=sb["scr"][:, 11:12],
                                                  in0=sb["top8"][:, 0:1], in1=sb["top8"][:, 1:2]),
                 reads=["top8"], writes=["scr"])
            p.do("vector", lambda v: v.scalar_tensor_tensor(
                out=sb["scr"][:, 12:13], in0=sb["scr"][:, 10:11], scalar=1e-8,
                in1=sb["scr"][:, 11:12], op0=ALU.mult, op1=ALU.add),
                reads=["scr"], writes=["scr"])
            p.do("vector", lambda v: v.reciprocal(out=sb["scr"][:, 13:14], in_=sb["scr"][:, 12:13]),
                 reads=["scr"], writes=["scr"])
            p.do("vector", lambda v: v.tensor_scalar(
                out=sb["scr"][:, 16:16 + E1], in0=sb["oh"][:, 0:E1],
                scalar1=sb["top8"][:, 1:2], scalar2=None, op0=ALU.is_ge),
                reads=["oh", "top8"], writes=["scr"])
            p.do("vector", lambda v: v.tensor_mul(out=sb["scr"][:, 24:24 + E1],
                                                  in0=sb["oh"][:, 0:E1], in1=sb["scr"][:, 16:16 + E1]),
                 reads=["oh", "scr"], writes=["scr"])
            p.do("vector", lambda v, tile=tile: v.tensor_scalar_mul(
                out=sb["w8a"][:, tile * E1:(tile + 1) * E1], in0=sb["scr"][:, 24:24 + E1],
                scalar1=sb["scr"][:, 13:14]),
                reads=["scr"], writes=["w8a"])
    for tile in range(TC // 128):
        p.do("tensor", lambda t, tile=tile: t.transpose(
            ps["log"][0:E1, 0:128], sb["w8a"][:, tile * E1:(tile + 1) * E1],
            sb["identf"][:]),
            reads=["w8a", "identf"], writes=["ps_log"])
        p.do("vector", lambda v, tile=tile: v.tensor_copy(
            out=sb["xsq"][0:E1, 2048 + tile * 128:2048 + (tile + 1) * 128],
            in_=ps["log"][0:E1, 0:128]),
            reads=["ps_log"], writes=["xsq", "atok"])

    for e in range(E1):
        for half, dst in ((0, "wq"), (1, "wk")):
            p.do("gpsimd", lambda g_, e=e, half=half, dst=dst: g_.dma_start(
                out=sb[dst][:].rearrange("p (kc mc m) -> p kc mc m", kc=NCH, mc=NCH),
                in_=d["Wg1"][e].rearrange("(kc p) (mc m) -> p kc mc m", p=128, m=128)[:, :, half * 8:(half + 1) * 8, :]),
                reads=["Wg1"], writes=[dst], dma=True)
        for half, dst in ((0, "wv"), (1, "wo")):
            p.do("gpsimd", lambda g_, e=e, half=half, dst=dst: g_.dma_start(
                out=sb[dst][:].rearrange("p (kc mc m) -> p kc mc m", kc=NCH, mc=NCH),
                in_=d["Wu1"][e].rearrange("(kc p) (mc m) -> p kc mc m", p=128, m=128)[:, :, half * 8:(half + 1) * 8, :]),
                reads=["Wu1"], writes=[dst], dma=True)
        p.do("scalar", lambda sc, e=e: sc.dma_start(out=sb["bgs"][:], in_=d["bg1"][e]),
             reads=["bg1"], writes=["bgs"], dma=True)
        p.do("scalar", lambda sc, e=e: sc.dma_start(out=sb["bus"][:], in_=d["bu1"][e]),
             reads=["bu1"], writes=["bus"], dma=True)
        for st in range(NST):
            for mc in range(M1 // 128):
                wname = "wq" if mc < 8 else "wk"
                mloc = mc % 8
                for kc in range(NCH):
                    p.do("tensor", lambda t, mc=mc, kc=kc, st=st, wname=wname, mloc=mloc: t.matmul(
                        ps["bmu"][:, 0:512], sb[wname][:, (kc * NCH + mloc) * 128:(kc * NCH + mloc) * 128 + 128],
                        _xn2_slice(sb, kc, st * 512, (st + 1) * 512),
                        start=(kc == 0), stop=(kc == NCH - 1)),
                        reads=[wname, _XN2[kc // 2]], writes=["ps_bmu"])
                hname = "xnta" if mc < 8 else "xntb"
                hloc = mc % 8
                p.do("scalar", lambda sc, mc=mc, hname=hname, hloc=hloc: sc.activation(
                    out=sb[hname][:, hloc * 512:(hloc + 1) * 512], in_=ps["bmu"][:, 0:512],
                    func=AF.Silu, bias=sb["bgs"][:, mc:mc + 1]),
                    reads=["ps_bmu", "bgs"], writes=[hname])
            for mc in range(M1 // 128):
                wname = "wv" if mc < 8 else "wo"
                mloc = mc % 8
                for kc in range(NCH):
                    p.do("tensor", lambda t, mc=mc, kc=kc, st=st, wname=wname, mloc=mloc: t.matmul(
                        ps["brr"][:, 0:512], sb[wname][:, (kc * NCH + mloc) * 128:(kc * NCH + mloc) * 128 + 128],
                        _xn2_slice(sb, kc, st * 512, (st + 1) * 512),
                        start=(kc == 0), stop=(kc == NCH - 1)),
                        reads=[wname, _XN2[kc // 2]], writes=["ps_brr"])
                hname = "xnta" if mc < 8 else "xntb"
                hloc = mc % 8
                p.do("vector", lambda v, mc=mc: v.tensor_scalar_add(
                    out=sb["xa"][:, 0:512], in0=ps["brr"][:, 0:512],
                    scalar1=sb["bus"][:, mc:mc + 1]),
                    reads=["ps_brr", "bus"], writes=["xa"])
                p.do("vector", lambda v, hname=hname, hloc=hloc: v.tensor_mul(
                    out=sb[hname][:, hloc * 512:(hloc + 1) * 512],
                    in0=sb[hname][:, hloc * 512:(hloc + 1) * 512], in1=sb["xa"][:, 0:512]),
                    reads=[hname, "xa"], writes=[hname])
            for half, srcn in ((0, "xnta"), (1, "xntb")):
                p.do("sync", lambda sy, st=st, half=half, srcn=srcn: sy.dma_start(
                    out=d["hid1"][:, half * 8:(half + 1) * 8, st * 512:(st + 1) * 512],
                    in_=sb[srcn][:].rearrange("p (c t) -> p c t", c=8)),
                    reads=[srcn], writes=["hid1"], dma=True)
        for half, dst in ((0, "wq"), (1, "wk")):
            p.do("gpsimd", lambda g_, e=e, half=half, dst=dst: g_.dma_start(
                out=sb[dst][:].rearrange("p (kc mc m) -> p kc mc m", kc=NCH, mc=NCH),
                in_=d["Wo1"][e].rearrange("(kc p) (mc m) -> p kc mc m", p=128, m=128)[:, half * 8:(half + 1) * 8, :, :]),
                reads=["Wo1"], writes=[dst], dma=True)
        p.do("scalar", lambda sc, e=e: sc.dma_start(out=sb["bos1"][:], in_=d["bo1"][e]),
             reads=["bo1"], writes=["bos1"], dma=True)
        for st in range(NST):
            for half, srcn in ((0, "xnta"), (1, "xntb")):
                p.do("sync", lambda sy, st=st, half=half, srcn=srcn: sy.dma_start(
                    out=sb[srcn][:].rearrange("p (c t) -> p c t", c=8),
                    in_=d["hid1"][:, half * 8:(half + 1) * 8, st * 512:(st + 1) * 512]),
                    reads=["hid1"], writes=[srcn], dma=True)
            p.do("tensor", lambda t, e=e, st=st: t.matmul(
                ps["brr"][:, 0:512], sb["gsel"][0:E1, e * 128:(e + 1) * 128],
                sb["xsq"][0:E1, 2048 + st * 512:2048 + (st + 1) * 512],
                start=True, stop=True),
                reads=["gsel", "xsq", "atok"], writes=["ps_brr"])
            p.do("vector", lambda v: v.tensor_copy(out=sb["xa"][:, 512:1024],
                                                   in_=ps["brr"][:, 0:512]),
                 reads=["ps_brr"], writes=["xa"])
            for mc in range(NCH):
                for kcm in range(M1 // 128):
                    wname = "wq" if kcm < 8 else "wk"
                    kloc = kcm % 8
                    hname = "xnta" if kcm < 8 else "xntb"
                    hloc = kcm % 8
                    p.do("tensor", lambda t, mc=mc, kcm=kcm, wname=wname, kloc=kloc, hname=hname, hloc=hloc: t.matmul(
                        ps["bmu"][:, 0:512], sb[wname][:, (kloc * NCH + mc) * 128:(kloc * NCH + mc) * 128 + 128],
                        sb[hname][:, hloc * 512:(hloc + 1) * 512],
                        start=(kcm == 0), stop=(kcm == M1 // 128 - 1)),
                        reads=[wname, hname], writes=["ps_bmu"])
                p.do("vector", lambda v, mc=mc: v.tensor_scalar_add(
                    out=sb["xa"][:, 0:512], in0=ps["bmu"][:, 0:512],
                    scalar1=sb["bos1"][:, mc:mc + 1]),
                    reads=["ps_bmu", "bos1"], writes=["xa"])
                p.do("vector", lambda v, mc=mc: v.tensor_mul(
                    out=sb["xb"][:, mc * 512:(mc + 1) * 512],
                    in0=sb["xa"][:, 0:512], in1=sb["xa"][:, 512:1024]),
                    reads=["xa"], writes=["xb"])
            for j in range(4):
                for half in range(2):
                    for c4 in range(4):
                        c = half * 4 + c4
                        p.do("tensor", lambda t, j=j, c=c, c4=c4: t.transpose(
                            ps["bmu"][:, c4 * 128:(c4 + 1) * 128],
                            sb["xb"][:, c * 512 + j * 128:c * 512 + (j + 1) * 128],
                            sb["identf"][:]),
                            reads=["xb", "identf"], writes=["ps_bmu"])
                    p.do("vector", lambda v, half=half: v.tensor_copy(
                        out=sb["xsq"][:, 512 + half * 512:512 + (half + 1) * 512],
                        in_=ps["bmu"][:, 0:512]),
                        reads=["ps_bmu"], writes=["xsq", "atok"])
                p.do("gpsimd", lambda g_, st=st, j=j: g_.dma_start(
                    out=d["y"][st * 512 + j * 128:st * 512 + (j + 1) * 128, :],
                    in_=sb["xsq"][:, 512:1536], accum_op=ALU.add),
                    reads=["xsq", "atok", "y"], writes=["y"], dma=True)
    return p


P567_SBUF = {
    "bfs": ([128, NCH], F32),
    "kshf": ([128, TB // 512], F32),
    "idxq": ([128, 1], U32),
    "mrun": ([128, 16], F32), "lrun": ([128, 16], F32),
    "wrt": ([128, NCH * E1], BF16),
    "w8a": ([128, 128], F32),
    "bgs": ([128, M1 // 128], F32), "bus": ([128, M1 // 128], F32),
    "bos1": ([128, NCH], F32),
}


def kernel(**inp):
    common = _stage_common(inp)
    cores = [(c // 2, c % 2) for c in range(N_CORES)]
    per_core = [_stage_core(inp, b, s) for (b, s) in cores]

    names_in = [n for n, (sh, dt, role) in DRAM_SPECS.items() if role == "in"]
    internal = [n for n, (sh, dt, role) in DRAM_SPECS.items() if role == "inter"]
    roles = _roles_for(names_in, ["y"], internal)
    sbs = merge_specs(P0_SBUF, P1_SBUF, P2_SBUF, P3_SBUF, P4_SBUF, P567_SBUF)
    psm = merge_specs(P0_PSUM, P1_PSUM, P2_PSUM, P3_PSUM, P4_PSUM)
    fns = [build_p0, build_p1, build_p2, build_p3, build_p4,
           build_p5, build_p6, build_p7]
    nc, _ = _build_launch(fns, sbs, psm, roles)
    in_maps = []
    for c in range(N_CORES):
        m = {}
        for n in names_in:
            m[n] = per_core[c][n] if n in per_core[c] else common[n]
        in_maps.append(m)
    res = run_bass_kernel_spmd(nc, in_maps, core_ids=list(range(N_CORES)))
    out = np.empty((B, S, D), np.float32)
    for c, (b, s) in enumerate(cores):
        out[b, s * TC:(s + 1) * TC, :] = res.results[c]["y"]
    return out


# version 06
# speedup vs baseline: 1.2646x; 1.2646x over previous
import math
from contextlib import ExitStack

import numpy as np
import concourse.bass as bass
import concourse.mybir as mybir
from concourse.bass_utils import run_bass_kernel_spmd
from concourse.bass_core_rust import IndirectOffsetOnAxis

F32 = mybir.dt.float32
BF16 = mybir.dt.bfloat16
U32 = mybir.dt.uint32
U8 = mybir.dt.uint8
AF = mybir.ActivationFunctionType
ALU = mybir.AluOpType
AX = mybir.AxisListType

B, S, D = 4, 4096, 1024
H, DH, NF = 16, 64, 256
E0, K0, CAP, F0, SF = 64, 6, 384, 512, 2048
E1, M1 = 8, 2048
T_ALL = B * S
TB = S
TC = 2048
NCH = D // 128
NSLOT = E0 * CAP
LN_EPS = 1e-5
YAT_EPS = 0.1
N_CORES = 8

ENGINES = ("tensor", "vector", "scalar", "gpsimd", "sync")


class _Op:
    __slots__ = ("eng", "fn", "reads", "writes", "is_dma", "needs_inc",
                 "inc_idx", "dep_ops", "stream_waits", "waits", "tag", "stream")

    def __init__(self, eng, fn, reads, writes, is_dma, tag):
        self.eng = eng
        self.fn = fn
        self.reads = reads
        self.writes = writes
        self.is_dma = is_dma
        self.needs_inc = False
        self.inc_idx = None
        self.dep_ops = set()
        self.stream_waits = {}
        self.waits = {}
        self.tag = tag
        self.stream = None


class AutoProg:
    def __init__(self, dram_names=()):
        self.ops = []
        self._writer = {}
        self._readers = {}
        self._streams = {}
        self._dram = set(dram_names)
        self._extra = {}

    def claim(self, new, olds):
        dep = set()
        for o in olds:
            j = self._writer.get(o)
            if j is not None:
                dep.add(j)
            dep.update(self._readers.get(o, ()))
        if dep:
            self._extra[new] = self._extra.get(new, set()) | dep

    def do(self, eng, fn, reads=(), writes=(), dma=False, stream=None,
           counted=None, stream_reads=(), tag=""):
        op = _Op(eng, fn, tuple(reads), tuple(writes), dma, tag)
        for s_ in stream_reads:
            st_ = self._streams.get(s_)
            if st_ is not None and st_[0] > 0:
                op.stream_waits[s_] = max(op.stream_waits.get(s_, 0), 16 * st_[0])
        raw = set()
        war_waw = set()
        for b_ in op.reads:
            j = self._writer.get(b_)
            if j is not None:
                raw.add(j)
            if b_ in self._extra:
                war_waw |= self._extra.pop(b_)
        for b_ in op.writes:
            j = self._writer.get(b_)
            if j is not None:
                war_waw.add(j)
            for j in self._readers.get(b_, ()):
                war_waw.add(j)
            if b_ in self._extra:
                war_waw |= self._extra.pop(b_)
        for j in sorted(raw | war_waw):
            oj = self.ops[j]
            if oj.is_dma:
                st = self._streams[oj.stream]
                op.stream_waits[oj.stream] = 16 * st[0]
                continue
            same = (oj.eng == eng) and not op.is_dma
            if same and eng == "tensor" and j not in raw:
                continue
            op.dep_ops.add(j)
        if dma:
            if stream is None:
                stream = op.writes[0]
            if counted is None:
                counted = stream in self._dram
            st = self._streams.setdefault(
                stream, [0, "counted" if counted else "ordered", eng])
            if st[1] == "counted":
                pass
            elif st[0] > 0:
                op.stream_waits[stream] = max(
                    op.stream_waits.get(stream, 0), 16 * st[0])
            st[0] += 1
            op.stream = stream
        i = len(self.ops)
        for j in op.dep_ops:
            self.ops[j].needs_inc = True
        self.ops.append(op)
        for b_ in op.reads:
            self._readers.setdefault(b_, []).append(i)
        for b_ in op.writes:
            self._writer[b_] = i
            self._readers[b_] = []
        return i

    def emit(self, nc, eng_map, stack):
        sems = {}
        for e in ENGINES:
            sems[("c", e)] = stack.enter_context(nc.semaphore(f"sc_{e}"))
        for s in self._streams:
            sems[("s", s)] = stack.enter_context(nc.semaphore(f"ss_{s}"))
        cnt = {k: 0 for k in sems}
        for s, st in self._streams.items():
            cnt[("s", s)] = 16 * st[0]
        ccount = {e: 0 for e in ENGINES}
        for op in self.ops:
            if op.needs_inc and not op.is_dma:
                ccount[op.eng] += 1
                op.inc_idx = (("c", op.eng), ccount[op.eng])
        for e in ENGINES:
            cnt[("c", e)] = ccount[e]
        for op in self.ops:
            tg = {}
            for j in op.dep_ops:
                key, val = self.ops[j].inc_idx
                tg[key] = max(tg.get(key, 0), val)
            for s, val in op.stream_waits.items():
                key = ("s", s)
                tg[key] = max(tg.get(key, 0), val)
            op.waits = tg

        blocks = {}
        for op in self.ops:
            blocks.setdefault(op.eng, []).append(op)

        with nc.Block() as block:
            for e, ops in blocks.items():
                def section(eng, ops=ops, e=e):
                    waited = {}
                    for op in ops:
                        for key, val in op.waits.items():
                            if waited.get(key, 0) < val:
                                eng.wait_ge(sems[key], val)
                                waited[key] = val
                        inst = op.fn(eng)
                        if op.is_dma:
                            inst.then_inc(sems[("s", op.stream)], 16)
                        elif op.needs_inc:
                            key, val = op.inc_idx
                            inst.then_inc(sems[key], 1)
                getattr(block, e)(section)

            def drain(eng):
                for key, val in cnt.items():
                    if val > 0:
                        eng.wait_ge(sems[key], val)
            block.sync(drain)
        return sems, cnt


def _tile_lhsT(w, kc, mc):
    return np.ascontiguousarray(
        w.reshape(kc, 128, mc, 128).transpose(1, 0, 2, 3).reshape(128, kc * mc * 128))


def _stage_common(inp):
    st = {}
    for nm in ("Wq", "Wk", "Wv", "Wo"):
        st[nm] = _tile_lhsT(np.asarray(inp[f"L0_attn_{nm}"], np.float32), NCH, NCH)
    for nm in ("bq", "bk", "bv", "bo"):
        v = np.asarray(inp[f"L0_attn_{nm}"], np.float32)
        st[nm] = np.ascontiguousarray(v.reshape(NCH, 128).T)
    P = np.asarray(inp["L0_attn_projection"], np.float32).transpose(0, 2, 1)
    st["projT"] = np.ascontiguousarray(P.reshape(8, 2 * DH, NF))
    st["rwT"] = np.ascontiguousarray(np.asarray(inp["L0_ffn_router_weight"], np.float32).T)
    st["ebias"] = np.ascontiguousarray(np.asarray(inp["L0_ffn_expert_bias"], np.float32).reshape(1, E0))
    st["w1"] = np.ascontiguousarray(
        np.asarray(inp["L0_ffn_w1"], np.float32).reshape(E0, 8, 128, 4, 128)
        .transpose(0, 2, 1, 3, 4).reshape(E0, 128, 4096))
    st["w2"] = np.ascontiguousarray(
        np.asarray(inp["L0_ffn_w2"], np.float32).reshape(E0, 4, 128, 1024)
        .transpose(0, 2, 1, 3).reshape(E0, 128, 4096))
    st["sw1T"] = _tile_lhsT(np.asarray(inp["L0_ffn_sw1"], np.float32).T.copy(), 8, 16)
    st["sw2T"] = _tile_lhsT(np.asarray(inp["L0_ffn_sw2"], np.float32).T.copy(), 16, 8)
    st["WfT"] = _tile_lhsT(np.asarray(inp["L1_attn_W"], np.float32).T.copy(), 8, 8)
    st["bfv"] = np.ascontiguousarray(np.asarray(inp["L1_attn_b"], np.float32).reshape(NCH, 128).T)
    st["Wrt1"] = np.ascontiguousarray(inp["L1_ffn_W_router"], np.float32)
    st["Wg1"] = np.ascontiguousarray(
        np.asarray(inp["L1_ffn_W_gate"], np.float32).reshape(E1, 8, 128, 16, 128)
        .transpose(0, 2, 1, 3, 4).reshape(E1, 128, 16384))
    st["Wu1"] = np.ascontiguousarray(
        np.asarray(inp["L1_ffn_W_up"], np.float32).reshape(E1, 8, 128, 16, 128)
        .transpose(0, 3, 2, 1, 4).reshape(E1, 16, 128, 1024))
    st["Wo1"] = np.ascontiguousarray(
        np.asarray(inp["L1_ffn_W_out"], np.float32).reshape(E1, 16, 128, 1024)
        .transpose(0, 2, 1, 3).reshape(E1, 128, 16384))
    st["bg1"] = np.ascontiguousarray(np.asarray(inp["L1_ffn_b_gate"], np.float32).reshape(E1, M1 // 128, 128).transpose(0, 2, 1))
    st["bu1"] = np.ascontiguousarray(np.asarray(inp["L1_ffn_b_up"], np.float32).reshape(E1, M1 // 128, 128).transpose(0, 2, 1))
    st["bo1r"] = np.ascontiguousarray(np.asarray(inp["L1_ffn_b_out"], np.float32))
    return st


def _stage_core(inp, b, s):
    st = {}
    x = np.asarray(inp["L0_attn_x"], np.float32)
    xf = x.reshape(T_ALL, D)
    order = [bb for bb in range(B) if bb != b] + [b]
    xT = np.concatenate([xf[bb * TB:(bb + 1) * TB].T for bb in order], axis=1)
    st["xT_perm"] = np.ascontiguousarray(xT)
    st["x_tok"] = np.ascontiguousarray(x[b])
    selg = np.zeros((4, 1), np.float32)
    for gi, bb in enumerate(order):
        selg[gi, 0] = 1.0 if bb < b else 0.0
    st["selg"] = selg
    rot = ((np.arange(TB) + s * TC) % TB).astype(np.uint32)
    st["rot_idx"] = rot.reshape(TB // 128, 128, 1)
    ksh = np.zeros((128, TB // 512), np.float32)
    for kt in range(TB // 512):
        kl0 = kt * 512
        ksh[:, kt] = 4096.0 if (s == 1 and kl0 >= TC) else 0.0
    st["kshift"] = np.ascontiguousarray(ksh)
    return st


DRAM_SPECS = {
    "xT_perm": ([D, T_ALL], F32, "in"),
    "x_tok": ([TB, D], F32, "in"),
    "selg": ([4, 1], F32, "in"),
    "rot_idx": ([TB // 128, 128, 1], U32, "in"),
    "kshift": ([128, TB // 512], F32, "in"),
    "Wq": ([128, 8192], F32, "in"), "Wk": ([128, 8192], F32, "in"),
    "Wv": ([128, 8192], F32, "in"), "Wo": ([128, 8192], F32, "in"),
    "bq": ([128, NCH], F32, "in"), "bk": ([128, NCH], F32, "in"),
    "bv": ([128, NCH], F32, "in"), "bo": ([128, NCH], F32, "in"),
    "projT": ([8, 128, NF], F32, "in"),
    "rwT": ([D, E0], F32, "in"),
    "ebias": ([1, E0], F32, "in"),
    "w1": ([E0, 128, 4096], F32, "in"),
    "w2": ([E0, 128, 4096], F32, "in"),
    "sw1T": ([128, 16384], F32, "in"),
    "sw2T": ([128, 16384], F32, "in"),
    "WfT": ([128, 8192], F32, "in"),
    "bfv": ([128, NCH], F32, "in"),
    "Wrt1": ([D, E1], F32, "in"),
    "Wg1": ([E1, 128, 16384], F32, "in"),
    "Wu1": ([E1, 16, 128, 1024], F32, "in"),
    "Wo1": ([E1, 128, 16384], F32, "in"),
    "bg1": ([E1, 128, M1 // 128], F32, "in"),
    "bu1": ([E1, 128, M1 // 128], F32, "in"),
    "bo1r": ([E1, D], F32, "in"),
    "mu_own": ([1, TB], F32, "inter"),
    "r_own": ([1, TB], F32, "inter"),
    "Tmap_w": ([NSLOT + 1, 2], F32, "inter"),
    "Smap": ([TB, 6], U32, "inter"),
    "y_slot": ([NSLOT + 1, D], F32, "inter"),
    "xnT": ([D, TB], BF16, "inter"),
    "xn_tok": ([TB + 1, D], BF16, "inter"),
    "x1": ([TB + 1, D], F32, "inter"),
    "f_tok": ([TB, D], BF16, "inter"),
    "y": ([TC, D], F32, "out"),
}


def _chunks(ap):
    return ap.rearrange("(c p) t -> p c t", p=128)


def build_p0(p, d, sb, ps):
    TILE = 512
    NT = T_ALL // TILE
    NOT_OWN = NT - TB // TILE

    p.do("gpsimd", lambda g: g.memset(sb["ones_f1"][:], 1.0), writes=["ones_f1"])
    p.do("gpsimd", lambda g: g.memset(sb["ones_b1"][:], 1.0), writes=["ones_b1"])
    p.do("gpsimd", lambda g: g.memset(sb["ones_row"][:], 1.0), writes=["ones_row"])
    p.do("gpsimd", lambda g: g.iota(sb["iota64"][:], pattern=[[1, E0]], base=0,
                                    channel_multiplier=0, allow_small_or_imprecise_dtypes=True),
         writes=["iota64"])
    p.do("gpsimd", lambda g: g.iota(sb["iotac"][:], pattern=[[1, 1]], base=0,
                                    channel_multiplier=1, allow_small_or_imprecise_dtypes=True),
         writes=["iotac"])
    p.do("gpsimd", lambda g: g.memset(sb["su"][:], 0.0), writes=["su"])
    p.do("gpsimd", lambda g: g.affine_select(out=sb["su"][:], in_=sb["su"][:],
                                             compare_op=ALU.is_ge, fill=1.0, base=0,
                                             pattern=[[-1, 128]], channel_multiplier=1),
         reads=["su"], writes=["su"])
    p.do("gpsimd", lambda g: g.memset(sb["ones128"][:], 1.0), writes=["ones128"])
    p.do("gpsimd", lambda g: g.memset(sb["identf"][:], 0.0), writes=["identf"])
    p.do("gpsimd", lambda g: g.affine_select(out=sb["identf"][:], in_=sb["identf"][:],
                                             compare_op=ALU.not_equal, fill=1.0, base=0,
                                             pattern=[[-1, 128]], channel_multiplier=1),
         reads=["identf"], writes=["identf"])
    p.do("gpsimd", lambda g: g.memset(sb["init"][:], 0.0), writes=["init"])
    p.do("gpsimd", lambda g: g.memset(sb["init"][:, 0:384:2], float(TB)),
         reads=["init"], writes=["init"])
    p.do("sync", lambda sy: sy.dma_start(
        out=d["Tmap_w"][0:NSLOT, :].rearrange("(p r) c -> p (r c)", p=128),
        in_=sb["init"][:, 0:384]), reads=["init"], writes=["Tmap_w_pre"],
        dma=True, stream="TmapP", counted=True)
    p.do("sync", lambda sy: sy.dma_start(
        out=d["Tmap_w"][NSLOT:NSLOT + 1, :], in_=sb["init"][0:1, 0:2]),
        reads=["init"], writes=["Tmap_w_pre2"], dma=True, stream="TmapP",
        counted=True)

    p.do("sync", lambda sy: sy.dma_start(
        out=sb["rwf"][:].rearrange("p (c e) -> p c e", c=NCH), in_=_chunks(d["rwT"])),
        writes=["rwf"], dma=True)
    p.do("sync", lambda sy: sy.dma_start(out=sb["row64"][0:1, :], in_=d["ebias"]),
         writes=["row64"], dma=True)
    p.do("sync", lambda sy: sy.dma_start(out=sb["sel4"][0:4, 0:1], in_=d["selg"]),
         writes=["sel4"], dma=True)
    p.do("tensor", lambda t: t.matmul(ps["m64"][:, 0:E0], sb["ones_row"][:],
                                      sb["row64"][0:1, :], start=True, stop=True),
         reads=["ones_row", "row64"], writes=["ps_brd"])
    p.do("vector", lambda v: v.tensor_copy(out=sb["ebias_b"][:], in_=ps["m64"][:, 0:E0]),
         reads=["ps_brd"], writes=["ebias_b"])
    for c in range(NCH):
        p.do("tensor", lambda t, c=c: t.matmul(
            ps["cnt"][0:E0, 4:5], sb["rwf"][:, c * E0:(c + 1) * E0], sb["ones_f1"][:, 0:1],
            start=(c == 0), stop=(c == NCH - 1)),
            reads=["rwf", "ones_f1"], writes=["ps_cnt"])
    p.do("vector", lambda v: v.tensor_copy(out=sb["cnt64"][0:E0, 0:1], in_=ps["cnt"][0:E0, 4:5]),
         reads=["ps_cnt"], writes=["cnt64"])
    p.do("tensor", lambda t: t.transpose(ps["tsm"][0:1, 0:E0], sb["cnt64"][0:E0, 0:1],
                                         sb["identf"][0:E0, 0:E0]),
         reads=["cnt64", "identf"], writes=["ps_trow"])
    p.do("vector", lambda v: v.tensor_copy(out=sb["row64b"][0:1, 0:E0], in_=ps["tsm"][0:1, 0:E0]),
         reads=["ps_trow"], writes=["row64b"])
    p.do("tensor", lambda t: t.matmul(ps["m64"][:, 0:E0], sb["ones_row"][:],
                                      sb["row64b"][0:1, 0:E0], start=True, stop=True),
         reads=["ones_row", "row64b"], writes=["ps_brd"])
    p.do("vector", lambda v: v.tensor_copy(out=sb["rwsum_b"][:], in_=ps["m64"][:, 0:E0]),
         reads=["ps_brd"], writes=["rwsum_b"])
    p.do("gpsimd", lambda g: g.memset(sb["offg_b"][:], 0.0), writes=["offg_b"])
    p.do("gpsimd", lambda g: g.memset(sb["nslotc"][:], float(NSLOT)), writes=["nslotc"])

    n_own = 0
    for i in range(NT):
        gi = i // (TB // TILE)
        own = gi == 3
        xbuf = "xa" if i % 2 == 0 else "xb"
        p.do("sync", lambda sy, i=i, xbuf=xbuf: sy.dma_start(
            out=sb[xbuf][:].rearrange("p (c t) -> p c t", c=NCH),
            in_=_chunks(d["xT_perm"])[:, :, i * TILE:(i + 1) * TILE]),
            writes=[xbuf], dma=True)
        p.do("scalar", lambda sc, xbuf=xbuf: sc.activation(
            out=sb["xsq"][:], in_=sb[xbuf][:], func=AF.Square),
            reads=[xbuf], writes=["xsq"])
        for c in range(NCH):
            p.do("tensor", lambda t, c=c, xbuf=xbuf: t.matmul(
                ps["stat"][0:1, 0:TILE], sb["ones_f1"][:, 0:1],
                sb[xbuf][:, c * TILE:(c + 1) * TILE], start=(c == 0), stop=(c == NCH - 1)),
                reads=[xbuf, "ones_f1"], writes=["ps_stat"])
        p.do("vector", lambda v: v.tensor_scalar_mul(
            out=sb["murow"][0:1, :], in0=ps["stat"][0:1, 0:TILE], scalar1=1.0 / D),
            reads=["ps_stat"], writes=["murow"])
        for c in range(NCH):
            p.do("tensor", lambda t, c=c: t.matmul(
                ps["stat"][0:1, 0:TILE], sb["ones_f1"][:, 0:1],
                sb["xsq"][:, c * TILE:(c + 1) * TILE], start=(c == 0), stop=(c == NCH - 1)),
                reads=["xsq", "ones_f1", "murow"], writes=["ps_stat"])
        for c in range(NCH):
            p.do("tensor", lambda t, c=c, xbuf=xbuf: t.matmul(
                ps["log"][:, 0:TILE], sb["rwf"][:, c * E0:(c + 1) * E0],
                sb[xbuf][:, c * TILE:(c + 1) * TILE], start=(c == 0), stop=(c == NCH - 1)),
                reads=[xbuf, "rwf"], writes=["ps_log"])
        p.do("vector", lambda v: v.tensor_scalar_mul(
            out=sb["m2row"][0:1, :], in0=ps["stat"][0:1, 0:TILE], scalar1=1.0 / D),
            reads=["ps_stat"], writes=["m2row"])
        p.do("vector", lambda v: v.tensor_mul(
            out=sb["rrow"][0:1, :], in0=sb["murow"][0:1, :], in1=sb["murow"][0:1, :]),
            reads=["murow"], writes=["rrow"])
        p.do("vector", lambda v: v.tensor_sub(
            out=sb["m2row"][0:1, :], in0=sb["m2row"][0:1, :], in1=sb["rrow"][0:1, :]),
            reads=["m2row", "rrow"], writes=["m2row"])
        p.do("vector", lambda v: v.tensor_scalar_add(
            out=sb["m2row"][0:1, :], in0=sb["m2row"][0:1, :], scalar1=LN_EPS),
            reads=["m2row"], writes=["m2row"])
        p.do("scalar", lambda sc: sc.activation(
            out=sb["rrow"][0:1, :], in_=sb["m2row"][0:1, :], func=AF.Sqrt),
            reads=["m2row"], writes=["rrow"])
        p.do("vector", lambda v: v.reciprocal(out=sb["rrow"][0:1, :], in_=sb["rrow"][0:1, :]),
             reads=["rrow"], writes=["rrow"])
        if own:
            j0 = (i - NOT_OWN) * TILE
            p.do("sync", lambda sy, j0=j0: sy.dma_start(
                out=d["mu_own"][0:1, j0:j0 + TILE], in_=sb["murow"][0:1, :]),
                reads=["murow"], writes=["mu_own"], dma=True)
            p.do("sync", lambda sy, j0=j0: sy.dma_start(
                out=d["r_own"][0:1, j0:j0 + TILE], in_=sb["rrow"][0:1, :]),
                reads=["rrow"], writes=["r_own"], dma=True)
        p.do("vector", lambda v: v.tensor_copy(out=sb["logsb"][0:E0, :], in_=ps["log"][:, 0:TILE]),
             reads=["ps_log"], writes=["logsb"])

        if own and i == NOT_OWN:
            p.do("vector", lambda v: v.tensor_copy(out=sb["cnt64"][0:E0, 0:3], in_=ps["cnt"][0:E0, 0:3]),
                 reads=["ps_cnt"], writes=["cnt64"])
            p.do("tensor", lambda t: t.transpose(ps["tsm"][0:3, 0:E0], sb["cnt64"][0:E0, 0:3],
                                                 sb["identf"][0:E0, 0:E0]),
                 reads=["cnt64", "identf"], writes=["ps_trow"])
            p.do("vector", lambda v: v.tensor_copy(out=sb["cntT"][0:3, 0:E0], in_=ps["tsm"][0:3, 0:E0]),
                 reads=["ps_trow"], writes=["cntT"])
            p.do("tensor", lambda t: t.matmul(ps["tsm"][0:1, 0:E0], sb["sel4"][0:3, 0:1],
                                              sb["cntT"][0:3, 0:E0], start=True, stop=True),
                 reads=["sel4", "cntT"], writes=["ps_trow"])
            p.do("vector", lambda v: v.tensor_copy(out=sb["row64b"][0:1, 0:E0],
                                                   in_=ps["tsm"][0:1, 0:E0]),
                 reads=["ps_trow"], writes=["row64b"])
            p.do("tensor", lambda t: t.matmul(ps["m64"][:, 0:E0], sb["ones_row"][:],
                                              sb["row64b"][0:1, 0:E0], start=True, stop=True),
                 reads=["ones_row", "row64b"], writes=["ps_brd"])
            p.do("vector", lambda v: v.tensor_copy(out=sb["offg_b"][:], in_=ps["m64"][:, 0:E0]),
                 reads=["ps_brd"], writes=["offg_b"])

        for j in range(TILE // 128):
            p.do("tensor", lambda t, j=j: t.transpose(
                ps["tsm"][:, E0:E0 + 1], sb["murow"][0:1, j * 128:(j + 1) * 128],
                sb["identf"][0:1, 0:1]),
                reads=["murow", "identf"], writes=["ps_tcol"])
            p.do("vector", lambda v: v.tensor_copy(out=sb["mucol"][:, 0:1], in_=ps["tsm"][:, E0:E0 + 1]),
                 reads=["ps_tcol"], writes=["mucol"])
            p.do("tensor", lambda t, j=j: t.transpose(
                ps["tsm"][:, E0 + 1:E0 + 2], sb["rrow"][0:1, j * 128:(j + 1) * 128],
                sb["identf"][0:1, 0:1]),
                reads=["rrow", "identf"], writes=["ps_tcol"])
            p.do("vector", lambda v: v.tensor_copy(out=sb["rcol"][:, 0:1], in_=ps["tsm"][:, E0 + 1:E0 + 2]),
                 reads=["ps_tcol"], writes=["rcol"])
            p.do("vector", lambda v: v.tensor_scalar_mul(
                out=sb["negr"][:, 0:1], in0=sb["rcol"][:, 0:1], scalar1=-1.0),
                reads=["rcol"], writes=["negr"])
            p.do("tensor", lambda t, j=j: t.transpose(
                ps["m64"][:, E0:2 * E0], sb["logsb"][0:E0, j * 128:(j + 1) * 128],
                sb["identf"][0:E0, 0:E0]),
                reads=["logsb", "identf"], writes=["ps_ttok"])
            p.do("vector", lambda v: v.scalar_tensor_tensor(
                out=sb["prs"][:], in0=sb["rwsum_b"][:], scalar=sb["mucol"][:, 0:1],
                in1=ps["m64"][:, E0:2 * E0], op0=ALU.mult, op1=ALU.subtract),
                reads=["rwsum_b", "mucol", "ps_ttok"], writes=["prs"])
            p.do("scalar", lambda sc: sc.activation(
                out=sb["sc64"][:], in_=sb["prs"][:], func=AF.Sigmoid, scale=sb["negr"][:, 0:1]),
                reads=["prs", "negr"], writes=["sc64"])
            p.do("vector", lambda v: v.tensor_add(
                out=sb["bsc"][:], in0=sb["sc64"][:], in1=sb["ebias_b"][:]),
                reads=["sc64", "ebias_b"], writes=["bsc"])
            p.do("vector", lambda v: v.max(out=sb["top8"][:], in_=sb["bsc"][:]),
                 reads=["bsc"], writes=["top8"])
            nn = n_own
            p.do("vector", lambda v, nn=nn, own=own: v.tensor_scalar(
                out=(sb["vtok"][:, nn * E0:(nn + 1) * E0] if own else sb["Mtmp"][:]),
                in0=sb["bsc"][:], scalar1=sb["top8"][:, 5:6], scalar2=None, op0=ALU.is_ge),
                reads=["bsc", "top8"], writes=["vtok" if own else "Mtmp"])
            first = (i % (TB // TILE) == 0) and j == 0
            last = (i % (TB // TILE) == TB // TILE - 1) and j == TILE // 128 - 1
            p.do("tensor", lambda t, nn=nn, own=own, gi=gi, first=first, last=last: t.matmul(
                ps["cnt"][0:E0, gi:gi + 1],
                (sb["vtok"][:, nn * E0:(nn + 1) * E0] if own else sb["Mtmp"][:]),
                sb["ones_b1"][:, 0:1], start=first, stop=last),
                reads=["vtok" if own else "Mtmp", "ones_b1"], writes=["ps_cnt"])

            if own:
                for m2 in range(nn + 1):
                    p.do("tensor", lambda t, m2=m2, nn=nn: t.matmul(
                        ps["m64"][:, 2 * E0:3 * E0], (sb["su"][:] if m2 == nn else sb["ones128"][:]),
                        sb["vtok"][:, m2 * E0:(m2 + 1) * E0], start=(m2 == 0), stop=(m2 == nn)),
                        reads=["su", "ones128", "vtok"], writes=["ps_pref"])
                p.do("vector", lambda v: v.tensor_copy(out=sb["prefx"][:], in_=ps["m64"][:, 2 * E0:3 * E0]),
                     reads=["ps_pref"], writes=["prefx"])
                p.do("vector", lambda v, nn=nn: v.tensor_mul(
                    out=sb["scr"][:], in0=sb["sc64"][:], in1=sb["vtok"][:, nn * E0:(nn + 1) * E0]),
                    reads=["sc64", "vtok"], writes=["scr"])
                p.do("vector", lambda v: v.reduce_sum(out=sb["ssum"][:, 0:1], in_=sb["scr"][:],
                                                      axis=AX.X),
                     reads=["scr"], writes=["ssum"])
                p.do("vector", lambda v: v.tensor_scalar_add(
                    out=sb["ssum"][:, 0:1], in0=sb["ssum"][:, 0:1], scalar1=1e-20),
                    reads=["ssum"], writes=["ssum"])
                p.do("vector", lambda v: v.reciprocal(out=sb["sinv"][:, 0:1], in_=sb["ssum"][:, 0:1]),
                     reads=["ssum"], writes=["sinv"])
                p.do("vector", lambda v, nn=nn: v.tensor_scalar_add(
                    out=sb["tokc"][:, 0:1], in0=sb["iotac"][:], scalar1=float(nn * 128)),
                    reads=["iotac"], writes=["tokc"])
                pb = nn % 2
                for kk in range(K0):
                    sl = pb * K0 + kk
                    p.do("vector", lambda v, kk=kk: v.tensor_scalar(
                        out=sb["oh"][:], in0=sb["bsc"][:], scalar1=sb["top8"][:, kk:kk + 1],
                        scalar2=None, op0=ALU.is_equal),
                        reads=["bsc", "top8"], writes=["oh"])
                    p.do("vector", lambda v: v.scalar_tensor_tensor(
                        out=sb["scr"][:], in0=sb["oh"][:], scalar=-16384.0,
                        in1=sb["iota64"][:], op0=ALU.mult, op1=ALU.add),
                        reads=["oh", "iota64"], writes=["scr"])
                    p.do("vector", lambda v: v.tensor_reduce(
                        out=sb["idxc"][:, 0:1], in_=sb["scr"][:], axis=AX.X, op=ALU.min),
                        reads=["scr"], writes=["idxc"])
                    p.do("vector", lambda v: v.tensor_scalar_add(
                        out=sb["idxc"][:, 0:1], in0=sb["idxc"][:, 0:1], scalar1=16384.0),
                        reads=["idxc"], writes=["idxc"])
                    p.do("vector", lambda v: v.tensor_mul(out=sb["scr"][:], in0=sb["sc64"][:],
                                                          in1=sb["oh"][:]),
                         reads=["sc64", "oh"], writes=["scr"])
                    p.do("vector", lambda v: v.reduce_max(out=sb["sck"][:, 0:1], in_=sb["scr"][:],
                                                          axis=AX.X),
                         reads=["scr"], writes=["sck"])
                    p.do("vector", lambda v: v.tensor_mul(out=sb["scr"][:], in0=sb["prefx"][:],
                                                          in1=sb["oh"][:]),
                         reads=["prefx", "oh"], writes=["scr"])
                    p.do("vector", lambda v: v.reduce_max(out=sb["rkc"][:, 0:1], in_=sb["scr"][:],
                                                          axis=AX.X),
                         reads=["scr"], writes=["rkc"])
                    p.do("vector", lambda v: v.tensor_mul(out=sb["scr"][:], in0=sb["offg_b"][:],
                                                          in1=sb["oh"][:]),
                         reads=["offg_b", "oh"], writes=["scr"])
                    p.do("vector", lambda v: v.reduce_max(out=sb["offc"][:, 0:1], in_=sb["scr"][:],
                                                          axis=AX.X),
                         reads=["scr"], writes=["offc"])
                    p.do("vector", lambda v: v.tensor_add(out=sb["posc"][:, 0:1],
                                                          in0=sb["offc"][:, 0:1], in1=sb["rkc"][:, 0:1]),
                         reads=["offc", "rkc"], writes=["posc"])
                    p.do("vector", lambda v: v.tensor_scalar(
                        out=sb["validf"][:, 0:1], in0=sb["posc"][:, 0:1], scalar1=float(CAP),
                        scalar2=None, op0=ALU.is_lt),
                        reads=["posc"], writes=["validf"])
                    p.do("vector", lambda v: v.tensor_mul(out=sb["wc"][:, 0:1], in0=sb["sck"][:, 0:1],
                                                          in1=sb["sinv"][:, 0:1]),
                         reads=["sck", "sinv"], writes=["wc"])
                    p.do("vector", lambda v: v.tensor_mul(out=sb["wc"][:, 0:1], in0=sb["wc"][:, 0:1],
                                                          in1=sb["validf"][:, 0:1]),
                         reads=["wc", "validf"], writes=["wc"])
                    p.do("vector", lambda v: v.scalar_tensor_tensor(
                        out=sb["slotc"][:, 0:1], in0=sb["idxc"][:, 0:1], scalar=float(CAP),
                        in1=sb["posc"][:, 0:1], op0=ALU.mult, op1=ALU.add),
                        reads=["idxc", "posc"], writes=["slotc"])
                    p.do("vector", lambda v: v.tensor_scalar_add(
                        out=sb["slotc"][:, 0:1], in0=sb["slotc"][:, 0:1], scalar1=-float(NSLOT)),
                        reads=["slotc"], writes=["slotc"])
                    p.do("vector", lambda v: v.scalar_tensor_tensor(
                        out=sb["slotc"][:, 0:1], in0=sb["slotc"][:, 0:1], scalar=sb["validf"][:, 0:1],
                        in1=sb["nslotc"][:, 0:1], op0=ALU.mult, op1=ALU.add),
                        reads=["slotc", "validf", "nslotc"], writes=["slotc"])
                    p.do("vector", lambda v, sl=sl: v.tensor_copy(
                        out=sb["pay"][:, 2 * sl:2 * sl + 1], in_=sb["tokc"][:, 0:1]),
                        reads=["tokc"], writes=[f"pay{sl}"])
                    p.do("vector", lambda v, sl=sl: v.tensor_copy(
                        out=sb["pay"][:, 2 * sl + 1:2 * sl + 2], in_=sb["wc"][:, 0:1]),
                        reads=["wc"], writes=[f"pay{sl}"])
                    p.do("vector", lambda v, sl=sl: v.tensor_copy(
                        out=sb["slotu"][:, sl:sl + 1], in_=sb["slotc"][:, 0:1]),
                        reads=["slotc"], writes=[f"slotu{sl}"])
                    p.do("gpsimd", lambda g, sl=sl: g.indirect_dma_start(
                        out=d["Tmap_w"], out_offset=IndirectOffsetOnAxis(sb["slotu"][:, sl:sl + 1], 0),
                        in_=sb["pay"][:, 2 * sl:2 * sl + 2], in_offset=None),
                        reads=[f"pay{sl}", f"slotu{sl}", "Tmap_w_pre"],
                        writes=[f"TmW{nn}_{kk}"], dma=True, stream="Tmap_w",
                        counted=True)
                p.do("sync", lambda sy, nn=nn, pb=pb: sy.dma_start(
                    out=d["Smap"][nn * 128:(nn + 1) * 128, :],
                    in_=sb["slotu"][:, pb * K0:pb * K0 + K0]),
                    reads=[f"slotu{pb * K0 + kk}" for kk in range(K0)],
                    writes=[f"smap{nn}"], dma=True, stream="Smap", counted=True)
                n_own += 1
    return p


P0_SBUF = {
    "ones_f1": ([128, 1], F32), "ones_b1": ([128, 1], BF16),
    "ones_row": ([1, 128], F32),
    "iota64": ([128, E0], F32), "iotac": ([128, 1], F32),
    "su": ([128, 128], BF16), "ones128": ([128, 128], BF16),
    "identf": ([128, 128], F32),
    "init": ([128, 384], F32),
    "rwf": ([128, NCH * E0], F32),
    "row64": ([1, E0], F32), "row64b": ([1, E0], F32),
    "sel4": ([4, 1], F32), "cnt64": ([E0, 8], F32), "cntT": ([4, E0], F32),
    "ebias_b": ([128, E0], F32), "rwsum_b": ([128, E0], F32), "offg_b": ([128, E0], F32),
    "xa": ([128, NCH * 512], F32), "xb": ([128, NCH * 512], F32),
    "xsq": ([128, NCH * 512], F32),
    "murow": ([1, 512], F32), "m2row": ([1, 512], F32), "rrow": ([1, 512], F32),
    "logsb": ([E0, 512], F32),
    "mucol": ([128, 1], F32), "rcol": ([128, 1], F32), "negr": ([128, 1], F32),
    "prs": ([128, E0], F32), "sc64": ([128, E0], F32), "bsc": ([128, E0], F32),
    "top8": ([128, 8], F32),
    "Mtmp": ([128, E0], BF16),
    "prefx": ([128, E0], F32), "scr": ([128, E0], F32),
    "ssum": ([128, 1], F32), "sinv": ([128, 1], F32), "tokc": ([128, 1], F32),
    "oh": ([128, E0], F32), "idxc": ([128, 1], F32), "sck": ([128, 1], F32),
    "rkc": ([128, 1], F32), "offc": ([128, 1], F32), "posc": ([128, 1], F32),
    "validf": ([128, 1], F32), "wc": ([128, 1], F32), "nslotc": ([128, 1], F32),
    "slotc": ([128, 1], F32),
    "pay": ([128, 24], F32), "slotu": ([128, 12], U32),
    "vtok": ([128, 4 * D], BF16),
}

P0_PSUM = {
    "stat": ([1, 512], F32),
    "log": ([E0, 512], F32),
    "cnt": ([E0, 8], F32),
    "m64": ([128, 3 * E0], F32),
    "tsm": ([128, 66], F32),
}


def _build_launch(phase_fns, sbuf_specs, psum_specs, dram_roles):
    nc = bass.Bass("TRN2", target_bir_lowering=False, debug=False)
    d = {}
    out_names = []
    for name, (shape, dt, role) in dram_roles.items():
        if role == "in":
            d[name] = nc.dram_tensor(name, shape, dt, kind="ExternalInput").ap()
        elif role == "out":
            d[name] = nc.dram_tensor(name, shape, dt, kind="ExternalOutput").ap()
            out_names.append(name)
        else:
            d[name] = nc.dram_tensor(name, shape, dt, kind="Internal").ap()
    with ExitStack() as stack:
        sb = {}
        for name, (shape, dt) in sbuf_specs.items():
            sb[name] = stack.enter_context(nc.sbuf_tensor(f"s_{name}", shape, dt))
        pk1 = stack.enter_context(nc.psum_tensor("p_pk1", [128, 1024], F32))
        pk2 = stack.enter_context(nc.psum_tensor("p_pk2", [128, 1024], F32))
        pk3 = stack.enter_context(nc.psum_tensor("p_pk3", [128, 1024], F32))
        pkm = stack.enter_context(nc.psum_tensor("p_pkm", [128, 512], F32))
        pkT = stack.enter_context(nc.psum_tensor("p_pkT", [128, 1024], BF16))
        ps = {
            "pk1": pk1, "pk2": pk2, "pk3": pk3, "pkm": pkm, "pkT": pkT,
            "bmu": pk1[:, 0:512], "stat": pk1[0:1, 512:1024],
            "brr": pk2[:, 0:512], "tsm": pk2[:, 512:578],
            "log": pk3[0:64, 0:512], "m64": pk3[:, 512:704],
            "cnt": pkm[0:64, 64:72],
            "offs": pkm[0:32, 0:64],
            "ptb": pkT[:, 0:1024],
        }
        prog = AutoProg(dram_names=set(dram_roles.keys()))
        for fn in phase_fns:
            fn(prog, d, sb, ps)
        sems, cnt = prog.emit(nc, _engines(nc), stack)
    return nc, out_names


def _engines(nc):
    return {"tensor": nc.tensor, "vector": nc.vector, "scalar": nc.scalar,
            "gpsimd": nc.gpsimd, "sync": nc.sync}


def _roles_for(names_in, names_out, extra_internal=()):
    roles = {}
    for n in names_in:
        shape, dt, _ = DRAM_SPECS[n]
        roles[n] = (shape, dt, "in")
    for n in names_out:
        shape, dt, _ = DRAM_SPECS[n]
        roles[n] = (shape, dt, "out")
    for n in extra_internal:
        shape, dt, _ = DRAM_SPECS[n]
        roles[n] = (shape, dt, "internal")
    return roles


def merge_specs(*dicts):
    out = {}
    for dd in dicts:
        for k, v in dd.items():
            if k in out:
                assert out[k][0] == v[0] and out[k][1] == v[1], f"spec clash {k}"
            out[k] = v
    return out


def run_phase(phase_fns, sbuf_specs, psum_specs, names_in, names_out, in_maps,
              internal=()):
    roles = _roles_for(names_in, names_out, internal)
    nc, out_names = _build_launch(phase_fns, sbuf_specs, psum_specs, roles)
    res = run_bass_kernel_spmd(nc, in_maps, core_ids=list(range(N_CORES)))
    return res.results




def build_p1(p, d, sb, ps):
    TILE = 512
    NOT_OWN = (T_ALL - TB) // TILE

    p.do("gpsimd", lambda g: g.memset(sb["identb"][:], 0.0), writes=["identb"])
    p.do("gpsimd", lambda g: g.affine_select(out=sb["identb"][:], in_=sb["identb"][:],
                                             compare_op=ALU.not_equal, fill=1.0, base=0,
                                             pattern=[[-1, 128]], channel_multiplier=1),
         reads=["identb"], writes=["identb"])

    for q in range(4):
        p.do("gpsimd", lambda g, q=q: g.dma_start(
            out=d["x1"][q * 1024:(q + 1) * 1024, :], in_=d["x_tok"][q * 1024:(q + 1) * 1024, :]),
            reads=["x_tok"], writes=[f"x1i{q}"], dma=True, stream="x1init",
            counted=True)
    p.do("gpsimd", lambda g: g.memset(sb["zrow16"][:], 0.0), writes=["zrow16"])
    p.do("gpsimd", lambda g: g.dma_start(out=d["x1"][TB:TB + 1, :], in_=sb["zrow16"][0:1, 0:D]),
         reads=["zrow16"], writes=["x1iz"], dma=True, stream="x1init", counted=True)
    p.do("gpsimd", lambda g: g.dma_start(out=d["xn_tok"][TB:TB + 1, :], in_=sb["zrow16"][0:1, 0:D]),
         reads=["zrow16"], writes=["xntkz"], dma=True, stream="xn_tok",
         counted=True)
    p.do("gpsimd", lambda g: g.memset(sb["zrow32"][:], 0.0), writes=["zrow32"])
    p.do("gpsimd", lambda g: g.dma_start(out=d["y_slot"][NSLOT:NSLOT + 1, :],
                                         in_=sb["zrow32"][0:1, 0:D]),
         reads=["zrow32"], writes=["yslz"], dma=True, stream="yslotz",
         counted=True)

    for i in range(TB // TILE):
        xbuf = "xa" if i % 2 == 0 else "xb"
        it = NOT_OWN + i
        p.do("sync", lambda sy, it=it, xbuf=xbuf: sy.dma_start(
            out=sb[xbuf][:].rearrange("p (c t) -> p c t", c=NCH),
            in_=_chunks(d["xT_perm"])[:, :, it * TILE:(it + 1) * TILE]),
            writes=[xbuf], dma=True)
        p.do("scalar", lambda sc, i=i: sc.dma_start(
            out=sb["murow"][0:1, :], in_=d["mu_own"][0:1, i * TILE:(i + 1) * TILE]),
            reads=["mu_own"], writes=["murow"], dma=True)
        p.do("scalar", lambda sc, i=i: sc.dma_start(
            out=sb["rrow"][0:1, :], in_=d["r_own"][0:1, i * TILE:(i + 1) * TILE]),
            reads=["r_own"], writes=["rrow"], dma=True)
        p.do("tensor", lambda t: t.matmul(ps["bmu"][:], sb["ones_row"][:],
                                          sb["murow"][0:1, :], start=True, stop=True),
             reads=["ones_row", "murow"], writes=["ps_bmu"])
        p.do("tensor", lambda t: t.matmul(ps["brr"][:], sb["ones_row"][:],
                                          sb["rrow"][0:1, :], start=True, stop=True),
             reads=["ones_row", "rrow"], writes=["ps_brr"])
        for c in range(NCH):
            p.do("vector", lambda v, c=c, xbuf=xbuf: v.tensor_sub(
                out=sb["xsq"][:, c * TILE:(c + 1) * TILE],
                in0=sb[xbuf][:, c * TILE:(c + 1) * TILE], in1=ps["bmu"][:]),
                reads=[xbuf, "ps_bmu"], writes=["xsq"])
        for c in range(NCH):
            p.do("vector", lambda v, c=c: v.tensor_mul(
                out=sb["xnt"][:, c * TILE:(c + 1) * TILE],
                in0=sb["xsq"][:, c * TILE:(c + 1) * TILE], in1=ps["brr"][:]),
                reads=["xsq", "ps_brr"], writes=["xnt"])
        p.do("sync", lambda sy, i=i: sy.dma_start(
            out=_chunks(d["xnT"])[:, :, i * TILE:(i + 1) * TILE],
            in_=sb["xnt"][:].rearrange("p (c t) -> p c t", c=NCH)),
            reads=["xnt"], writes=[f"xnT{i}"], dma=True, stream="xnT",
            counted=True)
        for j in range(TILE // 128):
            for c in range(NCH):
                p.do("tensor", lambda t, j=j, c=c: t.transpose(
                    ps["ptb"][:, c * 128:(c + 1) * 128],
                    sb["xnt"][:, c * TILE + j * 128:c * TILE + (j + 1) * 128],
                    sb["identb"][:]),
                    reads=["xnt", "identb"], writes=["ps_ptb"])
            p.do("vector", lambda v: v.tensor_copy(out=sb["tokrow"][:], in_=ps["ptb"][:]),
                 reads=["ps_ptb"], writes=["tokrow"])
            p.do("scalar", lambda sc, i=i, j=j: sc.dma_start(
                out=d["xn_tok"][i * TILE + j * 128:i * TILE + (j + 1) * 128, :],
                in_=sb["tokrow"][:]),
                reads=["tokrow"], writes=[f"xntk{i}_{j}"], dma=True,
                stream="xn_tok", counted=True)
    return p


P1_SBUF = {
    "identb": ([128, 128], BF16),
    "zrow16": ([1, D], BF16),
    "zrow32": ([1, D], F32),
    "murow": ([1, 512], F32), "rrow": ([1, 512], F32),
    "xa": ([128, NCH * 512], F32), "xb": ([128, NCH * 512], F32),
    "xsq": ([128, NCH * 512], F32),
    "xnt": ([128, NCH * 512], BF16),
    "tokrow": ([128, D], BF16),
    "ones_row": ([1, 128], F32),
}

P1_PSUM = {
    "bmu": ([128, 512], F32),
    "brr": ([128, 512], F32),
    "ptb": ([128, D], BF16),
}


def _load_w_cast(p, d, sb, name, dst):
    p.do("gpsimd", lambda g: g.dma_start(out=sb[dst][:], in_=d[name]),
         reads=[name], writes=[dst], dma=True)


def build_p2(p, d, sb, ps):
    TILE = 512
    NT = TB // TILE
    for nm, dst in (("Wq", "wq"), ("Wk", "wk"), ("Wv", "wv"), ("Wo", "wo")):
        _load_w_cast(p, d, sb, nm, dst)
    p.do("gpsimd", lambda g: g.dma_start(
        out=sb["projb"][:].rearrange("p (h f) -> p h f", h=8),
        in_=d["projT"].rearrange("h p f -> p h f")),
        reads=["projT"], writes=["projb"], dma=True)
    for nm in ("bq", "bk", "bv", "bo"):
        p.do("scalar", lambda sc, nm=nm: sc.dma_start(out=sb[f"{nm}s"][:], in_=d[nm]),
             reads=[nm], writes=[f"{nm}s"], dma=True)
    p.do("gpsimd", lambda g: g.memset(sb["hsel"][:], 0.0), writes=["hsel"])
    for c in range(NCH):
        p.do("gpsimd", lambda g, c=c: g.memset(sb["hsel"][0:64, c * 16 + 2 * c:c * 16 + 2 * c + 1], 1.0),
             reads=["hsel"], writes=["hsel"])
        p.do("gpsimd", lambda g, c=c: g.memset(sb["hsel"][64:128, c * 16 + 2 * c + 1:c * 16 + 2 * c + 2], 1.0),
             reads=["hsel"], writes=["hsel"])
    p.do("vector", lambda v: v.tensor_copy(out=sb["hself"][:], in_=sb["hsel"][:]),
         reads=["hsel"], writes=["hself"])
    for c in range(NCH):
        p.do("tensor", lambda t, c=c: t.transpose(
            ps["log"][0:16, 0:128], sb["hself"][:, c * 16:(c + 1) * 16], sb["identf"][:]),
            reads=["hself", "identf"], writes=["ps_log"])
        p.do("vector", lambda v, c=c: v.tensor_copy(
            out=sb["gsel"][0:16, c * 128:(c + 1) * 128], in_=ps["log"][0:16, 0:128]),
            reads=["ps_log"], writes=["gsel"])
    p.do("gpsimd", lambda g: g.memset(sb["xb"][:, 0:2048], 0.0), writes=["xb"])
    p.do("gpsimd", lambda g: g.memset(sb["ksum"][:], 0.0), writes=["ksum"])

    def spherical(xkind, tile_i, src_buf, dst_buf):
        p.do("scalar", lambda sc: sc.activation(out=sb["xnt"][:], in_=sb[src_buf][:],
                                                func=AF.Square),
             reads=[src_buf], writes=["xnt"])
        for c in range(NCH):
            p.do("tensor", lambda t, c=c: t.matmul(
                ps["log"][0:16, 0:TILE], sb["hsel"][:, c * 16:(c + 1) * 16],
                sb["xnt"][:, c * TILE:(c + 1) * TILE], start=(c == 0), stop=(c == NCH - 1)),
                reads=["xnt", "hsel"], writes=["ps_log"])
        p.do("scalar", lambda sc: sc.activation(out=sb["logsb"][0:16, 0:TILE], in_=ps["log"][0:16, 0:TILE],
                                                func=AF.Sqrt),
             reads=["ps_log"], writes=["logsb"])
        p.do("vector", lambda v: v.tensor_scalar_add(out=sb["logsb"][0:16, 0:TILE], in0=sb["logsb"][0:16, 0:TILE],
                                                     scalar1=1e-8),
             reads=["logsb"], writes=["logsb"])
        p.do("vector", lambda v: v.reciprocal(out=sb["logsb"][0:16, 0:TILE], in_=sb["logsb"][0:16, 0:TILE]),
             reads=["logsb"], writes=["logsb"])
        for c in range(NCH):
            p.do("tensor", lambda t, c=c: t.matmul(
                ps["bmu"][:, 0:TILE], sb["gsel"][0:16, c * 128:(c + 1) * 128],
                sb["logsb"][0:16, 0:TILE], start=True, stop=True),
                reads=["gsel", "logsb"], writes=["ps_bmu"])
            p.do("vector", lambda v, c=c: v.tensor_mul(
                out=sb[src_buf][:, c * TILE:(c + 1) * TILE],
                in0=sb[src_buf][:, c * TILE:(c + 1) * TILE], in1=ps["bmu"][:, 0:TILE]),
                reads=[src_buf, "ps_bmu"], writes=[src_buf])

    def proj_feats(src_buf, h, fc, out_buf):
        hp, hr = h // 2, (h % 2) * 64
        p.do("tensor", lambda t, hp=hp, hr=hr, h=h, fc=fc: t.matmul(
            ps["brr"][:, 0:TILE],
            sb["projb"][hr:hr + 64, hp * NF + fc * 128:hp * NF + (fc + 1) * 128],
            sb[src_buf][hr:hr + 64, hp * TILE:(hp + 1) * TILE], start=True, stop=True),
            reads=["projb", src_buf], writes=["ps_brr"])
        p.do("vector", lambda v: v.tensor_scalar(
            out=sb[out_buf][:], in0=ps["brr"][:, 0:TILE], scalar1=0.0, scalar2=YAT_EPS,
            op0=ALU.max, op1=ALU.add),
            reads=["ps_brr"], writes=[out_buf])

    for i in range(NT):
        xb_ = "xnta" if i % 2 == 0 else "xntb"
        p.do("sync", lambda sy, i=i, xb_=xb_: sy.dma_start(
            out=sb[xb_][:].rearrange("p (c t) -> p c t", c=NCH),
            in_=_chunks(d["xnT"])[:, :, i * TILE:(i + 1) * TILE]),
            stream_reads=["xnT"], writes=[xb_], dma=True)
        for dstb, wname, bname in (("ksb", "wk", "bks"), ("vsb", "wv", "bvs")):
            for mc in range(NCH):
                for kc in range(NCH):
                    p.do("tensor", lambda t, mc=mc, kc=kc, wname=wname, xb_=xb_: t.matmul(
                        ps["bmu"][:, 0:TILE], sb[wname][:, (kc * NCH + mc) * 128:(kc * NCH + mc) * 128 + 128],
                        sb[xb_][:, kc * TILE:(kc + 1) * TILE],
                        start=(kc == 0), stop=(kc == NCH - 1)),
                        reads=[wname, xb_], writes=["ps_bmu"])
                p.do("vector", lambda v, mc=mc, dstb=dstb, bname=bname: v.tensor_scalar_add(
                    out=sb[dstb][:, mc * TILE:(mc + 1) * TILE], in0=ps["bmu"][:, 0:TILE],
                    scalar1=sb[bname][:, mc:mc + 1]),
                    reads=["ps_bmu", bname], writes=[dstb])
        for j in range(TILE // 128):
            for c in range(NCH):
                p.do("tensor", lambda t, j=j, c=c: t.transpose(
                    ps["ptb"][:, c * 128:(c + 1) * 128],
                    sb["vsb"][:, c * TILE + j * 128:c * TILE + (j + 1) * 128],
                    sb["identb"][:]),
                    reads=["vsb", "identb"], writes=["ps_ptb"])
            p.do("vector", lambda v, j=j: v.tensor_copy(
                out=sb["vtok"][:, j * D:(j + 1) * D], in_=ps["ptb"][:]),
                reads=["ps_ptb"], writes=["vtok"])
        spherical("K", i, "ksb", None)
        for h in range(H):
            for fc in range(2):
                proj_feats("ksb", h, fc, "kp")
                col = (h * 2 + fc)
                p.do("vector", lambda v, col=col: v.reduce_sum(
                    out=sb["kps"][:, 0:1], in_=sb["kp"][:], axis=AX.X),
                    reads=["kp"], writes=["kps"])
                p.do("vector", lambda v, col=col: v.tensor_add(
                    out=sb["ksum"][:, col:col + 1], in0=sb["ksum"][:, col:col + 1],
                    in1=sb["kps"][:, 0:1]),
                    reads=["ksum", "kps"], writes=["ksum"])
                for j in range(TILE // 128):
                    p.do("tensor", lambda t, j=j: t.transpose(
                        ps["ptb"][:, j * 128:(j + 1) * 128],
                        sb["kp"][:, j * 128:(j + 1) * 128], sb["identb"][:]),
                        reads=["kp", "identb"], writes=["ps_ptb"])
                p.do("vector", lambda v: v.tensor_copy(out=sb["tokrow"][:, 0:512], in_=ps["ptb"][:, 0:512]),
                     reads=["ps_ptb"], writes=["tokrow"])
                for j in range(TILE // 128):
                    p.do("tensor", lambda t, j=j, h=h: t.matmul(
                        ps["m64"][:, 0:E0], sb["tokrow"][:, j * 128:(j + 1) * 128],
                        sb["vtok"][:, j * D + h * 64:j * D + (h + 1) * 64],
                        start=(j == 0), stop=(j == TILE // 128 - 1)),
                        reads=["tokrow", "vtok"], writes=["ps_brd"])
                p.do("vector", lambda v, col=col: v.tensor_add(
                    out=sb["xb"][:, col * 64:(col + 1) * 64],
                    in0=sb["xb"][:, col * 64:(col + 1) * 64], in1=ps["m64"][:, 0:E0]),
                    reads=["xb", "ps_brd"], writes=["xb"])

    p.do("vector", lambda v: v.tensor_copy(out=sb["vsb"][:, 0:2048], in_=sb["xb"][:, 0:2048]),
         reads=["xb"], writes=["vsb"])
    p.do("vector", lambda v: v.tensor_copy(out=sb["ksum16"][:], in_=sb["ksum"][:]),
         reads=["ksum"], writes=["ksum16"])

    for i in range(NT):
        xb_ = "xnta" if i % 2 == 0 else "xntb"
        p.do("sync", lambda sy, i=i, xb_=xb_: sy.dma_start(
            out=sb[xb_][:].rearrange("p (c t) -> p c t", c=NCH),
            in_=_chunks(d["xnT"])[:, :, i * TILE:(i + 1) * TILE]),
            stream_reads=["xnT"], writes=[xb_], dma=True)
        for mc in range(NCH):
            for kc in range(NCH):
                p.do("tensor", lambda t, mc=mc, kc=kc, xb_=xb_: t.matmul(
                    ps["bmu"][:, 0:TILE], sb["wq"][:, (kc * NCH + mc) * 128:(kc * NCH + mc) * 128 + 128],
                    sb[xb_][:, kc * TILE:(kc + 1) * TILE],
                    start=(kc == 0), stop=(kc == NCH - 1)),
                    reads=["wq", xb_], writes=["ps_bmu"])
            p.do("vector", lambda v, mc=mc: v.tensor_scalar_add(
                out=sb["ksb"][:, mc * TILE:(mc + 1) * TILE], in0=ps["bmu"][:, 0:TILE],
                scalar1=sb["bqs"][:, mc:mc + 1]),
                reads=["ps_bmu", "bqs"], writes=["ksb"])
        spherical("Q", i, "ksb", None)
        for h in range(H):
            for fc in range(2):
                proj_feats("ksb", h, fc, "kp")
                col = h * 2 + fc
                p.do("tensor", lambda t, col=col, fc=fc: t.matmul(
                    ps["log"][0:E0, 0:TILE], sb["vsb"][:, col * 64:(col + 1) * 64],
                    sb["kp"][:], start=(fc == 0), stop=(fc == 1)),
                    reads=["vsb", "kp"], writes=["ps_log"])
                p.do("tensor", lambda t, col=col, fc=fc: t.matmul(
                    ps["stat"][0:1, 0:TILE], sb["ksum16"][:, col:col + 1],
                    sb["kp"][:], start=(fc == 0), stop=(fc == 1)),
                    reads=["ksum16", "kp"], writes=["ps_stat"])
            p.do("vector", lambda v: v.tensor_scalar_max(
                out=sb["murow"][0:1, :], in0=ps["stat"][0:1, 0:TILE], scalar1=1e-6),
                reads=["ps_stat"], writes=["murow"])
            p.do("vector", lambda v: v.reciprocal(out=sb["murow"][0:1, :], in_=sb["murow"][0:1, :]),
                 reads=["murow"], writes=["murow"])
            p.do("tensor", lambda t: t.matmul(
                ps["bmu"][:, 0:TILE], sb["ones_row"][:], sb["murow"][0:1, :],
                start=True, stop=True),
                reads=["ones_row", "murow"], writes=["ps_bmu"])
            hp, hr = h // 2, (h % 2) * 64
            p.do("vector", lambda v: v.tensor_copy(out=sb["xsq"][:, 0:TILE], in_=ps["bmu"][:, 0:TILE]),
                 reads=["ps_bmu"], writes=["xsq"])
            p.do("vector", lambda v, hp=hp, hr=hr: v.tensor_mul(
                out=sb["asb"][hr:hr + 64, hp * TILE:(hp + 1) * TILE],
                in0=ps["log"][0:E0, 0:TILE], in1=sb["xsq"][0:64, 0:TILE]),
                reads=["ps_log", "xsq"], writes=["asb"])
        for mc in range(NCH):
            for kc in range(NCH):
                p.do("tensor", lambda t, mc=mc, kc=kc: t.matmul(
                    ps["brr"][:, 0:TILE], sb["wo"][:, (kc * NCH + mc) * 128:(kc * NCH + mc) * 128 + 128],
                    sb["asb"][:, kc * TILE:(kc + 1) * TILE],
                    start=(kc == 0), stop=(kc == NCH - 1)),
                    reads=["wo", "asb"], writes=["ps_brr"])
            p.do("vector", lambda v, mc=mc: v.tensor_scalar_add(
                out=sb["xa"][:, mc * TILE:(mc + 1) * TILE], in0=ps["brr"][:, 0:TILE],
                scalar1=sb["bos"][:, mc:mc + 1]),
                reads=["ps_brr", "bos"], writes=["xa"])
        for j in range(TILE // 128):
            for half in range(2):
                for c4 in range(4):
                    c = half * 4 + c4
                    p.do("tensor", lambda t, j=j, c=c, c4=c4: t.transpose(
                        ps["bmu"][:, c4 * 128:(c4 + 1) * 128],
                        sb["xa"][:, c * TILE + j * 128:c * TILE + (j + 1) * 128],
                        sb["identf"][:]),
                        reads=["xa", "identf"], writes=["ps_bmu"])
                p.do("vector", lambda v, half=half: v.tensor_copy(
                    out=sb["xsq"][:, 512 + half * 512:512 + (half + 1) * 512], in_=ps["bmu"][:, 0:512]),
                    reads=["ps_bmu"], writes=["atok"])
            p.do("gpsimd", lambda g, i=i, j=j: g.dma_start(
                out=d["x1"][i * TILE + j * 128:i * TILE + (j + 1) * 128, :],
                in_=sb["xsq"][:, 512:1536], accum_op=ALU.add),
                reads=["atok"], stream_reads=["x1init"], writes=[f"x1p2_{i}_{j}"],
                dma=True, stream="x1a", counted=True)
    return p


P2_SBUF = {
    "wq": ([128, NCH * NCH * 128], BF16), "wk": ([128, NCH * NCH * 128], BF16),
    "wv": ([128, NCH * NCH * 128], BF16), "wo": ([128, NCH * NCH * 128], BF16),
    "projb": ([128, 8 * NF], BF16),
    "bqs": ([128, NCH], F32), "bks": ([128, NCH], F32),
    "bvs": ([128, NCH], F32), "bos": ([128, NCH], F32),
    "hsel": ([128, NCH * 16], BF16), "hself": ([128, NCH * 16], F32),
    "gsel": ([16, NCH * 128], F32),
    "identf": ([128, 128], F32), "identb": ([128, 128], BF16),
    "ones_row": ([1, 128], F32),
    "xnta": ([128, NCH * 512], BF16), "xntb": ([128, NCH * 512], BF16),
    "ksb": ([128, NCH * 512], BF16), "vsb": ([128, NCH * 512], BF16),

    "kp": ([128, 512], BF16), "kps": ([128, 1], F32),
    "ksum": ([128, 32], F32), "ksum16": ([128, 32], BF16),
    "asb": ([128, NCH * 512], BF16),
}

P2_PSUM = {}


def build_p3(p, d, sb, ps):
    TILE = 512
    NT = TB // TILE
    NM1 = SF // 128
    for half, dst in ((0, "wq"), (1, "wk")):
        p.do("gpsimd", lambda g, half=half, dst=dst: g.dma_start(
            out=sb[dst][:].rearrange("p (kc mc m) -> p kc mc m", kc=NCH, mc=NCH),
            in_=d["sw1T"].rearrange("p (kc mc m) -> p kc mc m", kc=8, mc=16)[:, :, half * 8:(half + 1) * 8, :]),
            reads=["sw1T"], writes=[dst], dma=True)
    for half, dst in ((0, "wv"), (1, "wo")):
        p.do("gpsimd", lambda g, half=half, dst=dst: g.dma_start(
            out=sb[dst][:].rearrange("p (kc mc m) -> p kc mc m", kc=NCH, mc=NCH),
            in_=d["sw2T"].rearrange("p (kc mc m) -> p kc mc m", kc=16, mc=8)[:, half * 8:(half + 1) * 8, :, :]),
            reads=["sw2T"], writes=[dst], dma=True)

    for i in range(NT):
        xb_ = "xnta" if i % 2 == 0 else "xntb"
        p.do("sync", lambda sy, i=i, xb_=xb_: sy.dma_start(
            out=sb[xb_][:].rearrange("p (c t) -> p c t", c=NCH),
            in_=_chunks(d["xnT"])[:, :, i * TILE:(i + 1) * TILE]),
            stream_reads=["xnT"], writes=[xb_], dma=True)
        for mc in range(NM1):
            wname = "wq" if mc < 8 else "wk"
            mloc = mc % 8
            for kc in range(NCH):
                p.do("tensor", lambda t, mc=mc, kc=kc, wname=wname, mloc=mloc, xb_=xb_: t.matmul(
                    ps["bmu"][:, 0:TILE], sb[wname][:, (kc * NCH + mloc) * 128:(kc * NCH + mloc) * 128 + 128],
                    sb[xb_][:, kc * TILE:(kc + 1) * TILE],
                    start=(kc == 0), stop=(kc == NCH - 1)),
                    reads=[wname, xb_], writes=["ps_bmu"])
            dstb = "ksb" if mc < 8 else "vsb"
            p.do("scalar", lambda sc, mc=mc, dstb=dstb, mloc=mloc: sc.activation(
                out=sb[dstb][:, mloc * TILE:(mloc + 1) * TILE], in_=ps["bmu"][:, 0:TILE],
                func=AF.Gelu_apprx_tanh),
                reads=["ps_bmu"], writes=[dstb])
        for mc in range(NCH):
            for kc in range(SF // 128):
                wname = "wv" if kc < 8 else "wo"
                kloc = kc % 8
                sname = "ksb" if kc < 8 else "vsb"
                p.do("tensor", lambda t, mc=mc, kc=kc, wname=wname, kloc=kloc, sname=sname: t.matmul(
                    ps["brr"][:, 0:TILE], sb[wname][:, (kloc * NCH + mc) * 128:(kloc * NCH + mc) * 128 + 128],
                    sb[sname][:, (kloc) * TILE:(kloc + 1) * TILE],
                    start=(kc == 0), stop=(kc == SF // 128 - 1)),
                    reads=[wname, sname], writes=["ps_brr"])
            p.do("vector", lambda v, mc=mc: v.tensor_copy(
                out=sb["xa"][:, mc * TILE:(mc + 1) * TILE], in_=ps["brr"][:, 0:TILE]),
                reads=["ps_brr"], writes=["xa"])
        for j in range(TILE // 128):
            for half in range(2):
                for c4 in range(4):
                    c = half * 4 + c4
                    p.do("tensor", lambda t, j=j, c=c, c4=c4: t.transpose(
                        ps["bmu"][:, c4 * 128:(c4 + 1) * 128],
                        sb["xa"][:, c * TILE + j * 128:c * TILE + (j + 1) * 128],
                        sb["identf"][:]),
                        reads=["xa", "identf"], writes=["ps_bmu"])
                p.do("vector", lambda v, half=half: v.tensor_copy(
                    out=sb["xsq"][:, 512 + half * 512:512 + (half + 1) * 512], in_=ps["bmu"][:, 0:512]),
                    reads=["ps_bmu"], writes=["atok"])
            p.do("gpsimd", lambda g, i=i, j=j: g.dma_start(
                out=d["x1"][i * TILE + j * 128:i * TILE + (j + 1) * 128, :],
                in_=sb["xsq"][:, 512:1536], accum_op=ALU.add),
                reads=["atok"], stream_reads=["x1init", "x1a"], writes=[f"x1p3_{i}_{j}"],
                dma=True, stream="x1b", counted=True)
    return p


P3_SBUF = {}
P3_PSUM = {}


def build_p4(p, d, sb, ps):
    NS3 = CAP // 128
    p.claim("p4w1_0", ["wq"]); p.claim("p4w1_1", ["wq"])
    p.claim("p4w2_0", ["wv"]); p.claim("p4w2_1", ["wv"])
    for s3 in range(NS3):
        p.claim(f"p4xg_0_{s3}", ["xnta"]); p.claim(f"p4xg_1_{s3}", ["xntb"])
    p.claim("p4xsT_0", ["ksb"]); p.claim("p4xsT_1", ["vsb"])
    p.claim("p4h1_0", ["asb"]); p.claim("p4h1_1", ["asb"])
    for k in range(4):
        p.claim(f"p4ysl_{k // 2}_{k % 2}", ["xa"])
    p.claim("p4h1a", ["ps_bmu"]); p.claim("p4h1b", ["ps_stat"])
    p.claim("p4ya", ["ps_brr"]); p.claim("p4yb", ["ps_trow", "ps_tcol"])

    def w1buf(par):
        return sb["wq"][:, par * 4096:(par + 1) * 4096]

    def w2buf(par):
        return sb["wv"][:, par * 4096:(par + 1) * 4096]

    def xgbuf(par):
        return sb["xnta"] if par == 0 else sb["xntb"]

    def xsTbuf(par):
        return sb["ksb"] if par == 0 else sb["vsb"]

    def h1buf(par):
        return sb["asb"][:, par * 1536:(par + 1) * 1536]

    for e in range(E0):
        par = e % 2
        p.do("gpsimd", lambda g, e=e, par=par: g.dma_start(
            out=w1buf(par), in_=d["w1"][e]),
            reads=["w1"], writes=[f"p4w1_{par}"], dma=True, stream=f"sw1_{par}")
        p.do("gpsimd", lambda g, e=e, par=par: g.dma_start(
            out=w2buf(par), in_=d["w2"][e]),
            reads=["w2"], writes=[f"p4w2_{par}"], dma=True, stream=f"sw2_{par}")
        p.do("sync", lambda sy, e=e, par=par: sy.dma_start(
            out=sb["tmw"][:, par * 6:par * 6 + 6].rearrange("p (s c) -> p s c", s=NS3),
            in_=d["Tmap_w"][e * CAP:(e + 1) * CAP, :].rearrange("(s p) c -> p s c", p=128)),
            reads=[], stream_reads=["Tmap_w", "TmapP"], writes=[f"p4tm_{par}"],
            dma=True, stream=f"stm_{par}")
        p.do("vector", lambda v, par=par: v.tensor_copy(
            out=sb["idxu"][:, par * 3:par * 3 + 3],
            in_=sb["tmw"][:, par * 6:par * 6 + 6:2]),
            reads=[f"p4tm_{par}"], writes=[f"p4ix_{par}"])
        for s3 in range(NS3):
            p.do("gpsimd", lambda g, par=par, s3=s3: g.indirect_dma_start(
                out=xgbuf(par)[:, s3 * 1024:(s3 + 1) * 1024], out_offset=None,
                in_=d["xn_tok"],
                in_offset=IndirectOffsetOnAxis(sb["idxu"][:, par * 3 + s3:par * 3 + s3 + 1], 0)),
                reads=[f"p4ix_{par}"], stream_reads=["xn_tok"],
                writes=[f"p4xg_{par}_{s3}"], dma=True, stream=f"sxg_{par}_{s3}")
            for c in range(NCH):
                p.do("tensor", lambda t, par=par, s3=s3, c=c: t.transpose(
                    ps["ptb"][:, c * 128:(c + 1) * 128],
                    xgbuf(par)[:, s3 * 1024 + c * 128:s3 * 1024 + (c + 1) * 128],
                    sb["identb"][:]),
                    reads=[f"p4xg_{par}_{s3}", "identb"], writes=["ps_ptb"])
            p.do("vector", lambda v, par=par, s3=s3: v.tensor_copy(
                out=xsTbuf(par)[:, 0:NCH * CAP].rearrange("p (c t) -> p c t", c=NCH, t=CAP)[:, :, s3 * 128:(s3 + 1) * 128],
                in_=ps["ptb"][:].rearrange("p (c t) -> p c t", c=NCH, t=128)),
                reads=["ps_ptb"], writes=[f"p4xsT_{par}"])
        for mc in range(4):
            hps = ps["pk1"][:, 0:CAP] if mc % 2 == 0 else ps["pk1"][:, 512:512 + CAP]
            hname = "p4h1a" if mc % 2 == 0 else "p4h1b"
            for kc in range(NCH):
                p.do("tensor", lambda t, par=par, mc=mc, kc=kc, hps=hps: t.matmul(
                    hps, w1buf(par)[:, (kc * 4 + mc) * 128:(kc * 4 + mc) * 128 + 128],
                    xsTbuf(par)[:, kc * CAP:(kc + 1) * CAP],
                    start=(kc == 0), stop=(kc == NCH - 1)),
                    reads=[f"p4w1_{par}", f"p4xsT_{par}"], writes=[hname])
            p.do("scalar", lambda sc, par=par, mc=mc, hps=hps: sc.activation(
                out=h1buf(par)[:, mc * CAP:(mc + 1) * CAP], in_=hps,
                func=AF.Gelu_apprx_tanh),
                reads=[hname], writes=[f"p4h1_{par}"])
        for s3 in range(NS3):
            for dh in range(2):
                ypar = (s3 * 2 + dh) % 2
                yps = ps["pk2"][:, 0:512] if ypar == 0 else ps["pk2"][:, 512:1024]
                yname = "p4ya" if ypar == 0 else "p4yb"
                for mc in range(4):
                    p.do("tensor", lambda t, par=par, s3=s3, dh=dh, mc=mc, yps=yps: t.matmul(
                        yps,
                        h1buf(par)[:, mc * CAP + s3 * 128:mc * CAP + (s3 + 1) * 128],
                        w2buf(par)[:, mc * D + dh * 512:mc * D + (dh + 1) * 512],
                        start=(mc == 0), stop=(mc == 3)),
                        reads=[f"p4h1_{par}", f"p4w2_{par}"], writes=[yname])
                p.do("vector", lambda v, par=par, s3=s3, dh=dh, yps=yps: v.tensor_scalar_mul(
                    out=sb["xa"][:, (par * 2 + s3 % 2) * 1024 + dh * 512:(par * 2 + s3 % 2) * 1024 + (dh + 1) * 512],
                    in0=yps, scalar1=sb["tmw"][:, par * 6 + 2 * s3 + 1:par * 6 + 2 * s3 + 2]),
                    reads=[yname, f"p4tm_{par}"], writes=[f"p4ysl_{par}_{s3 % 2}"])
            p.do("sync", lambda sy, e=e, par=par, s3=s3: sy.dma_start(
                out=d["y_slot"][e * CAP + s3 * 128:e * CAP + (s3 + 1) * 128, :],
                in_=sb["xa"][:, (par * 2 + s3 % 2) * 1024:(par * 2 + s3 % 2 + 1) * 1024]),
                reads=[f"p4ysl_{par}_{s3 % 2}"], writes=[f"ysl{e}_{s3}"],
                dma=True, stream="yslot", counted=True)
    return p


P4_SBUF = {
    "tmw": ([128, 12], F32),
    "idxu": ([128, 6], U32),
    "smb": ([128, 12], U32),
}
P4_PSUM = {}


def build_p4b(p, d, sb, ps):
    for par in range(2):
        p.claim(f"p4bx1_{par}", ["xb"]); p.claim(f"p4bt_{par}", ["xb"])
    p.claim("p4bg_0_0", ["p4ysl_0_0"]); p.claim("p4bg_0_1", ["p4ysl_0_1"])
    p.claim("p4bg_0_2", ["p4ysl_1_0"]); p.claim("p4bg_1_0", ["p4ysl_1_1"])
    p.claim("p4bg_1_1", ["xsq", "atok"]); p.claim("p4bg_1_2", ["xsq", "atok"])

    def gslot(par, s):
        k = par * 3 + s
        if k < 4:
            return sb["xa"][:, k * 1024:(k + 1) * 1024]
        return sb["xsq"][:, (k - 4) * 1024:(k - 3) * 1024]

    def x1t(par):
        return sb["xb"][:, par * 2048:par * 2048 + 1024]

    def tmp(par):
        return sb["xb"][:, par * 2048 + 1024:par * 2048 + 2048]

    for t in range(TB // 128):
        par = t % 2
        p.do("scalar", lambda sc, t=t, par=par: sc.dma_start(
            out=x1t(par), in_=d["x1"][t * 128:(t + 1) * 128, :]),
            reads=[], stream_reads=["x1init", "x1a", "x1b"],
            writes=[f"p4bx1_{par}"], dma=True, stream=f"sx1l_{par}")
        p.do("scalar", lambda sc, t=t, par=par: sc.dma_start(
            out=sb["smb"][:, par * 6:par * 6 + 6], in_=d["Smap"][t * 128:(t + 1) * 128, :]),
            reads=[], stream_reads=["Smap"], writes=[f"p4bsm_{par}"],
            dma=True, stream=f"ssm_{par}")
        for s in range(3):
            p.do("gpsimd", lambda g, par=par, s=s: g.indirect_dma_start(
                out=gslot(par, s), out_offset=None, in_=d["y_slot"],
                in_offset=IndirectOffsetOnAxis(sb["smb"][:, par * 6 + s:par * 6 + s + 1], 0)),
                reads=[f"p4bsm_{par}"], stream_reads=["yslot", "yslotz"],
                writes=[f"p4bg_{par}_{s}"], dma=True, stream=f"syg_{par}_{s}")
        p.do("vector", lambda v, par=par: v.tensor_add(
            out=tmp(par), in0=gslot(par, 0), in1=gslot(par, 1)),
            reads=[f"p4bg_{par}_0", f"p4bg_{par}_1"], writes=[f"p4bt_{par}"])
        p.do("gpsimd", lambda g, par=par: g.tensor_add(
            out=x1t(par), in0=x1t(par), in1=gslot(par, 2)),
            reads=[f"p4bx1_{par}", f"p4bg_{par}_2"], writes=[f"p4bx1_{par}"])
        for s in range(3):
            p.do("gpsimd", lambda g, par=par, s=s: g.indirect_dma_start(
                out=gslot(par, s), out_offset=None, in_=d["y_slot"],
                in_offset=IndirectOffsetOnAxis(sb["smb"][:, par * 6 + 3 + s:par * 6 + 4 + s], 0)),
                reads=[f"p4bsm_{par}"], stream_reads=["yslot", "yslotz"],
                writes=[f"p4bg_{par}_{s}"], dma=True, stream=f"syg_{par}_{s}")
        p.do("vector", lambda v, par=par: v.tensor_add(
            out=gslot(par, 0), in0=gslot(par, 0), in1=gslot(par, 1)),
            reads=[f"p4bg_{par}_0", f"p4bg_{par}_1"], writes=[f"p4bg_{par}_0"])
        p.do("gpsimd", lambda g, par=par: g.tensor_add(
            out=x1t(par), in0=x1t(par), in1=gslot(par, 2)),
            reads=[f"p4bx1_{par}", f"p4bg_{par}_2"], writes=[f"p4bx1_{par}"])
        p.do("vector", lambda v, par=par: v.tensor_add(
            out=tmp(par), in0=tmp(par), in1=gslot(par, 0)),
            reads=[f"p4bt_{par}", f"p4bg_{par}_0"], writes=[f"p4bt_{par}"])
        p.do("vector", lambda v, par=par: v.tensor_add(
            out=x1t(par), in0=x1t(par), in1=tmp(par)),
            reads=[f"p4bx1_{par}", f"p4bt_{par}"], writes=[f"p4bx1_{par}"])
        p.do("gpsimd", lambda g, t=t, par=par: g.dma_start(
            out=d["x1"][t * 128:(t + 1) * 128, :], in_=x1t(par)),
            reads=[f"p4bx1_{par}"], writes=[f"x1c{t}"], dma=True, stream="x1c",
            counted=True)
    return p


_FT = ("wq", "wk", "wv", "wo")
_XN2 = ("ksb", "vsb", "asb", "xnt")


def _ft_slice(sb, c, t0, t1):
    return sb[_FT[c // 2]][:, (c % 2) * 4096 + t0:(c % 2) * 4096 + t1]


def _xn2_slice(sb, c, t0, t1):
    return sb[_XN2[c // 2]][:, (c % 2) * TC + t0:(c % 2) * TC + t1]


def build_p5(p, d, sb, ps):
    GT = 512
    p.claim("ps_bmu", ["p4h1a"]); p.claim("ps_stat", ["p4h1b"])
    p.claim("wq", ["p4w1_0", "p4w1_1"]); p.claim("wv", ["p4w2_0", "p4w2_1"])
    p.claim("xnta", ["p4xg_0_0", "p4xg_0_1", "p4xg_0_2"])
    p.claim("xntb", ["p4xg_1_0", "p4xg_1_1", "p4xg_1_2"])
    p.claim("ksb", ["p4xsT_0"]); p.claim("vsb", ["p4xsT_1"])
    p.claim("asb", ["p4h1_0", "p4h1_1"])
    p.claim("xa", ["p4bg_0_0", "p4bg_0_1", "p4bg_0_2", "p4bg_1_0"])
    p.claim("xsq", ["p4bg_1_1", "p4bg_1_2"])
    p.claim("atok", ["p4bg_1_1", "p4bg_1_2"])
    p.claim("xb", ["p4bx1_0", "p4bx1_1", "p4bt_0", "p4bt_1"])
    p.do("scalar", lambda sc: sc.dma_start(out=sb["bfs"][:], in_=d["bfv"]),
         reads=["bfv"], writes=["bfs"], dma=True)
    for half, dst in ((0, "xntb"), (1, "vtok")):
        p.do("gpsimd", lambda g_, half=half, dst=dst: g_.dma_start(
            out=sb[dst][:].rearrange("p (kc mc m) -> p kc mc m", kc=4, mc=NCH),
            in_=d["WfT"].rearrange("p (kc mc m) -> p kc mc m", kc=NCH, mc=NCH)[:, half * 4:(half + 1) * 4, :, :]),
            reads=["WfT"], writes=[dst], dma=True)
    for g in range(TB // GT):
        for j in range(4):
            lt = g * 4 + j
            p.do("scalar", lambda sc, lt=lt: sc.dma_start(
                out=sb["idxq"][:], in_=d["rot_idx"][lt]),
                reads=["rot_idx"], writes=["idxq"], dma=True)
            p.do("gpsimd", lambda g_: g_.indirect_dma_start(
                out=sb["xsq"][:, 0:D], out_offset=None,
                in_=d["x1"], in_offset=IndirectOffsetOnAxis(sb["idxq"][:], 0)),
                reads=["idxq"], stream_reads=["x1init", "x1a", "x1b", "x1c"],
                writes=["xsq", "atok"], dma=True)
            p.do("vector", lambda v: v.reduce_sum(out=sb["mrun"][:, 0:1],
                                                  in_=sb["xsq"][:, 0:D], axis=AX.X),
                 reads=["xsq", "atok"], writes=["mrun"])
            p.do("vector", lambda v: v.tensor_scalar_mul(out=sb["mrun"][:, 0:1],
                                                         in0=sb["mrun"][:, 0:1], scalar1=1.0 / D),
                 reads=["mrun"], writes=["mrun"])
            p.do("vector", lambda v: v.tensor_scalar(
                out=sb["xsq"][:, 0:D], in0=sb["xsq"][:, 0:D],
                scalar1=sb["mrun"][:, 0:1], scalar2=None, op0=ALU.subtract),
                reads=["xsq", "mrun", "atok"], writes=["xsq", "atok"])
            p.do("scalar", lambda sc: sc.activation(out=sb["xa"][:, 0:D],
                                                    in_=sb["xsq"][:, 0:D], func=AF.Square),
                 reads=["xsq", "atok"], writes=["xa"])
            p.do("vector", lambda v: v.reduce_sum(out=sb["lrun"][:, 0:1],
                                                  in_=sb["xa"][:, 0:D], axis=AX.X),
                 reads=["xa"], writes=["lrun"])
            p.do("vector", lambda v: v.tensor_scalar(
                out=sb["lrun"][:, 0:1], in0=sb["lrun"][:, 0:1],
                scalar1=1.0 / D, scalar2=LN_EPS, op0=ALU.mult, op1=ALU.add),
                reads=["lrun"], writes=["lrun"])
            p.do("scalar", lambda sc: sc.activation(out=sb["lrun"][:, 1:2],
                                                    in_=sb["lrun"][:, 0:1], func=AF.Sqrt),
                 reads=["lrun"], writes=["lrun"])
            p.do("vector", lambda v: v.reciprocal(out=sb["lrun"][:, 2:3], in_=sb["lrun"][:, 1:2]),
                 reads=["lrun"], writes=["lrun"])
            p.do("vector", lambda v: v.tensor_scalar_mul(
                out=sb["tokrow"][:], in0=sb["xsq"][:, 0:D], scalar1=sb["lrun"][:, 2:3]),
                reads=["xsq", "lrun", "atok"], writes=["tokrow"])
            for c in range(NCH):
                p.do("tensor", lambda t, c=c: t.transpose(
                    ps["ptb"][:, c * 128:(c + 1) * 128],
                    sb["tokrow"][:, c * 128:(c + 1) * 128], sb["identb"][:]),
                    reads=["tokrow", "identb"], writes=["ps_ptb"])
            p.do("vector", lambda v, j=j: v.tensor_copy(
                out=sb["xnta"][:].rearrange("p (c t) -> p c t", c=NCH, t=GT)[:, :, j * 128:(j + 1) * 128],
                in_=ps["ptb"][:].rearrange("p (c t) -> p c t", c=NCH, t=128)),
                reads=["ps_ptb"], writes=["xnta"])
        if g * GT < TC:
            for c in range(NCH):
                p.do("vector", lambda v, c=c, g=g: v.tensor_copy(
                    out=_xn2_slice(sb, c, g * GT, (g + 1) * GT),
                    in_=sb["xnta"][:].rearrange("p (c t) -> p c t", c=NCH, t=GT)[:, c, :]),
                    reads=["xnta"], writes=[_XN2[c // 2]])
        for mc in range(NCH):
            for kc in range(NCH):
                wsrc = "xntb" if kc < 4 else "vtok"
                kc4 = kc % 4
                p.do("tensor", lambda t, mc=mc, kc=kc, kc4=kc4, wsrc=wsrc: t.matmul(
                    ps["bmu"][:, 0:GT], sb[wsrc][:, (kc4 * NCH + mc) * 128:(kc4 * NCH + mc) * 128 + 128],
                    sb["xnta"][:].rearrange("p (c t) -> p c t", c=NCH, t=GT)[:, kc, :],
                    start=(kc == 0), stop=(kc == NCH - 1)),
                    reads=[wsrc, "xnta"], writes=["ps_bmu"])
            p.do("vector", lambda v, mc=mc, g=g: v.tensor_scalar_add(
                out=_ft_slice(sb, mc, g * GT, (g + 1) * GT), in0=ps["bmu"][:, 0:GT],
                scalar1=sb["bfs"][:, mc:mc + 1]),
                reads=["ps_bmu", "bfs"], writes=[_FT[mc // 2]])
        for j in range(4):
            for c in range(NCH):
                p.do("tensor", lambda t, c=c, g=g, j=j: t.transpose(
                    ps["ptb"][:, c * 128:(c + 1) * 128],
                    _ft_slice(sb, c, g * GT + j * 128, g * GT + (j + 1) * 128),
                    sb["identb"][:]),
                    reads=[_FT[c // 2], "identb"], writes=["ps_ptb"])
            p.do("vector", lambda v: v.tensor_copy(out=sb["tokrow"][:], in_=ps["ptb"][:]),
                 reads=["ps_ptb"], writes=["tokrow"])
            p.do("scalar", lambda sc, g=g, j=j: sc.dma_start(
                out=d["f_tok"][g * GT + j * 128:g * GT + (j + 1) * 128, :],
                in_=sb["tokrow"][:]),
                reads=["tokrow"], writes=[f"ftk{g}_{j}"], dma=True,
                stream="f_tok", counted=True)
    return p


def build_p6(p, d, sb, ps):
    p.claim("ps_brr", ["p4ya"])
    p.do("gpsimd", lambda g_: g_.iota(sb["xsq"][:, 0:512], pattern=[[-1, 512]], base=0,
                                      channel_multiplier=1, allow_small_or_imprecise_dtypes=True),
         writes=["xsq", "atok"])
    p.do("scalar", lambda sc: sc.dma_start(out=sb["kshf"][:], in_=d["kshift"]),
         reads=["kshift"], writes=["kshf"], dma=True)
    for qg in range(2):
        accbuf = {qi: ("xa" if qi < 4 else "xb", (qi % 4) * 1024) for qi in range(8)}
        for qi in range(8):
            bufn, c0 = accbuf[qi]
            p.do("vector", lambda v, bufn=bufn, c0=c0: v.memset(sb[bufn][:, c0:c0 + 1024], 0.0),
                 writes=[bufn])
        p.do("vector", lambda v, qg=qg: v.memset(sb["mrun"][:, qg * 8:(qg + 1) * 8], -1e30),
             writes=["mrun"])
        p.do("vector", lambda v, qg=qg: v.memset(sb["lrun"][:, qg * 8:(qg + 1) * 8], 0.0),
             writes=["lrun"])
        for kt in range(TB // 512):
            p.do("sync", lambda sy, kt=kt: sy.dma_start(
                out=sb["vtok"][:].rearrange("p (j t) -> p j t", j=4),
                in_=d["f_tok"][kt * 512:(kt + 1) * 512, :].rearrange("(j p) t -> p j t", p=128)),
                stream_reads=["f_tok"], writes=["vtok"], dma=True)
            for qi in range(8):
                qt = qg * 8 + qi
                bufn, c0 = accbuf[qi]
                for c in range(NCH):
                    p.do("tensor", lambda t, c=c, qt=qt, kt=kt: t.matmul(
                        ps["brr"][:, 0:512],
                        _ft_slice(sb, c, qt * 128, (qt + 1) * 128),
                        _ft_slice(sb, c, kt * 512, (kt + 1) * 512),
                        start=(c == 0), stop=(c == NCH - 1)),
                        reads=[_FT[c // 2]], writes=["ps_brr"])
                base = float(qt * 128 - kt * 512)
                p.do("vector", lambda v, base=base: v.tensor_scalar_add(
                    out=sb["xsq"][:, 512:1024], in0=sb["xsq"][:, 0:512], scalar1=base),
                    reads=["xsq", "atok"], writes=["xsq", "atok"])
                p.do("vector", lambda v, kt=kt: v.tensor_scalar_add(
                    out=sb["xsq"][:, 512:1024], in0=sb["xsq"][:, 512:1024],
                    scalar1=sb["kshf"][:, kt:kt + 1]),
                    reads=["xsq", "kshf", "atok"], writes=["xsq", "atok"])
                p.do("scalar", lambda sc: sc.activation(out=sb["xsq"][:, 512:1024],
                                                        in_=sb["xsq"][:, 512:1024], func=AF.Abs),
                     reads=["xsq", "atok"], writes=["xsq", "atok"])
                p.do("vector", lambda v: v.tensor_scalar_min(
                    out=sb["xsq"][:, 512:1024], in0=sb["xsq"][:, 512:1024], scalar1=10.0),
                    reads=["xsq", "atok"], writes=["xsq", "atok"])
                p.do("vector", lambda v: v.scalar_tensor_tensor(
                    out=sb["xsq"][:, 1024:1536], in0=sb["xsq"][:, 512:1024], scalar=0.01,
                    in1=ps["brr"][:, 0:512], op0=ALU.mult, op1=ALU.add),
                    reads=["xsq", "ps_brr", "atok"], writes=["xsq", "atok"])
                p.do("vector", lambda v: v.reduce_max(out=sb["scr"][:, 0:1],
                                                      in_=sb["xsq"][:, 1024:1536], axis=AX.X),
                     reads=["xsq", "atok"], writes=["scr"])
                p.do("vector", lambda v, qt=qt: v.tensor_max(
                    out=sb["scr"][:, 1:2], in0=sb["mrun"][:, qt:qt + 1], in1=sb["scr"][:, 0:1]),
                    reads=["mrun", "scr"], writes=["scr"])
                p.do("vector", lambda v, qt=qt: v.tensor_sub(
                    out=sb["scr"][:, 2:3], in0=sb["mrun"][:, qt:qt + 1], in1=sb["scr"][:, 1:2]),
                    reads=["mrun", "scr"], writes=["scr"])
                p.do("scalar", lambda sc: sc.activation(out=sb["scr"][:, 3:4],
                                                        in_=sb["scr"][:, 2:3], func=AF.Exp),
                     reads=["scr"], writes=["scr"])
                p.do("vector", lambda v: v.tensor_scalar_mul(
                    out=sb["scr"][:, 4:5], in0=sb["scr"][:, 1:2], scalar1=-1.0),
                    reads=["scr"], writes=["scr"])
                p.do("scalar", lambda sc: sc.activation(
                    out=sb["kp"][:], in_=sb["xsq"][:, 1024:1536], func=AF.Exp,
                    bias=sb["scr"][:, 4:5], accum_out=sb["scr"][:, 5:6]),
                    reads=["xsq", "scr", "atok"], writes=["kp", "scr"])
                p.do("vector", lambda v, qt=qt: v.scalar_tensor_tensor(
                    out=sb["lrun"][:, qt:qt + 1], in0=sb["lrun"][:, qt:qt + 1],
                    scalar=sb["scr"][:, 3:4], in1=sb["scr"][:, 5:6],
                    op0=ALU.mult, op1=ALU.add),
                    reads=["lrun", "scr"], writes=["lrun"])
                p.do("vector", lambda v, qt=qt: v.tensor_copy(
                    out=sb["mrun"][:, qt:qt + 1], in_=sb["scr"][:, 1:2]),
                    reads=["scr"], writes=["mrun"])
                p.do("vector", lambda v, bufn=bufn, c0=c0: v.tensor_scalar_mul(
                    out=sb[bufn][:, c0:c0 + 1024], in0=sb[bufn][:, c0:c0 + 1024],
                    scalar1=sb["scr"][:, 3:4]),
                    reads=[bufn, "scr"], writes=[bufn])
                for j in range(4):
                    p.do("tensor", lambda t, j=j: t.transpose(
                        ps["ptb"][:, j * 128:(j + 1) * 128],
                        sb["kp"][:, j * 128:(j + 1) * 128], sb["identb"][:]),
                        reads=["kp", "identb"], writes=["ps_ptb"])
                p.do("vector", lambda v: v.tensor_copy(out=sb["tokrow"][:, 0:512],
                                                       in_=ps["ptb"][:, 0:512]),
                     reads=["ps_ptb"], writes=["tokrow"])
                for dh in range(2):
                    for j in range(4):
                        p.do("tensor", lambda t, j=j, dh=dh: t.matmul(
                            ps["bmu"][:, 0:512], sb["tokrow"][:, j * 128:(j + 1) * 128],
                            sb["vtok"][:].rearrange("p (j t) -> p j t", j=4)[:, j, dh * 512:(dh + 1) * 512],
                            start=(j == 0), stop=(j == 3)),
                            reads=["tokrow", "vtok"], writes=["ps_bmu"])
                    p.do("vector", lambda v, bufn=bufn, c0=c0, dh=dh: v.tensor_add(
                        out=sb[bufn][:, c0 + dh * 512:c0 + (dh + 1) * 512],
                        in0=sb[bufn][:, c0 + dh * 512:c0 + (dh + 1) * 512],
                        in1=ps["bmu"][:, 0:512]),
                        reads=[bufn, "ps_bmu"], writes=[bufn])
        for qi in range(8):
            qt = qg * 8 + qi
            bufn, c0 = accbuf[qi]
            p.do("vector", lambda v, qt=qt: v.reciprocal(out=sb["scr"][:, 6:7],
                                                         in_=sb["lrun"][:, qt:qt + 1]),
                 reads=["lrun"], writes=["scr"])
            p.do("vector", lambda v, bufn=bufn, c0=c0: v.tensor_scalar_mul(
                out=sb["xsq"][:, 1536:2560], in0=sb[bufn][:, c0:c0 + 1024],
                scalar1=sb["scr"][:, 6:7]),
                reads=[bufn, "scr"], writes=["xsq", "atok"])
            p.do("scalar", lambda sc, qt=qt: sc.dma_start(
                out=sb["idxq"][:], in_=d["rot_idx"][qt]),
                reads=["rot_idx"], writes=["idxq"], dma=True)
            p.do("gpsimd", lambda g_: g_.indirect_dma_start(
                out=sb["xsq"][:, 2560:3584], out_offset=None,
                in_=d["x1"], in_offset=IndirectOffsetOnAxis(sb["idxq"][:], 0)),
                reads=["idxq"], stream_reads=["x1init", "x1a", "x1b", "x1c"],
                writes=["xsq", "atok"], dma=True)
            p.do("vector", lambda v: v.tensor_add(
                out=sb["xsq"][:, 1536:2560], in0=sb["xsq"][:, 1536:2560],
                in1=sb["xsq"][:, 2560:3584]),
                reads=["xsq", "atok"], writes=["xsq", "atok"])
            p.do("sync", lambda sy, qt=qt: sy.dma_start(
                out=d["y"][qt * 128:(qt + 1) * 128, :], in_=sb["xsq"][:, 1536:2560]),
                reads=["xsq", "atok"], writes=[f"y{qt}"], dma=True, stream="y",
                counted=True)
    return p


def build_p7(p, d, sb, ps):
    NST = TC // 512
    p.do("gpsimd", lambda g_: g_.dma_start(
        out=sb["wrt"][:].rearrange("p (kc e) -> p kc e", kc=NCH),
        in_=d["Wrt1"].rearrange("(kc p) e -> p kc e", p=128)),
        reads=["Wrt1"], writes=["wrt"], dma=True)
    for st in range(NST):
        for kc in range(NCH):
            p.do("tensor", lambda t, st=st, kc=kc: t.matmul(
                ps["log"][0:E1, 0:512], sb["wrt"][:, kc * E1:(kc + 1) * E1],
                _xn2_slice(sb, kc, st * 512, (st + 1) * 512),
                start=(kc == 0), stop=(kc == NCH - 1)),
                reads=["wrt", _XN2[kc // 2]], writes=["ps_log"])
        for j4 in range(4):
            tile = st * 4 + j4
            p.do("vector", lambda v, j4=j4: v.tensor_copy(
                out=sb["logsb"][0:E1, 0:128], in_=ps["log"][0:E1, j4 * 128:(j4 + 1) * 128]),
                reads=["ps_log"], writes=["logsb"])
            p.do("tensor", lambda t: t.transpose(
                ps["tsm"][:, 0:E1], sb["logsb"][0:E1, 0:128], sb["identf"][0:E1, 0:E1]),
                reads=["logsb", "identf"], writes=["ps_tcol"])
            p.do("vector", lambda v: v.tensor_copy(out=sb["scr"][:, 0:E1], in_=ps["tsm"][:, 0:E1]),
                 reads=["ps_tcol"], writes=["scr"])
            p.do("vector", lambda v: v.reduce_max(out=sb["scr"][:, 8:9], in_=sb["scr"][:, 0:E1],
                                                  axis=AX.X),
                 reads=["scr"], writes=["scr"])
            p.do("vector", lambda v: v.tensor_scalar_mul(out=sb["scr"][:, 9:10],
                                                         in0=sb["scr"][:, 8:9], scalar1=-1.0),
                 reads=["scr"], writes=["scr"])
            p.do("scalar", lambda sc: sc.activation(
                out=sb["oh"][:, 0:E1], in_=sb["scr"][:, 0:E1], func=AF.Exp,
                bias=sb["scr"][:, 9:10], accum_out=sb["scr"][:, 10:11]),
                reads=["scr"], writes=["oh", "scr"])
            p.do("vector", lambda v: v.max(out=sb["top8"][:], in_=sb["oh"][:, 0:E1]),
                 reads=["oh"], writes=["top8"])
            p.do("vector", lambda v: v.tensor_add(out=sb["scr"][:, 11:12],
                                                  in0=sb["top8"][:, 0:1], in1=sb["top8"][:, 1:2]),
                 reads=["top8"], writes=["scr"])
            p.do("vector", lambda v: v.scalar_tensor_tensor(
                out=sb["scr"][:, 12:13], in0=sb["scr"][:, 10:11], scalar=1e-8,
                in1=sb["scr"][:, 11:12], op0=ALU.mult, op1=ALU.add),
                reads=["scr"], writes=["scr"])
            p.do("vector", lambda v: v.reciprocal(out=sb["scr"][:, 13:14], in_=sb["scr"][:, 12:13]),
                 reads=["scr"], writes=["scr"])
            p.do("vector", lambda v: v.tensor_scalar(
                out=sb["scr"][:, 16:16 + E1], in0=sb["oh"][:, 0:E1],
                scalar1=sb["top8"][:, 1:2], scalar2=None, op0=ALU.is_ge),
                reads=["oh", "top8"], writes=["scr"])
            p.do("vector", lambda v: v.tensor_mul(out=sb["scr"][:, 24:24 + E1],
                                                  in0=sb["oh"][:, 0:E1], in1=sb["scr"][:, 16:16 + E1]),
                 reads=["oh", "scr"], writes=["scr"])
            p.do("vector", lambda v, tile=tile: v.tensor_scalar_mul(
                out=sb["w8a"][:, tile * E1:(tile + 1) * E1], in0=sb["scr"][:, 24:24 + E1],
                scalar1=sb["scr"][:, 13:14]),
                reads=["scr"], writes=["w8a"])
    p.claim("p7g", ["ps_log"])
    p.claim("p7u", ["ps_brd", "ps_ttok", "ps_pref", "ps_cnt"])
    p.claim("p7oa", ["ps_bmu", "ps_stat", "p4h1b"])
    p.claim("p7ob", ["ps_brr", "ps_trow", "ps_tcol", "p4yb"])
    p.claim("p7wo_q", ["wq"]); p.claim("p7wo_k", ["wk"])
    p.claim("p7wg_v", ["wv"]); p.claim("p7wg_o", ["wo"])
    for k in range(4):
        p.claim(f"p7wu{k}", ["vtok"])
    for mc in range(M1 // 128):
        p.claim(f"p7h{mc}", ["xnta" if mc < 8 else "xntb"])
    p.claim("p7ht0", ["kp"]); p.claim("p7ht1", ["tokrow"])
    p.claim("p7ysl_0", ["xa"]); p.claim("p7ysl_1", ["xa"])
    p.do("gpsimd", lambda g_: g_.memset(sb["ones_rowb"][:], 1.0),
         writes=["ones_rowb"])

    def hchunk(mc):
        return (sb["xnta"] if mc < 8 else sb["xntb"])[:, (mc % 8) * 512:(mc % 8 + 1) * 512]

    for e in range(E1):
        for half, dst, nm in ((0, "wq", "p7wo_q"), (1, "wk", "p7wo_k")):
            p.do("gpsimd", lambda g_, e=e, half=half, dst=dst: g_.dma_start(
                out=sb[dst][:], in_=d["Wo1"][e][:, half * 8192:(half + 1) * 8192]),
                reads=["Wo1"], writes=[nm], dma=True, stream=f"swo{half}")
        for half, dst, nm in ((0, "wv", "p7wg_v"), (1, "wo", "p7wg_o")):
            p.do("gpsimd", lambda g_, e=e, half=half, dst=dst: g_.dma_start(
                out=sb[dst][:].rearrange("p (kc mc m) -> p kc mc m", kc=NCH, mc=NCH),
                in_=d["Wg1"][e].rearrange("p (kc mc m) -> p kc mc m", kc=NCH, mc=M1 // 128)[:, :, half * 8:(half + 1) * 8, :]),
                reads=["Wg1"], writes=[nm], dma=True, stream=f"swg{half}")
        p.do("scalar", lambda sc, e=e: sc.dma_start(out=sb["bgs"][:], in_=d["bg1"][e]),
             reads=["bg1"], writes=["bgs"], dma=True)
        p.do("scalar", lambda sc, e=e: sc.dma_start(out=sb["bus"][:], in_=d["bu1"][e]),
             reads=["bu1"], writes=["bus"], dma=True)
        p.do("gpsimd", lambda g_, e=e: g_.dma_start(
            out=sb["bo1r"][0:1, :], in_=d["bo1r"][e:e + 1, :]),
            reads=["bo1r_d"], writes=["bo1r"], dma=True, stream="sbo1")
        for st in range(NST):
            for mc in range(M1 // 128):
                slot = (st * (M1 // 128) + mc) % 4
                p.do("gpsimd", lambda g_, e=e, mc=mc, slot=slot: g_.dma_start(
                    out=sb["vtok"][:, slot * 1024:(slot + 1) * 1024],
                    in_=d["Wu1"][e][mc]),
                    reads=["Wu1"], writes=[f"p7wu{slot}"], dma=True,
                    stream=f"swu{slot}")
                wname = "wv" if mc < 8 else "wo"
                wnm = "p7wg_v" if mc < 8 else "p7wg_o"
                mloc = mc % 8
                for kc in range(NCH):
                    p.do("tensor", lambda t, mc=mc, kc=kc, st=st, wname=wname, mloc=mloc: t.matmul(
                        ps["pk3"][:, 0:512], sb[wname][:, (kc * NCH + mloc) * 128:(kc * NCH + mloc) * 128 + 128],
                        _xn2_slice(sb, kc, st * 512, (st + 1) * 512),
                        start=(kc == 0), stop=(kc == NCH - 1)),
                        reads=[wnm, _XN2[kc // 2]], writes=["p7g"])
                htn = "p7ht0" if mc % 2 == 0 else "p7ht1"
                htb = sb["kp"][:] if mc % 2 == 0 else sb["tokrow"][:, 0:512]
                p.do("scalar", lambda sc, mc=mc, htb=htb: sc.activation(
                    out=htb, in_=ps["pk3"][:, 0:512],
                    func=AF.Silu, bias=sb["bgs"][:, mc:mc + 1]),
                    reads=["p7g", "bgs"], writes=[htn])
                for kc in range(NCH):
                    p.do("tensor", lambda t, mc=mc, kc=kc, st=st, slot=slot: t.matmul(
                        ps["pk3"][:, 512:1024], sb["vtok"][:, slot * 1024 + kc * 128:slot * 1024 + (kc + 1) * 128],
                        _xn2_slice(sb, kc, st * 512, (st + 1) * 512),
                        start=(kc == 0), stop=(kc == NCH - 1)),
                        reads=[f"p7wu{slot}", _XN2[kc // 2]], writes=["p7u"])
                p.do("vector", lambda v, mc=mc, htb=htb: v.scalar_tensor_tensor(
                    out=hchunk(mc), in0=ps["pk3"][:, 512:1024],
                    scalar=sb["bus"][:, mc:mc + 1], in1=htb,
                    op0=ALU.add, op1=ALU.mult),
                    reads=["p7u", "bus", htn], writes=[f"p7h{mc}"])
            for sl in range(4):
                par = sl % 2
                pso = ps["pk1"] if par == 0 else ps["pk2"]
                psn = "p7oa" if par == 0 else "p7ob"
                for dh in range(2):
                    p.do("tensor", lambda t, pso=pso, dh=dh: t.matmul(
                        pso[:, dh * 512:(dh + 1) * 512], sb["ones_rowb"][0:1, :],
                        sb["bo1r"][0:1, dh * 512:(dh + 1) * 512],
                        start=True, stop=False),
                        reads=["ones_rowb", "bo1r"], writes=[psn])
                    for kcm in range(M1 // 128):
                        wsrc = "wq" if kcm < 8 else "wk"
                        wnm = "p7wo_q" if kcm < 8 else "p7wo_k"
                        kloc = kcm % 8
                        p.do("tensor", lambda t, sl=sl, kcm=kcm, wsrc=wsrc, kloc=kloc, pso=pso, dh=dh: t.matmul(
                            pso[:, dh * 512:(dh + 1) * 512], hchunk(kcm)[:, sl * 128:(sl + 1) * 128],
                            sb[wsrc][:, kloc * 1024 + dh * 512:kloc * 1024 + (dh + 1) * 512],
                            start=False, stop=(kcm == M1 // 128 - 1)),
                            reads=[f"p7h{kcm}", wnm], writes=[psn])
                tile = st * 4 + sl
                p.do("vector", lambda v, e=e, tile=tile, par=par, pso=pso: v.tensor_scalar_mul(
                    out=sb["xa"][:, par * 1024:(par + 1) * 1024], in0=pso[:, 0:1024],
                    scalar1=sb["w8a"][:, tile * E1 + e:tile * E1 + e + 1]),
                    reads=[psn, "w8a"], writes=[f"p7ysl_{par}"])
                p.do("gpsimd", lambda g_, e=e, tile=tile, par=par: g_.dma_start(
                    out=d["y"][tile * 128:(tile + 1) * 128, :],
                    in_=sb["xa"][:, par * 1024:(par + 1) * 1024], accum_op=ALU.add),
                    reads=[f"p7ysl_{par}"], stream_reads=["y"],
                    writes=[f"yac{e}_{tile}"], dma=True, stream="y", counted=True)
    return p


P567_SBUF = {
    "bfs": ([128, NCH], F32),
    "kshf": ([128, TB // 512], F32),
    "idxq": ([128, 1], U32),
    "mrun": ([128, 16], F32), "lrun": ([128, 16], F32),
    "wrt": ([128, NCH * E1], BF16),
    "w8a": ([128, 128], F32),
    "bgs": ([128, M1 // 128], F32), "bus": ([128, M1 // 128], F32),
    "bo1r": ([1, D], BF16),
    "ones_rowb": ([1, 128], BF16),
}


def kernel(**inp):
    common = _stage_common(inp)
    cores = [(c // 2, c % 2) for c in range(N_CORES)]
    per_core = [_stage_core(inp, b, s) for (b, s) in cores]

    names_in = [n for n, (sh, dt, role) in DRAM_SPECS.items() if role == "in"]
    internal = [n for n, (sh, dt, role) in DRAM_SPECS.items() if role == "inter"]
    roles = _roles_for(names_in, ["y"], internal)
    sbs = merge_specs(P0_SBUF, P1_SBUF, P2_SBUF, P3_SBUF, P4_SBUF, P567_SBUF)
    psm = merge_specs(P0_PSUM, P1_PSUM, P2_PSUM, P3_PSUM, P4_PSUM)
    fns = [build_p0, build_p1, build_p2, build_p3, build_p4, build_p4b,
           build_p5, build_p6, build_p7]
    nc, _ = _build_launch(fns, sbs, psm, roles)
    in_maps = []
    for c in range(N_CORES):
        m = {}
        for n in names_in:
            m[n] = per_core[c][n] if n in per_core[c] else common[n]
        in_maps.append(m)
    res = run_bass_kernel_spmd(nc, in_maps, core_ids=list(range(N_CORES)))
    out = np.empty((B, S, D), np.float32)
    for c, (b, s) in enumerate(cores):
        out[b, s * TC:(s + 1) * TC, :] = res.results[c]["y"]
    return out
